# version 09
# baseline (speedup 1.0000x reference)
import numpy as np
import ml_dtypes
from contextlib import ExitStack
import concourse.bass as bass
import concourse.mybir as mybir
from concourse.bass_utils import run_bass_kernel_spmd
from concourse.expressions import RuntimeValue

F32 = mybir.dt.float32
BF16 = mybir.dt.bfloat16
AF = mybir.ActivationFunctionType
ALU = mybir.AluOpType
AX = mybir.AxisListType
ds = bass.ds

D = 2048
DT = 16
TB = 4096
TTB = 32
TC = 2048
TTC = 16
NHC = 8
KT = 32
E = 1024
ET = 8
NX = 6
G = 512
NG = 4
ST = 16
SCALE = float(1.0 / np.sqrt(128.0))
EPS = 1e-5
RG_PAIRS = [[0, 1], [2, 3], [4, 5], [6, 7]]
PHASES = 99
FFN_GROUPS = 99

SEMS = [
    "b0", "b0v", "p0",
    "ld0", "ld1", "lnv", "rs1", "apl", "wxh0", "wxh1",
    "txl0", "txl1", "wqkv", "mm2", "ev2",
    "spl0", "spl1", "spl2", "spl3",
    "hld0", "hld1", "hcons", "aS", "aexp", "amask", "acons", "aev",
    "am0", "am1", "am2", "am3", "cg",
    "gq", "wo4", "mm4", "ev4", "ln4", "rs4", "ap4", "wrs0", "wrs1",
    "xtb0", "xtb1", "wlA0", "wlA1", "wlB0", "wlB1", "wlC0", "wlC1",
    "rsg0", "rsg1",
    "gt", "gtv", "gch", "gex", "gey", "gtw", "gts", "wrp",
    "fh", "fhev", "fz", "fzev", "fhw", "p2", "p2ev",
    "fq0", "fq1", "fq2", "fq3", "oq0", "oq1", "oq2", "oq3",
]


class P:
    def __init__(self, nc, stack, names):
        self.sem = {}
        self.n = {}
        for s in names:
            self.sem[s] = stack.enter_context(nc.semaphore(s))
            self.n[s] = 0

    def inc(self, instr, name, k=1):
        instr.then_inc(self.sem[name], k)
        self.n[name] += k
        return instr

    def wait(self, eng, name, val=None):
        v = self.n[name] if val is None else val
        if v > 0:
            eng.wait_ge(self.sem[name], v)


def build(nc):
    PH = PHASES
    def inp(name, shape, dt=F32):
        return nc.dram_tensor(name, list(shape), dt, kind="ExternalInput").ap()

    x_full = inp("x_full", (TB, D))
    x_mine = inp("x_mine", (TC, D))
    wq_pm = inp("wq_pm", (128, DT, NHC * 128))
    wk_pm = inp("wk_pm", (128, DT, NHC * 128))
    wv_pm = inp("wv_pm", (128, DT, NHC * 128))
    wo_pm = inp("wo_pm", (128, 4, DT, 512))
    wg_pm = inp("wg_pm", (128, DT, 2, 8))
    wup_pm = inp("wup_pm", (128, NX, 2, DT, 512))
    wsw_pm = inp("wsw_pm", (128, NX, 2, ET, 512))
    wsu_pm = inp("wsu_pm", (128, 4, DT, 512))
    wsa_pm = inp("wsa_pm", (128, 2, ET, 512))
    wdn_pm = inp("wdn_pm", (128, 8, NX, ET, 256))
    wsd_pm = inp("wsd_pm", (128, 8, ST, 256))
    bupT = inp("bupT", (128, NX, ET))
    bswT = inp("bswT", (128, NX, ET))
    bsuT = inp("bsuT", (128, ST))
    bsaT = inp("bsaT", (128, ET))
    bg0 = inp("bg0", (1, 8))
    bg1 = inp("bg1", (1, 8))
    bdn_r = inp("bdn_r", (NX, D))
    bsd_r = inp("bsd_r", (1, D))
    c_ident = inp("c_ident", (128, 128))
    c_ones = inp("c_ones", (128, 128), BF16)
    c_band = inp("c_band", (128, 896), BF16)
    c_sel = inp("c_sel", (8, NX * 128), BF16)

    y_mine = nc.dram_tensor("y_mine", [TC, D], F32, kind="ExternalOutput").ap()

    def scr(name, shape, dt):
        return nc.dram_tensor(name, list(shape), dt, kind="Internal").ap()

    xln_tm = scr("xln_tm", (TB, D), BF16)
    kT_sc = scr("kT_sc", (NHC, 128, TB), BF16)
    qT_sc = scr("qT_sc", (NHC, 128, TB), BF16)
    v_sc = scr("v_sc", (NHC, 128, KT, 128), BF16)
    attn_mine = scr("attn_mine", (4, NHC * 128, 1024), BF16)
    attn_full = scr("attn_full", (4, D, 1024), BF16)
    res_tm = scr("res_tm", (TC, D), F32)
    base_tm = scr("base_tm", (TC, D), F32)
    xt0_tm = scr("xt0_tm", (TC, D), BF16)
    xt1_tm = scr("xt1_tm", (TC, D), BF16)
    wsc_up = scr("wsc_up", (128, NX, 2, DT, 512), BF16)
    wsc_sw = scr("wsc_sw", (128, NX, 2, ET, 512), BF16)
    wsc_su = scr("wsc_su", (128, 4, DT, 512), BF16)
    wsc_sa = scr("wsc_sa", (128, 2, ET, 512), BF16)
    wsc_dn = scr("wsc_dn", (128, 8, NX, ET, 256), BF16)
    wsc_sd = scr("wsc_sd", (128, 8, ST, 256), BF16)
    bdn_sc = scr("bdn_sc", (NX, D), BF16)
    bsd_sc = scr("bsd_sc", (1, D), BF16)

    a3 = []
    ACONS_END = {}
    cS = cexp = ccons = cmask = cev = 0
    for q in range(4):
        for h in range(NHC):
            for ci in range(2):
                s = 2 * q + ci
                npair = 2 * (s + 1)
                a3.append(dict(q=q, h=h, ci=ci, s=s, npair=npair,
                               Sb=cS, Eb=cexp, Cb=ccons, Mb=cmask, Vb=cev,
                               Ce=ccons + npair))
                cS += npair
                cexp += npair
                ccons += npair
                cmask += 2
                cev += 2
                if ci == 1:
                    ACONS_END[q * NHC + h] = ccons

    with ExitStack() as stk:
        p = P(nc, stk, SEMS)

        def T(name, shape, dt=F32):
            return stk.enter_context(nc.sbuf_tensor(name, list(shape), dt))

        bup_sb = T("bup_sb", (128, NX, ET))
        bsw_sb = T("bsw_sb", (128, NX, ET))
        bsu_sb = T("bsu_sb", (128, ST))
        bsa_sb = T("bsa_sb", (128, ET))
        bgA_sb = T("bgA_sb", (1, 8), BF16)
        bgB_sb = T("bgB_sb", (1, 8), BF16)
        bgr_sb = T("bgr_sb", (128, 2, 8))
        eps_sb = T("eps_sb", (128, 1))
        wg_sb = T("wg_sb", (128, DT, 2, 8), BF16)
        ident_sb = T("ident_sb", (128, 128))
        ones_sb = T("ones_sb", (128, 128), BF16)
        band_sb = T("band_sb", (128, 896), BF16)
        sel_sb = T("sel_sb", (8, NX * 128), BF16)

        pBig0 = stk.enter_context(nc.psum_tensor("pBig0", [128, 1024], F32))
        pBig1 = stk.enter_context(nc.psum_tensor("pBig1", [128, 1024], F32))
        pM = [stk.enter_context(nc.psum_tensor(f"pM{i}", [128, 512], F32))
              for i in range(4)]
        PB4 = [pBig0[:, 0:512], pBig0[:, 512:1024],
               pBig1[:, 0:512], pBig1[:, 512:1024]]
        BANK8 = [pM[0][:, :], pM[1][:, :], pM[2][:, :], pM[3][:, :]] + PB4

        with nc.Block() as block:
            @block.gpsimd
            def _(g):
                for dst, src in [
                        (bup_sb[:, :, :], bupT), (bsw_sb[:, :, :], bswT),
                        (bsu_sb[:, :], bsuT), (bsa_sb[:, :], bsaT),
                        (bgA_sb[0:1, :], bg0), (bgB_sb[0:1, :], bg1),
                        (wg_sb[:, :, :, :], wg_pm),
                        (ident_sb[:, :], c_ident), (ones_sb[:, :], c_ones),
                        (band_sb[:, :], c_band), (sel_sb[:, :], c_sel)]:
                    p.inc(g.dma_start(out=dst, in_=src), "b0", 16)
                for n in range(NX):
                    p.inc(g.dma_start(out=wsc_up[:, n], in_=wup_pm[:, n]),
                          "p0", 16)
                    p.inc(g.dma_start(out=wsc_sw[:, n], in_=wsw_pm[:, n]),
                          "p0", 16)
                p.inc(g.dma_start(out=wsc_su[:, :], in_=wsu_pm), "p0", 16)
                p.inc(g.dma_start(out=wsc_sa[:, :], in_=wsa_pm), "p0", 16)
                for d8 in range(8):
                    p.inc(g.dma_start(out=wsc_dn[:, d8], in_=wdn_pm[:, d8]),
                          "p0", 16)
                p.inc(g.dma_start(out=wsc_sd[:, :], in_=wsd_pm), "p0", 16)
                p.inc(g.dma_start(out=bdn_sc, in_=bdn_r), "p0", 16)
                p.inc(g.dma_start(out=bsd_sc, in_=bsd_r), "p0", 16)
            B0_TOTAL = p.n["b0"]
            P0_TOTAL = p.n["p0"]

            @block.tensor
            def _(t):
                p.wait(t, "b0", B0_TOTAL)
                t.matmul(pM[0][:, 0:8], ones_sb[0:1, :], bgA_sb[0:1, :],
                         start=True, stop=True)
                p.inc(t.matmul(pM[1][:, 0:8], ones_sb[0:1, :], bgB_sb[0:1, :],
                               start=True, stop=True), "b0v")

            @block.vector
            def _(v):
                v.memset(eps_sb[:, :], EPS)
                p.wait(v, "b0v", 1)
                v.tensor_copy(bgr_sb[:, 0, :], pM[0][:, 0:8])
                p.inc(v.tensor_copy(bgr_sb[:, 1, :], pM[1][:, 0:8]), "b0v")

            with nc.sbuf_tensor("xlnT", [128, DT, TB], BF16) as xlnT, \
                 nc.sbuf_tensor("a2w", [128, DT, 1024], BF16) as a2w, \
                 nc.sbuf_tensor("a2st", [128, 4, 512], BF16) as a2st, \
                 nc.sbuf_tensor("a1x", [128, 2, D], F32) as a1x, \
                 nc.sbuf_tensor("a1st", [128, 2, 4, 6], F32) as a1st, \
                 nc.sbuf_tensor("a1ag", [128, 2, 2], F32) as a1ag, \
                 nc.sbuf_tensor("a1rs", [128, 2, 2], F32) as a1rs, \
                 nc.sbuf_tensor("a1o", [128, 2, D], BF16) as a1o:

                @block.sync
                def _(s):
                    for tt in range(TTB):
                        if tt >= 2:
                            p.wait(s, "apl", tt - 1)
                        p.inc(s.dma_start(
                            out=a1x[:, tt % 2, :],
                            in_=x_full[tt * 128:(tt + 1) * 128, :]),
                            f"ld{tt % 2}", 16)

                def a1_apply(v, m):
                    b = m % 2
                    if m >= 2:
                        hh = (m - 2) // 16
                        p.wait(v, f"wxh{hh}", 16 * min(m - 16 * hh, 16))
                    p.wait(v, "rs1", 2 * (m + 1))
                    p.inc(v.tensor_scalar(a1o[:, b, :], a1x[:, b, :],
                                          a1ag[:, b, 0:1], a1rs[:, b, 1:2],
                                          ALU.subtract, ALU.mult), "apl")

                @block.vector
                def _(v):
                    for tt in range(TTB):
                        b = tt % 2
                        p.wait(v, f"ld{tt % 2}", 16 * (tt // 2 + 1))
                        for i in range(4):
                            ins = v.bn_stats(a1st[:, b, i, :],
                                             a1x[:, b, 512 * i:512 * (i + 1)])
                        p.inc(ins, "lnv")
                        p.wait(v, "lnv", p.n["lnv"])
                        if tt >= 2:
                            p.wait(v, "rs1", 2 * (tt - 1))
                        p.inc(v.bn_aggr(a1ag[:, b, :], a1st[:, b, :, :]),
                              "lnv")
                        if tt >= 1:
                            a1_apply(v, tt - 1)
                    a1_apply(v, TTB - 1)

                @block.scalar
                def _(sc):
                    p.wait(sc, "b0v", 2)
                    for tt in range(TTB):
                        b = tt % 2
                        p.wait(sc, "lnv", 2 * (tt + 1))
                        p.inc(sc.activation(a1rs[:, b, 0:1], a1ag[:, b, 1:2],
                                            AF.Ln, bias=eps_sb[:, 0:1]), "rs1")
                        p.wait(sc, "rs1", p.n["rs1"])
                        p.inc(sc.activation(a1rs[:, b, 1:2], a1rs[:, b, 0:1],
                                            AF.Exp, scale=-0.5), "rs1")
                        p.wait(sc, "apl", tt + 1)
                        p.inc(sc.dma_start(
                            out=xln_tm[tt * 128:(tt + 1) * 128, :],
                            in_=a1o[:, tt % 2, :]), f"wxh{tt // 16}", 16)

                @block.sync
                def _(s):
                    if PH < 2:
                        return
                    for th in range(2):
                        p.wait(s, f"wxh{th}", 16 * 16)
                        for dt in range(DT):
                            p.inc(s.dma_start(
                                out=xlnT[:, dt, th * 2048:(th + 1) * 2048],
                                in_=xln_tm[th * 2048:(th + 1) * 2048,
                                           dt * 128:(dt + 1) * 128],
                                transpose=True), f"txl{th}", 16)

                @block.gpsimd
                def _(g):
                    if PH < 2:
                        return
                    p.inc(g.dma_start(out=a2w[:, :, :], in_=wk_pm), "wqkv", 16)
                    p.wait(g, "mm2", 64)
                    p.inc(g.dma_start(out=a2w[:, :, :], in_=wv_pm), "wqkv", 16)
                    p.wait(g, "mm2", 128)
                    p.inc(g.dma_start(out=a2w[:, :, :], in_=wq_pm), "wqkv", 16)

                @block.tensor
                def _(t):
                    if PH < 2:
                        return
                    mm = 0
                    p.wait(t, "wqkv", 16)
                    p.wait(t, "b0v", 2)
                    for ot in range(NHC):
                        for tc in range(8):
                            p.wait(t, f"txl{tc // 4}", 16 * 16)
                            if mm >= 4:
                                p.wait(t, "ev2", mm - 3)
                            for dt in range(DT):
                                ins = t.matmul(
                                    pM[mm % 4][:, :],
                                    a2w[:, dt, ot * 128:(ot + 1) * 128],
                                    xlnT[:, dt, tc * 512:(tc + 1) * 512],
                                    start=(dt == 0), stop=(dt == DT - 1))
                            p.inc(ins, "mm2")
                            mm += 1
                    for tt in range(KT):
                        for oc in range(2):
                            if tt == 0 and oc == 0:
                                p.wait(t, "wqkv", 32)
                                p.wait(t, "txl0", 16 * 16)
                            if tt == 16 and oc == 0:
                                p.wait(t, "txl1", 16 * 16)
                            if mm >= 4:
                                p.wait(t, "ev2", mm - 3)
                            for dt in range(DT):
                                ins = t.matmul(
                                    pM[mm % 4][:, :],
                                    xlnT[:, dt, tt * 128:(tt + 1) * 128],
                                    a2w[:, dt, oc * 512:(oc + 1) * 512],
                                    start=(dt == 0), stop=(dt == DT - 1))
                            p.inc(ins, "mm2")
                            mm += 1
                    p.wait(t, "wqkv", 48)
                    for ot in range(NHC):
                        for tc in range(8):
                            if mm >= 4:
                                p.wait(t, "ev2", mm - 3)
                            for dt in range(DT):
                                ins = t.matmul(
                                    pM[mm % 4][:, :],
                                    a2w[:, dt, ot * 128:(ot + 1) * 128],
                                    xlnT[:, dt, tc * 512:(tc + 1) * 512],
                                    start=(dt == 0), stop=(dt == DT - 1))
                            p.inc(ins, "mm2")
                            mm += 1

                @block.scalar
                def _(sc):
                    if PH < 2:
                        return
                    for i in range(192):
                        p.wait(sc, "mm2", i + 1)
                        if i >= 4:
                            p.wait(sc, f"spl{i % 4}", 16 * (i // 4))
                        p.inc(sc.copy(a2st[:, i % 4, :], pM[i % 4][:, :]),
                              "ev2")
                        p.wait(sc, "ev2", p.n["ev2"])
                        if i < 64:
                            ot, tc = divmod(i, 8)
                            ins = sc.dma_start(
                                out=kT_sc[ot, :, tc * 512:(tc + 1) * 512],
                                in_=a2st[:, i % 4, :])
                        elif i < 128:
                            tt, oc = divmod(i - 64, 2)
                            ins = sc.dma_start(
                                out=v_sc[oc * 4:(oc + 1) * 4, :, tt, :]
                                .rearrange("hh pp j -> pp hh j"),
                                in_=a2st[:, i % 4, :].rearrange(
                                    "pp (hh j) -> pp hh j", hh=4))
                        else:
                            ot, tc = divmod(i - 128, 8)
                            ins = sc.dma_start(
                                out=qT_sc[ot, :, tc * 512:(tc + 1) * 512],
                                in_=a2st[:, i % 4, :])
                        p.inc(ins, f"spl{i % 4}", 16)
            A2_MM_TOTAL = p.n["mm2"]

            with nc.sbuf_tensor("kbuf", [128, 2, TB], BF16) as kbuf, \
                 nc.sbuf_tensor("vbuf", [128, 2, KT, 128], BF16) as vbuf, \
                 nc.sbuf_tensor("qbuf", [128, 2, 1024], BF16) as qbuf, \
                 nc.sbuf_tensor("abuf", [128, 2, 1024], BF16) as abuf, \
                 nc.sbuf_tensor("rbuf", [128, 512], F32) as rbuf, \
                 nc.sbuf_tensor("attn", [128, NHC, TB], BF16) as attn:

                @block.sync
                def _(s):
                    if PH < 3:
                        return
                    for sl in range(4):
                        p.wait(s, f"spl{sl}", 16 * 48)
                    for q in range(4):
                        for h in range(NHC):
                            gh = q * NHC + h
                            if gh >= 2:
                                p.wait(s, "acons", ACONS_END[gh - 2])
                            p.inc(s.dma_start(out=kbuf[:, gh % 2, :],
                                              in_=kT_sc[h]), f"hld{gh % 2}", 16)
                            p.inc(s.dma_start(out=vbuf[:, gh % 2, :, :],
                                              in_=v_sc[h]), f"hld{gh % 2}", 16)
                            p.inc(s.dma_start(
                                out=qbuf[:, gh % 2, :],
                                in_=qT_sc[h, :, q * 1024:(q + 1) * 1024]),
                                f"hld{gh % 2}", 16)

                @block.tensor
                def _(t):
                    if PH < 3:
                        return
                    p.wait(t, "ev2", 192)
                    for ck in a3:
                        gh = ck["q"] * NHC + ck["h"]
                        b = gh % 2
                        qsl = slice(ck["ci"] * 512, (ck["ci"] + 1) * 512)
                        p.wait(t, f"hld{gh % 2}", 48 * (gh // 2 + 1))
                        npair, s = ck["npair"], ck["s"]

                        def consume(pp):
                            first, last = pp == 0, pp == npair - 1
                            p.wait(t, "aexp", ck["Eb"] + pp + 1)
                            if pp >= 2 * s:
                                p.wait(t, "amask",
                                       ck["Mb"] + (pp - 2 * s) + 1)
                            if first:
                                p.wait(t, "aev", ck["Vb"])
                            for u in range(2):
                                kt = 2 * pp + u
                                asl = slice(u * 512, (u + 1) * 512)
                                t.matmul(pM[0][:, :], ones_sb[:, :],
                                         abuf[:, pp % 2, asl],
                                         start=(first and u == 0),
                                         stop=(last and u == 1))
                                ins = t.matmul(pM[1][:, :],
                                               vbuf[:, b, kt, :],
                                               abuf[:, pp % 2, asl],
                                               start=(first and u == 0),
                                               stop=(last and u == 1))
                            p.inc(ins, "acons")

                        for pp in range(npair):
                            if pp >= 2:
                                p.wait(t, "aexp", ck["Eb"] + pp - 1)
                            for u in range(2):
                                kt = 2 * pp + u
                                pS = pBig0 if pp % 2 == 0 else pBig1
                                ins = t.matmul(
                                    pS[:, u * 512:(u + 1) * 512],
                                    kbuf[:, b, kt * 128:(kt + 1) * 128],
                                    qbuf[:, b, qsl], start=True, stop=True)
                            p.inc(ins, "aS")
                            if pp >= 1:
                                consume(pp - 1)
                        consume(npair - 1)

                @block.scalar
                def _(sc):
                    if PH < 3:
                        return
                    for ck in a3:
                        for pp in range(ck["npair"]):
                            pS = pBig0 if pp % 2 == 0 else pBig1
                            p.wait(sc, "aS", ck["Sb"] + pp + 1)
                            p.wait(sc, "acons", ck["Cb"] + pp - 1)
                            p.inc(sc.activation(abuf[:, pp % 2, :], pS[:, :],
                                                AF.Exp, scale=SCALE), "aexp")

                @block.vector
                def _(v):
                    if PH < 3:
                        return
                    for ck in a3:
                        s, npair = ck["s"], ck["npair"]
                        for pp in (2 * s, 2 * s + 1):
                            p.wait(v, "aexp", ck["Eb"] + pp + 1)
                            for u in range(2):
                                i = 2 * (pp - 2 * s) + u
                                off = 384 - 128 * i
                                asl = slice(u * 512, (u + 1) * 512)
                                ins = v.tensor_mul(abuf[:, pp % 2, asl],
                                                   abuf[:, pp % 2, asl],
                                                   band_sb[:, off:off + 512])
                            p.inc(ins, "amask")
                        p.wait(v, "acons", ck["Cb"] + npair)
                        p.inc(v.reciprocal(rbuf[:, :], pM[0][:, :]), "aev")
                        p.wait(v, "aev", p.n["aev"])
                        p.inc(v.tensor_mul(
                            attn[:, ck["h"], s * 512:(s + 1) * 512],
                            pM[1][:, :], rbuf[:, :]), "aev")

                @block.scalar
                def _(sc):
                    if PH < 3:
                        return
                    for q in range(4):
                        p.wait(sc, "aev", 32 * (q + 1))
                        for ch in range(2):
                            p.inc(sc.dma_start(
                                out=attn_mine[q, 512 * ch:512 * (ch + 1), :]
                                .rearrange("(ct pp) j -> pp ct j", pp=128),
                                in_=attn[:, 4 * ch:4 * (ch + 1),
                                         q * 1024:(q + 1) * 1024]),
                                f"am{q}", 16)

                @block.gpsimd
                def _(g):
                    if PH < 3:
                        return
                    for q in range(4):
                        p.wait(g, f"am{q}", 32)
                        p.inc(g.collective_compute(
                            "AllGather", ALU.bypass,
                            ins=[attn_mine[q]], outs=[attn_full[q]],
                            replica_groups=RG_PAIRS), "cg", 1)
            A3_CONS = p.n["acons"]

            with nc.sbuf_tensor("gqb", [128, 2, DT, 1024], BF16) as gqb, \
                 nc.sbuf_tensor("wob", [128, 4, DT, 512], BF16) as wob, \
                 nc.sbuf_tensor("a4x", [128, 2, D], F32) as a4x, \
                 nc.sbuf_tensor("x1b", [128, 2, D], F32) as x1b, \
                 nc.sbuf_tensor("a4st", [128, 2, 4, 6], F32) as a4st, \
                 nc.sbuf_tensor("a4ag", [128, 2, 2], F32) as a4ag, \
                 nc.sbuf_tensor("a4rs", [128, 2, 2], F32) as a4rs, \
                 nc.sbuf_tensor("resb", [128, 2, D], F32) as resb, \
                 nc.sbuf_tensor("xt0b", [128, 2, D], BF16) as xt0b, \
                 nc.sbuf_tensor("bseb", [128, 2, D], F32) as bseb:

                @block.sync
                def _(s):
                    if PH < 4:
                        return
                    pid = s.partition_id()
                    with s.register("rq") as rq, s.register("rqi0") as rqi0, \
                            s.register("rqi1") as rqi1:
                        s.reg_alu(rq, pid, 1, ALU.bitwise_and)
                        s.reg_mul(rq, rq, 2)
                        s.reg_alu(rqi0, rq, 0, ALU.add)
                        s.reg_alu(rqi1, rq, 1, ALU.add)
                        p.wait(s, "cg", 4)
                        p.wait(s, "acons", A3_CONS)
                        for qi, rr in ((0, rqi0), (1, rqi1)):
                            for ch in range(2):
                                p.inc(s.dma_start(
                                    out=gqb[:, qi, 8 * ch:8 * (ch + 1), :],
                                    in_=attn_full[ds(RuntimeValue(rr, 0, 3), 1),
                                                  1024 * ch:1024 * (ch + 1),
                                                  :].rearrange(
                                        "one (ct pp) j -> pp (one ct) j",
                                        pp=128)), "gq", 16)
                    for lt in range(TTC):
                        if lt >= 2:
                            p.wait(s, "ap4", 3 * (lt - 1))
                        p.inc(s.dma_start(
                            out=a4x[:, lt % 2, :],
                            in_=x_mine[lt * 128:(lt + 1) * 128, :]),
                            f"ld{lt % 2}", 16)

                @block.gpsimd
                def _(g):
                    if PH < 4:
                        return
                    p.wait(g, "acons", A3_CONS)
                    for qq in range(4):
                        p.wait(g, f"am{qq}", 32)
                    p.inc(g.dma_start(out=wob[:, :], in_=wo_pm), "wo4", 16)

                @block.tensor
                def _(t):
                    if PH < 4:
                        return
                    p.wait(t, "wo4", 16)
                    for lt in range(TTC):
                        qi, ltq = divmod(lt, 8)
                        p.wait(t, "gq", 64)
                        for dc in range(4):
                            i = lt * 4 + dc
                            if i >= 2:
                                p.wait(t, "ev4", i - 1)
                            for ct in range(DT):
                                ins = t.matmul(
                                    pM[i % 2][:, :],
                                    gqb[:, qi, ct,
                                        ltq * 128:(ltq + 1) * 128],
                                    wob[:, dc, ct, :],
                                    start=(ct == 0), stop=(ct == DT - 1))
                            p.inc(ins, "mm4")

                def a4_apply(v, m):
                    b = m % 2
                    if m >= 2:
                        p.wait(v, f"wrs{m % 2}", 48 * (m // 2))
                    p.wait(v, "rs4", 2 * (m + 1))
                    p.inc(v.tensor_scalar(resb[:, b, :], x1b[:, b, :],
                                          a4ag[:, b, 0:1], a4rs[:, b, 1:2],
                                          ALU.subtract, ALU.mult), "ap4")
                    p.wait(v, "ap4", p.n["ap4"])
                    p.inc(v.tensor_copy(xt0b[:, b, :], resb[:, b, :]), "ap4")
                    p.inc(v.tensor_add(bseb[:, b, :], x1b[:, b, :],
                                       resb[:, b, :]), "ap4")

                @block.vector
                def _(v):
                    if PH < 4:
                        return
                    for lt in range(TTC):
                        b = lt % 2
                        p.wait(v, f"ld{lt % 2}", 16 * (16 + lt // 2 + 1))
                        for dc in range(4):
                            i = lt * 4 + dc
                            p.wait(v, "mm4", i + 1)
                            p.inc(v.scalar_tensor_tensor(
                                x1b[:, b, dc * 512:(dc + 1) * 512],
                                pM[i % 2][:, :], 1.0,
                                a4x[:, b, dc * 512:(dc + 1) * 512],
                                ALU.mult, ALU.add), "ev4")
                        p.wait(v, "ev4", p.n["ev4"])
                        for i in range(4):
                            ins = v.bn_stats(a4st[:, b, i, :],
                                             x1b[:, b, 512 * i:512 * (i + 1)])
                        p.inc(ins, "ln4")
                        p.wait(v, "ln4", p.n["ln4"])
                        if lt >= 2:
                            p.wait(v, "rs4", 2 * (lt - 1))
                        p.inc(v.bn_aggr(a4ag[:, b, :], a4st[:, b, :, :]),
                              "ln4")
                        if lt >= 1:
                            a4_apply(v, lt - 1)
                    a4_apply(v, TTC - 1)

                @block.scalar
                def _(sc):
                    if PH < 4:
                        return
                    for lt in range(TTC):
                        b = lt % 2
                        p.wait(sc, "ln4", 2 * (lt + 1))
                        p.inc(sc.activation(a4rs[:, b, 0:1], a4ag[:, b, 1:2],
                                            AF.Ln, bias=eps_sb[:, 0:1]), "rs4")
                        p.wait(sc, "rs4", p.n["rs4"])
                        p.inc(sc.activation(a4rs[:, b, 1:2], a4rs[:, b, 0:1],
                                            AF.Exp, scale=-0.5), "rs4")
                        rows = slice(lt * 128, (lt + 1) * 128)
                        p.wait(sc, "ap4", 3 * (lt + 1))
                        p.inc(sc.dma_start(out=res_tm[rows, :],
                                           in_=resb[:, b, :]), f"wrs{b}", 16)
                        p.inc(sc.dma_start(out=xt0_tm[rows, :],
                                           in_=xt0b[:, b, :]), f"wrs{b}", 16)
                        p.inc(sc.dma_start(out=base_tm[rows, :],
                                           in_=bseb[:, b, :]), f"wrs{b}", 16)
            A4_MM_TOTAL = p.n["mm4"]

            with nc.sbuf_tensor("xtT", [128, 2, DT, G], BF16) as xtT, \
                 nc.sbuf_tensor("hbuf", [128, 2, ET, G], BF16) as hbuf, \
                 nc.sbuf_tensor("h2w", [128, ET, 8, G], BF16) as h2w, \
                 nc.sbuf_tensor("warA", [128, 2, DT, 512], BF16) as warA, \
                 nc.sbuf_tensor("warB", [128, 2, ET, 512], BF16) as warB, \
                 nc.sbuf_tensor("wsdb", [128, 2, ST, 256], BF16) as wsdb, \
                 nc.sbuf_tensor("wrep", [128, NX, 512], BF16) as wrep, \
                 nc.sbuf_tensor("resg", [128, 2, 4, 256], F32) as resg, \
                 nc.sbuf_tensor("stg", [128, 4, 256], F32) as stg, \
                 nc.sbuf_tensor("stgb", [128, 4, 256], BF16) as stgb, \
                 nc.sbuf_tensor("gw", [128, 4, 8], F32) as gw, \
                 nc.sbuf_tensor("gmx", [128, 4, 8], F32) as gmx, \
                 nc.sbuf_tensor("gcmp", [128, 4, 8], F32) as gcmp, \
                 nc.sbuf_tensor("gnm", [128, 4, 2], F32) as gnm, \
                 nc.sbuf_tensor("wT8", [8, G], BF16) as wT8, \
                 nc.sbuf_tensor("bdnb", [NX, 2, 256], BF16) as bdnb, \
                 nc.sbuf_tensor("bsdb", [1, 2, 256], BF16) as bsdb:

                def emit_h(t, gi, n):
                    for et in range(ET):
                        u = n * ET + et
                        i1 = gi * 16 + 2 * n + et // 4
                        p.wait(t, f"wlA{i1 % 2}", 16 * (i1 // 2 + 1))
                        p.wait(t, "fhev", 64 * gi + u - 3)
                        if n == 0 and et == 0:
                            p.wait(t, "gtv", 4 * (gi + 1))
                        if n == 0 and et == 2:
                            p.wait(t, "gts", gi + 1)
                        for dt in range(DT):
                            ins = t.matmul(
                                PB4[u % 4],
                                warA[:, i1 % 2, dt,
                                     (et % 4) * 128:(et % 4) * 128 + 128],
                                xtT[:, gi % 2, dt, :],
                                start=(dt == 0), stop=(dt == DT - 1))
                        p.inc(ins, "fh")

                def emit_s(t, gi):
                    for ot in range(ST):
                        u = 48 + ot
                        i1 = gi * 16 + 12 + ot // 4
                        p.wait(t, f"wlA{i1 % 2}", 16 * (i1 // 2 + 1))
                        p.wait(t, "fhev", 64 * gi + u - 3)
                        for dt in range(DT):
                            ins = t.matmul(
                                PB4[u % 4],
                                warA[:, i1 % 2, dt,
                                     (ot % 4) * 128:(ot % 4) * 128 + 128],
                                xtT[:, gi % 2, dt, :],
                                start=(dt == 0), stop=(dt == DT - 1))
                        p.inc(ins, "fh")

                def emit_z(t, gi, n):
                    for ft in range(ET):
                        u = n * ET + ft
                        i2 = gi * 14 + 2 * n + ft // 4
                        p.wait(t, f"wlB{i2 % 2}", 16 * (i2 // 2 + 1))
                        p.wait(t, "fhev", 64 * gi + ET * (n + 1))
                        p.wait(t, "fzev", 64 * gi + u - 3)
                        if u < 4:
                            p.wait(t, "wrp", 6 * (gi + 1))
                        for et in range(ET):
                            ins = t.matmul(
                                pM[u % 4][:, :],
                                warB[:, i2 % 2, et,
                                     (ft % 4) * 128:(ft % 4) * 128 + 128],
                                hbuf[:, n % 2, et, :],
                                start=(et == 0), stop=(et == ET - 1))
                        p.inc(ins, "fz")

                def emit_zs(t, gi):
                    for k in range(2):
                        for ft in range(ET):
                            u = 48 + k * ET + ft
                            i2 = gi * 14 + 12 + ft // 4
                            p.wait(t, f"wlB{i2 % 2}", 16 * (i2 // 2 + 1))
                            p.wait(t, "fhev", 64 * gi + 48 + ET * (k + 1))
                            p.wait(t, "fzev", 64 * gi + u - 3)
                            for et in range(ET):
                                ins = t.matmul(
                                    pM[u % 4][:, :],
                                    warB[:, i2 % 2, et,
                                         (ft % 4) * 128:(ft % 4) * 128 + 128],
                                    hbuf[:, k, et, :],
                                    start=(et == 0), stop=(et == ET - 1))
                            p.inc(ins, "fz")

                for j in range(2 if PH >= 5 else 0):
                    for grp in range(NG):
                        gi = j * NG + grp
                        if gi >= FFN_GROUPS:
                            continue
                        src_tm = xt0_tm if j == 0 else xt1_tm
                        rows = slice(grp * G, (grp + 1) * G)

                        @block.sync
                        def _(s, j=j, grp=grp, gi=gi, src_tm=src_tm,
                              rows=rows):
                            if j == 1:
                                for sl in range(4):
                                    p.wait(s, f"fq{sl}", 16 * 8 * (grp + 1))
                            if gi == 0:
                                p.wait(s, "mm4", A4_MM_TOTAL)
                                p.wait(s, "wrs0", 48 * 8)
                                p.wait(s, "wrs1", 48 * 8)
                                p.wait(s, "ap4", 3 * TTC)
                                p.wait(s, "b0v", 2)
                                p.wait(s, "p0", P0_TOTAL)
                            if gi >= 2:
                                p.wait(s, "fhw", 112 * (gi - 1))
                            for dt in range(DT):
                                p.inc(s.dma_start(
                                    out=xtT[:, gi % 2, dt, :],
                                    in_=src_tm[rows,
                                               dt * 128:(dt + 1) * 128],
                                    transpose=True), f"xtb{gi % 2}", 16)
                            lorder = []
                            for n in range(NX):
                                lorder += [("wl1", 2 * n), ("wl1", 2 * n + 1)]
                                if n >= 1:
                                    lorder += [("wl2", 2 * n - 2),
                                               ("wl2", 2 * n - 1)]
                            lorder += [("wl2", 10), ("wl2", 11),
                                       ("wl1", 12), ("wl1", 13),
                                       ("wl1", 14), ("wl1", 15),
                                       ("wl2", 12), ("wl2", 13)]
                            p.wait(s, "p2", 2112 * gi)
                            for kind, li in lorder:
                                if kind == "wl1":
                                    i1 = gi * 16 + li
                                    if i1 >= 2:
                                        pg, pl = divmod(i1 - 2, 16)
                                        p.wait(s, "fh", 64 * pg + 4 * pl + 4)
                                    if li < 12:
                                        n, eh = divmod(li, 2)
                                        src = wsc_up[:, n, eh]
                                    else:
                                        src = wsc_su[:, li - 12]
                                    p.inc(s.dma_start(out=warA[:, i1 % 2],
                                                      in_=src),
                                          f"wlA{i1 % 2}", 16)
                                else:
                                    i2 = gi * 14 + li
                                    if i2 >= 2:
                                        pg, pl = divmod(i2 - 2, 14)
                                        after = (4 * pl + 4 if pl < 12
                                                 else (60 if pl == 12 else 64))
                                        p.wait(s, "fz", 64 * pg + after)
                                    if li < 12:
                                        n, fh = divmod(li, 2)
                                        src = wsc_sw[:, n, fh]
                                    else:
                                        src = wsc_sa[:, li - 12]
                                    p.inc(s.dma_start(out=warB[:, i2 % 2],
                                                      in_=src),
                                          f"wlB{i2 % 2}", 16)
                            for d8 in range(8):
                                i3 = gi * 8 + d8
                                par = d8 % 2
                                if d8 == 0:
                                    p.wait(s, "fh", 64 * (gi + 1))
                                    p.wait(s, "fz", 64 * (gi + 1))
                                if i3 >= 2:
                                    p.wait(s, "p2", 264 * (i3 - 1))
                                p.inc(s.dma_start(
                                    out=warA[:, par].rearrange(
                                        "pp a c -> pp (a c)"),
                                    in_=wsc_dn[:, d8, 0:4].rearrange(
                                        "pp a b c -> pp (a b c)")),
                                    f"wlC{d8 % 2}", 16)
                                p.inc(s.dma_start(
                                    out=warB[:, par].rearrange(
                                        "pp a c -> pp (a c)"),
                                    in_=wsc_dn[:, d8, 4:6].rearrange(
                                        "pp a b c -> pp (a b c)")),
                                    f"wlC{d8 % 2}", 16)
                                p.inc(s.dma_start(out=wsdb[:, par],
                                                  in_=wsc_sd[:, d8]),
                                      f"wlC{d8 % 2}", 16)
                                p.inc(s.dma_start(
                                    out=bdnb[:, par, :],
                                    in_=bdn_sc[:, d8 * 256:(d8 + 1) * 256]),
                                    f"wlC{d8 % 2}", 16)
                                p.inc(s.dma_start(
                                    out=bsdb[:, par, :],
                                    in_=bsd_sc[:, d8 * 256:(d8 + 1) * 256]),
                                    f"wlC{d8 % 2}", 16)
                                rsrc = res_tm if j == 0 else base_tm
                                if i3 >= 2:
                                    p.wait(s, "p2ev", 4 * (i3 - 1))
                                p.inc(s.dma_start(
                                    out=resg[:, par],
                                    in_=rsrc[rows, d8 * 256:(d8 + 1) * 256]
                                    .rearrange("(tt pp) c -> pp tt c",
                                               pp=128)), f"rsg{d8 % 2}", 16)

                        @block.tensor
                        def _(t, j=j, grp=grp, gi=gi):
                            b = gi % 2
                            p.wait(t, f"xtb{gi % 2}", 16 * 16 * (gi // 2 + 1))
                            p.wait(t, "p2ev", 32 * gi)
                            if gi == 0:
                                p.wait(t, "ev4", 64)
                            for tt in range(4):
                                p.wait(t, "gtv", 4 * gi + tt)
                                if gi == 0 and tt == 0:
                                    p.wait(t, "b0v", 2)
                                for dt in range(DT):
                                    ins = t.matmul(
                                        pBig0[:, 0:8],
                                        xtT[:, b, dt,
                                            tt * 128:(tt + 1) * 128],
                                        wg_sb[:, dt, j, :],
                                        start=(dt == 0),
                                        stop=(dt == DT - 1))
                                p.inc(ins, "gt")
                            for tt in range(4):
                                p.wait(t, "gtw", 4 * gi + tt + 1)
                                p.inc(t.transpose(
                                    pBig1[0:8, tt * 128:(tt + 1) * 128],
                                    gw[:, tt, :], ident_sb[:, :]), "gt")
                            p.wait(t, "gts", gi + 1)
                            for n in range(NX):
                                if 6 * gi + n >= 4:
                                    p.wait(t, "wrp", 6 * gi + n - 3)
                                p.inc(t.matmul(
                                    pM[n % 4][:, :],
                                    sel_sb[:, n * 128:(n + 1) * 128],
                                    wT8[0:8, :], start=True, stop=True),
                                    "gt")
                            emit_h(t, gi, 0)
                            for n in range(1, NX):
                                emit_h(t, gi, n)
                                emit_z(t, gi, n - 1)
                            emit_z(t, gi, NX - 1)
                            emit_s(t, gi)
                            emit_zs(t, gi)
                            p.wait(t, "fhw", 112 * (gi + 1))
                            for d8 in range(8):
                                i3 = gi * 8 + d8
                                par = d8 % 2
                                p.wait(t, f"wlC{d8 % 2}", 80 * (gi * 4 + d8 // 2 + 1))
                                p.wait(t, "p2ev", 4 * (i3 - 1))
                                wAf = warA[:, par].rearrange(
                                    "pp a c -> pp (a c)")
                                wBf = warB[:, par].rearrange(
                                    "pp a c -> pp (a c)")
                                first = True
                                for n in range(NX):
                                    for et in range(ET):
                                        idx = n * ET + et
                                        if idx < 32:
                                            rhs = wAf[:, idx * 256:
                                                      (idx + 1) * 256]
                                        else:
                                            rhs = wBf[:, (idx - 32) * 256:
                                                      (idx - 31) * 256]
                                        for tt in range(4):
                                            t.matmul(
                                                BANK8[2 * tt + par][:, 0:256],
                                                h2w[:, et, n,
                                                    tt * 128:(tt + 1) * 128],
                                                rhs, start=first, stop=False)
                                        first = False
                                for st in range(ST):
                                    k, ft = divmod(st, ET)
                                    for tt in range(4):
                                        t.matmul(
                                            BANK8[2 * tt + par][:, 0:256],
                                            h2w[:, ft, 6 + k,
                                                tt * 128:(tt + 1) * 128],
                                            wsdb[:, par, st, :],
                                            start=False, stop=False)
                                for tt in range(4):
                                    t.matmul(
                                        BANK8[2 * tt + par][:, 0:256],
                                        wT8[0:6, tt * 128:(tt + 1) * 128],
                                        bdnb[0:6, par, :],
                                        start=False, stop=False)
                                    ins = t.matmul(
                                        BANK8[2 * tt + par][:, 0:256],
                                        ones_sb[0:1, :],
                                        bsdb[0:1, par, :],
                                        start=False, stop=True)
                                p.inc(ins, "p2", 264)

                        @block.scalar
                        def _(sc, j=j, grp=grp, gi=gi):
                            for tt in range(4):
                                p.wait(sc, "gex", 4 * gi + tt + 1)
                                p.inc(sc.activation(
                                    gmx[:, tt, :], gw[:, tt, :], AF.Exp,
                                    bias=gnm[:, tt, 0:1]), "gey")

                            def hev(n):
                                for et in range(ET):
                                    u = n * ET + et
                                    p.wait(sc, "fh", 64 * gi + u + 1)
                                    if u == 0:
                                        p.wait(sc, "fhw", 112 * gi)
                                    if n >= 2 and et == 0:
                                        p.wait(sc, "fz",
                                               64 * gi + ET * (n - 1))
                                        p.wait(sc, "fhw",
                                               112 * gi + 16 * (n - 1))
                                    p.inc(sc.activation(
                                        hbuf[:, n % 2, et, :], PB4[u % 4],
                                        AF.Identity,
                                        bias=bup_sb[:, n, et:et + 1]),
                                        "fhev")

                            def sev(half):
                                for ot in range(8 * half, 8 * half + 8):
                                    u = 48 + ot
                                    p.wait(sc, "fh", 64 * gi + u + 1)
                                    if ot == 0:
                                        p.wait(sc, "fz", 64 * gi + 40)
                                        p.wait(sc, "fhw", 112 * gi + 80)
                                    if ot == 8:
                                        p.wait(sc, "fz", 64 * gi + 48)
                                        p.wait(sc, "fhw", 112 * gi + 96)
                                    p.inc(sc.activation(
                                        hbuf[:, ot // 8, ot % 8, :],
                                        PB4[u % 4], AF.Identity,
                                        bias=bsu_sb[:, ot:ot + 1]), "fhev")

                            def zev(n):
                                for ft in range(ET):
                                    u = n * ET + ft
                                    p.wait(sc, "fz", 64 * gi + u + 1)
                                    if u == 0:
                                        p.wait(sc, "p2", 2112 * gi)
                                    p.inc(sc.activation(
                                        h2w[:, ft, n, :], pM[u % 4][:, :],
                                        AF.Silu,
                                        bias=bsw_sb[:, n, ft:ft + 1]),
                                        "fzev")

                            def zsev():
                                for k in range(2):
                                    for ft in range(ET):
                                        u = 48 + k * ET + ft
                                        p.wait(sc, "fz", 64 * gi + u + 1)
                                        p.inc(sc.activation(
                                            h2w[:, ft, 6 + k, :],
                                            pM[u % 4][:, :], AF.Silu,
                                            bias=bsa_sb[:, ft:ft + 1]),
                                            "fzev")

                            hev(0)
                            for n in range(1, NX):
                                hev(n)
                                zev(n - 1)
                            zev(NX - 1)
                            sev(0)
                            sev(1)
                            zsev()
                            for d8 in range(8):
                                i3 = gi * 8 + d8
                                for tt in range(4):
                                    p.wait(sc, "p2ev", 4 * i3 + tt + 1)
                                    rr = slice(grp * G + tt * 128,
                                               grp * G + (tt + 1) * 128)
                                    cc = slice(d8 * 256, (d8 + 1) * 256)
                                    if j == 0:
                                        ins = sc.dma_start(
                                            out=xt1_tm[rr, cc],
                                            in_=stgb[:, tt, :])
                                        p.inc(ins, f"fq{tt}", 16)
                                    else:
                                        ins = sc.dma_start(
                                            out=y_mine[rr, cc],
                                            in_=stg[:, tt, :])
                                        p.inc(ins, f"oq{tt}", 16)

                        @block.vector
                        def _(v, j=j, grp=grp, gi=gi):
                            for tt in range(4):
                                p.wait(v, "gt", 14 * gi + tt + 1)
                                p.inc(v.tensor_add(gw[:, tt, :],
                                                   pBig0[:, 0:8],
                                                   bgr_sb[:, j, :]), "gtv")
                                p.wait(v, "gtv", p.n["gtv"])
                                p.inc(v.max(gmx[:, tt, :], gw[:, tt, :]),
                                      "gch")
                                p.wait(v, "gch", p.n["gch"])
                                p.inc(v.tensor_scalar(
                                    gcmp[:, tt, :], gw[:, tt, :],
                                    gmx[:, tt, 5:6], None, ALU.is_ge),
                                    "gch")
                                p.inc(v.tensor_scalar_mul(
                                    gnm[:, tt, 0:1], gmx[:, tt, 0:1], -1.0),
                                    "gex")
                            for tt in range(4):
                                p.wait(v, "gey", 4 * gi + tt + 1)
                                p.wait(v, "gch", p.n["gch"])
                                p.inc(v.tensor_mul(gw[:, tt, :],
                                                   gmx[:, tt, :],
                                                   gcmp[:, tt, :]), "gch")
                                p.wait(v, "gch", p.n["gch"])
                                p.inc(v.reduce_sum(gnm[:, tt, 1:2],
                                                   gw[:, tt, :],
                                                   axis=AX.X), "gch")
                                p.wait(v, "gch", p.n["gch"])
                                p.inc(v.reciprocal(gnm[:, tt, 0:1],
                                                   gnm[:, tt, 1:2]), "gch")
                                p.wait(v, "gch", p.n["gch"])
                                p.inc(v.tensor_scalar_mul(
                                    gw[:, tt, :], gw[:, tt, :],
                                    gnm[:, tt, 0:1]), "gtw")
                            p.wait(v, "gt", 14 * gi + 8)
                            p.inc(v.tensor_copy(wT8[0:8, :],
                                                pBig1[0:8, 0:512]), "gts")
                            for n in range(NX):
                                p.wait(v, "gt", 14 * gi + 8 + n + 1)
                                p.inc(v.tensor_copy(wrep[:, n, :],
                                                    pM[n % 4][:, :]), "wrp")
                            for u in range(64):
                                p.wait(v, "fzev", 64 * gi + u + 1)
                                if u < 48:
                                    n, ft = divmod(u, ET)
                                    p.wait(v, "wrp", 6 * gi + n + 1)
                                    p.inc(v.tensor_mul(
                                        h2w[:, ft, n, :], h2w[:, ft, n, :],
                                        hbuf[:, n % 2, ft, :]), "fhw")
                                    p.wait(v, "fhw", p.n["fhw"])
                                    p.inc(v.tensor_mul(
                                        h2w[:, ft, n, :], h2w[:, ft, n, :],
                                        wrep[:, n, :]), "fhw")
                                else:
                                    k, ft = divmod(u - 48, ET)
                                    p.inc(v.tensor_mul(
                                        h2w[:, ft, 6 + k, :],
                                        h2w[:, ft, 6 + k, :],
                                        hbuf[:, k, ft, :]), "fhw")
                            for d8 in range(8):
                                i3 = gi * 8 + d8
                                par = d8 % 2
                                p.wait(v, f"rsg{d8 % 2}", 16 * (gi * 4 + d8 // 2 + 1))
                                p.wait(v, "p2", 264 * (i3 + 1))
                                for tt in range(4):
                                    i3l = grp * 8 + d8
                                    if i3l >= 1:
                                        p.wait(v,
                                               (f"fq{tt}" if j == 0
                                                else f"oq{tt}"), 16 * i3l)
                                    dst = stgb if j == 0 else stg
                                    p.inc(v.scalar_tensor_tensor(
                                        dst[:, tt, :],
                                        BANK8[2 * tt + par][:, 0:256], 1.0,
                                        resg[:, par, tt, :],
                                        ALU.mult, ALU.add), "p2ev")

            @block.sync
            def _(s):
                if PH >= 5:
                    p.wait(s, "p2ev", p.n["p2ev"])
                    for sl in range(4):
                        p.wait(s, f"fq{sl}", p.n[f"fq{sl}"])
                        p.wait(s, f"oq{sl}", p.n[f"oq{sl}"])
                elif PH == 4:
                    p.wait(s, "wrs0", p.n["wrs0"])
                    p.wait(s, "wrs1", p.n["wrs1"])
                elif PH == 3:
                    p.wait(s, "cg", 4)
                elif PH == 2:
                    for sl in range(4):
                        p.wait(s, f"spl{sl}", p.n[f"spl{sl}"])
                else:
                    p.wait(s, "wxh0", p.n["wxh0"])
                    p.wait(s, "wxh1", p.n["wxh1"])

    return nc


def _prep(inputs, core):
    b, g = core // 2, core % 2
    A = np.ascontiguousarray
    x = np.asarray(inputs["L0_attn_x"])

    def pm_qkv(w):
        wt = np.asarray(w).T[:, g * 1024:(g + 1) * 1024]
        return A(wt.reshape(DT, 128, 1024).transpose(1, 0, 2))

    d = {
        "x_full": A(x[b]),
        "x_mine": A(x[b, g * TC:(g + 1) * TC]),
        "wq_pm": pm_qkv(inputs["L0_attn_wq"]),
        "wk_pm": pm_qkv(inputs["L0_attn_wk"]),
        "wv_pm": pm_qkv(inputs["L0_attn_wv"]),
        "wo_pm": A(np.asarray(inputs["L0_attn_wo"]).T
                   .reshape(DT, 128, 4, 512).transpose(1, 2, 0, 3)),
        "wg_pm": A(np.asarray(inputs["L0_ffn_Wg"]).transpose(1, 0, 2)
                   .reshape(DT, 128, 2, 8).transpose(1, 0, 2, 3)),
        "wup_pm": A(np.asarray(inputs["L0_ffn_Wup"])
                    .reshape(NX, DT, 128, 2, 512).transpose(2, 0, 3, 1, 4)),
        "wsw_pm": A(np.asarray(inputs["L0_ffn_Wsw"])
                    .reshape(NX, ET, 128, 2, 512).transpose(2, 0, 3, 1, 4)),
        "wsu_pm": A(np.asarray(inputs["L0_ffn_Wsu"])
                    .reshape(DT, 128, 4, 512).transpose(1, 2, 0, 3)),
        "wsa_pm": A(np.asarray(inputs["L0_ffn_Wsa"])
                    .reshape(ET, 128, 2, 512).transpose(1, 2, 0, 3)),
        "wdn_pm": A(np.asarray(inputs["L0_ffn_Wdn"])
                    .reshape(NX, ET, 128, 8, 256).transpose(2, 3, 0, 1, 4)),
        "wsd_pm": A(np.asarray(inputs["L0_ffn_Wsd"])
                    .reshape(ST, 128, 8, 256).transpose(1, 2, 0, 3)),
        "bupT": A(np.asarray(inputs["L0_ffn_bup"]).reshape(NX, ET, 128)
                  .transpose(2, 0, 1)),
        "bswT": A(np.asarray(inputs["L0_ffn_bsw"]).reshape(NX, ET, 128)
                  .transpose(2, 0, 1)),
        "bsuT": A(np.asarray(inputs["L0_ffn_bsu"]).reshape(ST, 128).T),
        "bsaT": A(np.asarray(inputs["L0_ffn_bsa"]).reshape(ET, 128).T),
        "bg0": A(np.asarray(inputs["L0_ffn_bg"])[0:1, :]),
        "bg1": A(np.asarray(inputs["L0_ffn_bg"])[1:2, :]),
        "bdn_r": A(np.asarray(inputs["L0_ffn_bdn"])),
        "bsd_r": A(np.asarray(inputs["L0_ffn_bsd"]).reshape(1, D)),
        "c_ident": np.eye(128, dtype=np.float32),
        "c_ones": np.ones((128, 128), dtype=ml_dtypes.bfloat16),
    }
    band = (np.arange(128)[:, None] <= (np.arange(896)[None, :] - 384))
    d["c_band"] = band.astype(ml_dtypes.bfloat16)
    sel = np.zeros((8, NX * 128), dtype=ml_dtypes.bfloat16)
    for n in range(NX):
        sel[n, n * 128:(n + 1) * 128] = 1
    d["c_sel"] = sel
    return d


def kernel(**inputs):
    nc = bass.Bass("TRN2", target_bir_lowering=False)
    build(nc)
    in_maps = [_prep(inputs, c) for c in range(8)]
    res = run_bass_kernel_spmd(nc, in_maps, core_ids=list(range(8)))
    out = np.zeros((4, TB, D), dtype=np.float32)
    for c in range(8):
        b, g = c // 2, c % 2
        out[b, g * TC:(g + 1) * TC] = np.asarray(res.results[c]["y_mine"])
    return out


# version 10
# speedup vs baseline: 1.0414x; 1.0414x over previous
import numpy as np
import ml_dtypes
from contextlib import ExitStack
import concourse.bass as bass
import concourse.mybir as mybir
from concourse.bass_utils import run_bass_kernel_spmd
from concourse.expressions import RuntimeValue

F32 = mybir.dt.float32
BF16 = mybir.dt.bfloat16
AF = mybir.ActivationFunctionType
ALU = mybir.AluOpType
AX = mybir.AxisListType
ds = bass.ds

D = 2048
DT = 16
TB = 4096
TTB = 32
TC = 2048
TTC = 16
NHC = 8
KT = 32
E = 1024
ET = 8
NX = 6
G = 512
NG = 4
ST = 16
SCALE = float(1.0 / np.sqrt(128.0))
EPS = 1e-5
RG_PAIRS = [[0, 1], [2, 3], [4, 5], [6, 7]]
PHASES = 99
FFN_GROUPS = 99

SEMS = [
    "b0", "b0v", "p0",
    "ld0", "ld1", "lnv", "rs1", "apl", "wxh0", "wxh1",
    "txl0", "txl1", "wqkv", "mm2", "ev2",
    "spl0", "spl1", "spl2", "spl3",
    "hld0", "hld1", "hcons", "aS", "aexp", "amask", "acons", "aev",
    "am0", "am1", "am2", "am3", "cg",
    "gq", "wo4", "mm4", "ev4", "ln4", "rs4", "ap4", "wrs0", "wrs1",
    "xtb0", "xtb1", "wlA0", "wlA1", "wlB0", "wlB1", "wlC0", "wlC1",
    "rsg0", "rsg1",
    "gt", "gtv", "gch", "gex", "gey", "gtw", "gts", "wrp",
    "fh", "fhev", "fz", "fzev", "fhw", "p2", "p2ev",
    "fq0", "fq1", "fq2", "fq3", "oq0", "oq1", "oq2", "oq3",
]


class P:
    def __init__(self, nc, stack, names):
        self.sem = {}
        self.n = {}
        for s in names:
            self.sem[s] = stack.enter_context(nc.semaphore(s))
            self.n[s] = 0

    def inc(self, instr, name, k=1):
        instr.then_inc(self.sem[name], k)
        self.n[name] += k
        return instr

    def wait(self, eng, name, val=None):
        v = self.n[name] if val is None else val
        if v > 0:
            eng.wait_ge(self.sem[name], v)


def build(nc):
    PH = PHASES
    def inp(name, shape, dt=F32):
        return nc.dram_tensor(name, list(shape), dt, kind="ExternalInput").ap()

    x_full = inp("x_full", (TB, D))
    x_mine = inp("x_mine", (TC, D))
    wq_pm = inp("wq_pm", (128, DT, NHC * 128))
    wk_pm = inp("wk_pm", (128, DT, NHC * 128))
    wv_pm = inp("wv_pm", (128, DT, NHC * 128))
    wo_pm = inp("wo_pm", (128, 4, DT, 512))
    wg_pm = inp("wg_pm", (128, DT, 2, 8))
    wup_pm = inp("wup_pm", (128, NX, 2, DT, 512))
    wsw_pm = inp("wsw_pm", (128, NX, 2, ET, 512))
    wsu_pm = inp("wsu_pm", (128, 4, DT, 512))
    wsa_pm = inp("wsa_pm", (128, 2, ET, 512))
    wdn_pm = inp("wdn_pm", (128, 8, NX, ET, 256))
    wsd_pm = inp("wsd_pm", (128, 8, ST, 256))
    bupT = inp("bupT", (128, NX, ET))
    bswT = inp("bswT", (128, NX, ET))
    bsuT = inp("bsuT", (128, ST))
    bsaT = inp("bsaT", (128, ET))
    bg0 = inp("bg0", (1, 8))
    bg1 = inp("bg1", (1, 8))
    bdn_r = inp("bdn_r", (NX, D))
    bsd_r = inp("bsd_r", (1, D))
    c_ident = inp("c_ident", (128, 128))
    c_ones = inp("c_ones", (128, 128), BF16)
    c_band = inp("c_band", (128, 896), BF16)
    c_sel = inp("c_sel", (8, NX * 128), BF16)

    y_mine = nc.dram_tensor("y_mine", [TC, D], F32, kind="ExternalOutput").ap()

    def scr(name, shape, dt):
        return nc.dram_tensor(name, list(shape), dt, kind="Internal").ap()

    xln_tm = scr("xln_tm", (TB, D), BF16)
    kT_sc = scr("kT_sc", (NHC, 128, TB), BF16)
    qT_sc = scr("qT_sc", (NHC, 128, TB), BF16)
    v_sc = scr("v_sc", (NHC, 128, KT, 128), BF16)
    attn_mine = scr("attn_mine", (4, NHC * 128, 1024), BF16)
    attn_full = scr("attn_full", (4, D, 1024), BF16)
    res_tm = scr("res_tm", (TC, D), F32)
    base_tm = scr("base_tm", (TC, D), F32)
    xt0_tm = scr("xt0_tm", (TC, D), BF16)
    xt1_tm = scr("xt1_tm", (TC, D), BF16)
    wsc_up = scr("wsc_up", (128, NX, 2, DT, 512), BF16)
    wsc_sw = scr("wsc_sw", (128, NX, 2, ET, 512), BF16)
    wsc_su = scr("wsc_su", (128, 4, DT, 512), BF16)
    wsc_sa = scr("wsc_sa", (128, 2, ET, 512), BF16)
    wsc_dn = scr("wsc_dn", (128, 8, NX, ET, 256), BF16)
    wsc_sd = scr("wsc_sd", (128, 8, ST, 256), BF16)
    bdn_sc = scr("bdn_sc", (NX, D), BF16)
    bsd_sc = scr("bsd_sc", (1, D), BF16)

    a3 = []
    ACONS_END = {}
    cS = cexp = ccons = cmask = cev = 0
    for q in range(4):
        for h in range(NHC):
            for ci in range(2):
                s = 2 * q + ci
                npair = 2 * (s + 1)
                a3.append(dict(q=q, h=h, ci=ci, s=s, npair=npair,
                               Sb=cS, Eb=cexp, Cb=ccons, Mb=cmask, Vb=cev,
                               Ce=ccons + npair))
                cS += npair
                cexp += npair
                ccons += npair
                cmask += 2
                cev += 2
                if ci == 1:
                    ACONS_END[q * NHC + h] = ccons

    with ExitStack() as stk:
        p = P(nc, stk, SEMS)

        def T(name, shape, dt=F32):
            return stk.enter_context(nc.sbuf_tensor(name, list(shape), dt))

        bup_sb = T("bup_sb", (128, NX, ET))
        bsw_sb = T("bsw_sb", (128, NX, ET))
        bsu_sb = T("bsu_sb", (128, ST))
        bsa_sb = T("bsa_sb", (128, ET))
        bgA_sb = T("bgA_sb", (1, 8), BF16)
        bgB_sb = T("bgB_sb", (1, 8), BF16)
        bgr_sb = T("bgr_sb", (128, 2, 8))
        eps_sb = T("eps_sb", (128, 1))
        wg_sb = T("wg_sb", (128, DT, 2, 8), BF16)
        ident_sb = T("ident_sb", (128, 128))
        ones_sb = T("ones_sb", (128, 128), BF16)
        band_sb = T("band_sb", (128, 896), BF16)
        sel_sb = T("sel_sb", (8, NX * 128), BF16)

        pBig0 = stk.enter_context(nc.psum_tensor("pBig0", [128, 1024], F32))
        pBig1 = stk.enter_context(nc.psum_tensor("pBig1", [128, 1024], F32))
        pM = [stk.enter_context(nc.psum_tensor(f"pM{i}", [128, 512], F32))
              for i in range(4)]
        PB4 = [pBig0[:, 0:512], pBig0[:, 512:1024],
               pBig1[:, 0:512], pBig1[:, 512:1024]]
        BANK8 = [pM[0][:, :], pM[1][:, :], pM[2][:, :], pM[3][:, :]] + PB4

        with nc.Block() as block:

            @block.tensor
            def _(t):
                p.wait(t, "b0", 176)
                t.matmul(pM[0][:, 0:8], ones_sb[0:1, :], bgA_sb[0:1, :],
                         start=True, stop=True)
                p.inc(t.matmul(pM[1][:, 0:8], ones_sb[0:1, :], bgB_sb[0:1, :],
                               start=True, stop=True), "b0v")

            @block.vector
            def _(v):
                v.memset(eps_sb[:, :], EPS)
                p.wait(v, "b0v", 1)
                v.tensor_copy(bgr_sb[:, 0, :], pM[0][:, 0:8])
                p.inc(v.tensor_copy(bgr_sb[:, 1, :], pM[1][:, 0:8]), "b0v")

            with nc.sbuf_tensor("xlnT", [128, DT, TB], BF16) as xlnT, \
                 nc.sbuf_tensor("a2st", [128, 4, 512], BF16) as a2st, \
                 nc.sbuf_tensor("a1x", [128, 2, D], F32) as a1x, \
                 nc.sbuf_tensor("a1st", [128, 2, 4, 6], F32) as a1st, \
                 nc.sbuf_tensor("a1ag", [128, 2, 2], F32) as a1ag, \
                 nc.sbuf_tensor("a1rs", [128, 2, 2], F32) as a1rs, \
                 nc.sbuf_tensor("a1o", [128, 2, D], BF16) as a1o, \
                 nc.sbuf_tensor("a2w", [128, DT, 1024], BF16) as a2w:
                @block.gpsimd
                def _(g):
                    p.inc(g.dma_start(out=a2w[:, :, :], in_=wk_pm), "wqkv", 16)
                    for dst, src in [
                                (bup_sb[:, :, :], bupT), (bsw_sb[:, :, :], bswT),
                                (bsu_sb[:, :], bsuT), (bsa_sb[:, :], bsaT),
                                (bgA_sb[0:1, :], bg0), (bgB_sb[0:1, :], bg1),
                                (wg_sb[:, :, :, :], wg_pm),
                                (ident_sb[:, :], c_ident), (ones_sb[:, :], c_ones),
                                (band_sb[:, :], c_band), (sel_sb[:, :], c_sel)]:
                        p.inc(g.dma_start(out=dst, in_=src), "b0", 16)
                    for n in range(NX):
                        p.inc(g.dma_start(out=wsc_up[:, n], in_=wup_pm[:, n]),
                                  "p0", 16)
                        p.inc(g.dma_start(out=wsc_sw[:, n], in_=wsw_pm[:, n]),
                                  "p0", 16)
                    p.inc(g.dma_start(out=wsc_su[:, :], in_=wsu_pm), "p0", 16)
                    p.inc(g.dma_start(out=wsc_sa[:, :], in_=wsa_pm), "p0", 16)
                    for d8 in range(8):
                        p.inc(g.dma_start(out=wsc_dn[:, d8], in_=wdn_pm[:, d8]),
                                  "p0", 16)
                    p.inc(g.dma_start(out=wsc_sd[:, :], in_=wsd_pm), "p0", 16)
                    p.inc(g.dma_start(out=bdn_sc, in_=bdn_r), "p0", 16)
                    p.inc(g.dma_start(out=bsd_sc, in_=bsd_r), "p0", 16)
                B0_TOTAL = p.n["b0"]
                assert B0_TOTAL == 176, B0_TOTAL
                P0_TOTAL = p.n["p0"]

                @block.sync
                def _(s):
                    for tt in range(TTB):
                        if tt >= 2:
                            p.wait(s, "apl", tt - 1)
                        p.inc(s.dma_start(
                            out=a1x[:, tt % 2, :],
                            in_=x_full[tt * 128:(tt + 1) * 128, :]),
                            f"ld{tt % 2}", 16)

                def a1_apply(v, m):
                    b = m % 2
                    if m >= 2:
                        hh = (m - 2) // 16
                        p.wait(v, f"wxh{hh}", 16 * min(m - 16 * hh, 16))
                    p.wait(v, "rs1", 2 * (m + 1))
                    p.inc(v.tensor_scalar(a1o[:, b, :], a1x[:, b, :],
                                          a1ag[:, b, 0:1], a1rs[:, b, 1:2],
                                          ALU.subtract, ALU.mult), "apl")

                @block.vector
                def _(v):
                    for tt in range(TTB):
                        b = tt % 2
                        p.wait(v, f"ld{tt % 2}", 16 * (tt // 2 + 1))
                        for i in range(4):
                            ins = v.bn_stats(a1st[:, b, i, :],
                                             a1x[:, b, 512 * i:512 * (i + 1)])
                        p.inc(ins, "lnv")
                        p.wait(v, "lnv", p.n["lnv"])
                        if tt >= 2:
                            p.wait(v, "rs1", 2 * (tt - 1))
                        p.inc(v.bn_aggr(a1ag[:, b, :], a1st[:, b, :, :]),
                              "lnv")
                        if tt >= 1:
                            a1_apply(v, tt - 1)
                    a1_apply(v, TTB - 1)

                @block.scalar
                def _(sc):
                    p.wait(sc, "b0v", 2)
                    for tt in range(TTB):
                        b = tt % 2
                        p.wait(sc, "lnv", 2 * (tt + 1))
                        p.inc(sc.activation(a1rs[:, b, 0:1], a1ag[:, b, 1:2],
                                            AF.Ln, bias=eps_sb[:, 0:1]), "rs1")
                        p.wait(sc, "rs1", p.n["rs1"])
                        p.inc(sc.activation(a1rs[:, b, 1:2], a1rs[:, b, 0:1],
                                            AF.Exp, scale=-0.5), "rs1")
                        p.wait(sc, "apl", tt + 1)
                        p.inc(sc.dma_start(
                            out=xln_tm[tt * 128:(tt + 1) * 128, :],
                            in_=a1o[:, tt % 2, :]), f"wxh{tt // 16}", 16)

                @block.sync
                def _(s):
                    if PH < 2:
                        return
                    for th in range(2):
                        p.wait(s, f"wxh{th}", 16 * 16)
                        for dt in range(DT):
                            p.inc(s.dma_start(
                                out=xlnT[:, dt, th * 2048:(th + 1) * 2048],
                                in_=xln_tm[th * 2048:(th + 1) * 2048,
                                           dt * 128:(dt + 1) * 128],
                                transpose=True), f"txl{th}", 16)

                @block.gpsimd
                def _(g):
                    if PH < 2:
                        return
                    p.wait(g, "mm2", 64)
                    p.inc(g.dma_start(out=a2w[:, :, :], in_=wv_pm), "wqkv", 16)
                    p.wait(g, "mm2", 128)
                    p.inc(g.dma_start(out=a2w[:, :, :], in_=wq_pm), "wqkv", 16)

                @block.tensor
                def _(t):
                    if PH < 2:
                        return
                    mm = 0
                    p.wait(t, "wqkv", 16)
                    p.wait(t, "b0v", 2)
                    for ot in range(NHC):
                        for tc in range(8):
                            p.wait(t, f"txl{tc // 4}", 16 * 16)
                            if mm >= 4:
                                p.wait(t, "ev2", mm - 3)
                            for dt in range(DT):
                                ins = t.matmul(
                                    pM[mm % 4][:, :],
                                    a2w[:, dt, ot * 128:(ot + 1) * 128],
                                    xlnT[:, dt, tc * 512:(tc + 1) * 512],
                                    start=(dt == 0), stop=(dt == DT - 1))
                            p.inc(ins, "mm2")
                            mm += 1
                    for tt in range(KT):
                        for oc in range(2):
                            if tt == 0 and oc == 0:
                                p.wait(t, "wqkv", 32)
                                p.wait(t, "txl0", 16 * 16)
                            if tt == 16 and oc == 0:
                                p.wait(t, "txl1", 16 * 16)
                            if mm >= 4:
                                p.wait(t, "ev2", mm - 3)
                            for dt in range(DT):
                                ins = t.matmul(
                                    pM[mm % 4][:, :],
                                    xlnT[:, dt, tt * 128:(tt + 1) * 128],
                                    a2w[:, dt, oc * 512:(oc + 1) * 512],
                                    start=(dt == 0), stop=(dt == DT - 1))
                            p.inc(ins, "mm2")
                            mm += 1
                    p.wait(t, "wqkv", 48)
                    for ot in range(NHC):
                        for tc in range(8):
                            if mm >= 4:
                                p.wait(t, "ev2", mm - 3)
                            for dt in range(DT):
                                ins = t.matmul(
                                    pM[mm % 4][:, :],
                                    a2w[:, dt, ot * 128:(ot + 1) * 128],
                                    xlnT[:, dt, tc * 512:(tc + 1) * 512],
                                    start=(dt == 0), stop=(dt == DT - 1))
                            p.inc(ins, "mm2")
                            mm += 1

                @block.scalar
                def _(sc):
                    if PH < 2:
                        return
                    for i in range(192):
                        p.wait(sc, "mm2", i + 1)
                        if i >= 4:
                            p.wait(sc, f"spl{i % 4}", 16 * (i // 4))
                        p.inc(sc.copy(a2st[:, i % 4, :], pM[i % 4][:, :]),
                              "ev2")
                        p.wait(sc, "ev2", p.n["ev2"])
                        if i < 64:
                            ot, tc = divmod(i, 8)
                            ins = sc.dma_start(
                                out=kT_sc[ot, :, tc * 512:(tc + 1) * 512],
                                in_=a2st[:, i % 4, :])
                        elif i < 128:
                            tt, oc = divmod(i - 64, 2)
                            ins = sc.dma_start(
                                out=v_sc[oc * 4:(oc + 1) * 4, :, tt, :]
                                .rearrange("hh pp j -> pp hh j"),
                                in_=a2st[:, i % 4, :].rearrange(
                                    "pp (hh j) -> pp hh j", hh=4))
                        else:
                            ot, tc = divmod(i - 128, 8)
                            ins = sc.dma_start(
                                out=qT_sc[ot, :, tc * 512:(tc + 1) * 512],
                                in_=a2st[:, i % 4, :])
                        p.inc(ins, f"spl{i % 4}", 16)
            A2_MM_TOTAL = p.n["mm2"]

            with nc.sbuf_tensor("kbuf", [128, 2, TB], BF16) as kbuf, \
                 nc.sbuf_tensor("vbuf", [128, 2, KT, 128], BF16) as vbuf, \
                 nc.sbuf_tensor("qbuf", [128, 2, 1024], BF16) as qbuf, \
                 nc.sbuf_tensor("abuf", [128, 2, 1024], BF16) as abuf, \
                 nc.sbuf_tensor("rbuf", [128, 512], F32) as rbuf, \
                 nc.sbuf_tensor("attn", [128, NHC, TB], BF16) as attn:

                @block.sync
                def _(s):
                    if PH < 3:
                        return
                    for sl in range(4):
                        p.wait(s, f"spl{sl}", 16 * 48)
                    for q in range(4):
                        for h in range(NHC):
                            gh = q * NHC + h
                            if gh >= 2:
                                p.wait(s, "acons", ACONS_END[gh - 2])
                            p.inc(s.dma_start(out=kbuf[:, gh % 2, :],
                                              in_=kT_sc[h]), f"hld{gh % 2}", 16)
                            p.inc(s.dma_start(out=vbuf[:, gh % 2, :, :],
                                              in_=v_sc[h]), f"hld{gh % 2}", 16)
                            p.inc(s.dma_start(
                                out=qbuf[:, gh % 2, :],
                                in_=qT_sc[h, :, q * 1024:(q + 1) * 1024]),
                                f"hld{gh % 2}", 16)

                @block.tensor
                def _(t):
                    if PH < 3:
                        return
                    p.wait(t, "ev2", 192)
                    for ck in a3:
                        gh = ck["q"] * NHC + ck["h"]
                        b = gh % 2
                        qsl = slice(ck["ci"] * 512, (ck["ci"] + 1) * 512)
                        p.wait(t, f"hld{gh % 2}", 48 * (gh // 2 + 1))
                        npair, s = ck["npair"], ck["s"]

                        def consume(pp):
                            first, last = pp == 0, pp == npair - 1
                            p.wait(t, "aexp", ck["Eb"] + pp + 1)
                            if pp >= 2 * s:
                                p.wait(t, "amask",
                                       ck["Mb"] + (pp - 2 * s) + 1)
                            if first:
                                p.wait(t, "aev", ck["Vb"])
                            for u in range(2):
                                kt = 2 * pp + u
                                asl = slice(u * 512, (u + 1) * 512)
                                t.matmul(pM[0][:, :], ones_sb[:, :],
                                         abuf[:, pp % 2, asl],
                                         start=(first and u == 0),
                                         stop=(last and u == 1))
                                ins = t.matmul(pM[1][:, :],
                                               vbuf[:, b, kt, :],
                                               abuf[:, pp % 2, asl],
                                               start=(first and u == 0),
                                               stop=(last and u == 1))
                            p.inc(ins, "acons")

                        for pp in range(npair):
                            if pp >= 2:
                                p.wait(t, "aexp", ck["Eb"] + pp - 1)
                            for u in range(2):
                                kt = 2 * pp + u
                                pS = pBig0 if pp % 2 == 0 else pBig1
                                ins = t.matmul(
                                    pS[:, u * 512:(u + 1) * 512],
                                    kbuf[:, b, kt * 128:(kt + 1) * 128],
                                    qbuf[:, b, qsl], start=True, stop=True)
                            p.inc(ins, "aS")
                            if pp >= 1:
                                consume(pp - 1)
                        consume(npair - 1)

                @block.scalar
                def _(sc):
                    if PH < 3:
                        return
                    for ck in a3:
                        for pp in range(ck["npair"]):
                            pS = pBig0 if pp % 2 == 0 else pBig1
                            p.wait(sc, "aS", ck["Sb"] + pp + 1)
                            p.wait(sc, "acons", ck["Cb"] + pp - 1)
                            p.inc(sc.activation(abuf[:, pp % 2, :], pS[:, :],
                                                AF.Exp, scale=SCALE), "aexp")

                @block.vector
                def _(v):
                    if PH < 3:
                        return
                    for ck in a3:
                        s, npair = ck["s"], ck["npair"]
                        for pp in (2 * s, 2 * s + 1):
                            p.wait(v, "aexp", ck["Eb"] + pp + 1)
                            for u in range(2):
                                i = 2 * (pp - 2 * s) + u
                                off = 384 - 128 * i
                                asl = slice(u * 512, (u + 1) * 512)
                                ins = v.tensor_mul(abuf[:, pp % 2, asl],
                                                   abuf[:, pp % 2, asl],
                                                   band_sb[:, off:off + 512])
                            p.inc(ins, "amask")
                        p.wait(v, "acons", ck["Cb"] + npair)
                        p.inc(v.reciprocal(rbuf[:, :], pM[0][:, :]), "aev")
                        p.wait(v, "aev", p.n["aev"])
                        p.inc(v.tensor_mul(
                            attn[:, ck["h"], s * 512:(s + 1) * 512],
                            pM[1][:, :], rbuf[:, :]), "aev")

                @block.scalar
                def _(sc):
                    if PH < 3:
                        return
                    for q in range(4):
                        p.wait(sc, "aev", 32 * (q + 1))
                        for ch in range(2):
                            p.inc(sc.dma_start(
                                out=attn_mine[q, 512 * ch:512 * (ch + 1), :]
                                .rearrange("(ct pp) j -> pp ct j", pp=128),
                                in_=attn[:, 4 * ch:4 * (ch + 1),
                                         q * 1024:(q + 1) * 1024]),
                                f"am{q}", 16)

                @block.gpsimd
                def _(g):
                    if PH < 3:
                        return
                    for q in range(4):
                        p.wait(g, f"am{q}", 32)
                        p.inc(g.collective_compute(
                            "AllGather", ALU.bypass,
                            ins=[attn_mine[q]], outs=[attn_full[q]],
                            replica_groups=RG_PAIRS), "cg", 1)
            A3_CONS = p.n["acons"]

            with nc.sbuf_tensor("gqb", [128, 2, DT, 1024], BF16) as gqb, \
                 nc.sbuf_tensor("wob", [128, 4, DT, 512], BF16) as wob, \
                 nc.sbuf_tensor("a4x", [128, 2, D], F32) as a4x, \
                 nc.sbuf_tensor("x1b", [128, 2, D], F32) as x1b, \
                 nc.sbuf_tensor("a4st", [128, 2, 4, 6], F32) as a4st, \
                 nc.sbuf_tensor("a4ag", [128, 2, 2], F32) as a4ag, \
                 nc.sbuf_tensor("a4rs", [128, 2, 2], F32) as a4rs, \
                 nc.sbuf_tensor("resb", [128, 2, D], F32) as resb, \
                 nc.sbuf_tensor("xt0b", [128, 2, D], BF16) as xt0b, \
                 nc.sbuf_tensor("bseb", [128, 2, D], F32) as bseb:

                @block.sync
                def _(s):
                    if PH < 4:
                        return
                    pid = s.partition_id()
                    with s.register("rq") as rq, s.register("rqi0") as rqi0, \
                            s.register("rqi1") as rqi1:
                        s.reg_alu(rq, pid, 1, ALU.bitwise_and)
                        s.reg_mul(rq, rq, 2)
                        s.reg_alu(rqi0, rq, 0, ALU.add)
                        s.reg_alu(rqi1, rq, 1, ALU.add)
                        p.wait(s, "cg", 4)
                        p.wait(s, "acons", A3_CONS)
                        for qi, rr in ((0, rqi0), (1, rqi1)):
                            for ch in range(2):
                                p.inc(s.dma_start(
                                    out=gqb[:, qi, 8 * ch:8 * (ch + 1), :],
                                    in_=attn_full[ds(RuntimeValue(rr, 0, 3), 1),
                                                  1024 * ch:1024 * (ch + 1),
                                                  :].rearrange(
                                        "one (ct pp) j -> pp (one ct) j",
                                        pp=128)), "gq", 16)
                    for lt in range(TTC):
                        if lt >= 2:
                            p.wait(s, "ap4", 3 * (lt - 1))
                        p.inc(s.dma_start(
                            out=a4x[:, lt % 2, :],
                            in_=x_mine[lt * 128:(lt + 1) * 128, :]),
                            f"ld{lt % 2}", 16)

                @block.gpsimd
                def _(g):
                    if PH < 4:
                        return
                    p.wait(g, "acons", A3_CONS)
                    for qq in range(4):
                        p.wait(g, f"am{qq}", 32)
                    p.inc(g.dma_start(out=wob[:, :], in_=wo_pm), "wo4", 16)

                @block.tensor
                def _(t):
                    if PH < 4:
                        return
                    p.wait(t, "wo4", 16)
                    for lt in range(TTC):
                        qi, ltq = divmod(lt, 8)
                        p.wait(t, "gq", 64)
                        for dc in range(4):
                            i = lt * 4 + dc
                            if i >= 2:
                                p.wait(t, "ev4", i - 1)
                            for ct in range(DT):
                                ins = t.matmul(
                                    pM[i % 2][:, :],
                                    gqb[:, qi, ct,
                                        ltq * 128:(ltq + 1) * 128],
                                    wob[:, dc, ct, :],
                                    start=(ct == 0), stop=(ct == DT - 1))
                            p.inc(ins, "mm4")

                def a4_apply(v, m):
                    b = m % 2
                    if m >= 2:
                        p.wait(v, f"wrs{m % 2}", 48 * (m // 2))
                    p.wait(v, "rs4", 2 * (m + 1))
                    p.inc(v.tensor_scalar(resb[:, b, :], x1b[:, b, :],
                                          a4ag[:, b, 0:1], a4rs[:, b, 1:2],
                                          ALU.subtract, ALU.mult), "ap4")
                    p.wait(v, "ap4", p.n["ap4"])
                    p.inc(v.tensor_copy(xt0b[:, b, :], resb[:, b, :]), "ap4")
                    p.inc(v.tensor_add(bseb[:, b, :], x1b[:, b, :],
                                       resb[:, b, :]), "ap4")

                @block.vector
                def _(v):
                    if PH < 4:
                        return
                    for lt in range(TTC):
                        b = lt % 2
                        p.wait(v, f"ld{lt % 2}", 16 * (16 + lt // 2 + 1))
                        for dc in range(4):
                            i = lt * 4 + dc
                            p.wait(v, "mm4", i + 1)
                            p.inc(v.scalar_tensor_tensor(
                                x1b[:, b, dc * 512:(dc + 1) * 512],
                                pM[i % 2][:, :], 1.0,
                                a4x[:, b, dc * 512:(dc + 1) * 512],
                                ALU.mult, ALU.add), "ev4")
                        p.wait(v, "ev4", p.n["ev4"])
                        for i in range(4):
                            ins = v.bn_stats(a4st[:, b, i, :],
                                             x1b[:, b, 512 * i:512 * (i + 1)])
                        p.inc(ins, "ln4")
                        p.wait(v, "ln4", p.n["ln4"])
                        if lt >= 2:
                            p.wait(v, "rs4", 2 * (lt - 1))
                        p.inc(v.bn_aggr(a4ag[:, b, :], a4st[:, b, :, :]),
                              "ln4")
                        if lt >= 1:
                            a4_apply(v, lt - 1)
                    a4_apply(v, TTC - 1)

                @block.scalar
                def _(sc):
                    if PH < 4:
                        return
                    for lt in range(TTC):
                        b = lt % 2
                        p.wait(sc, "ln4", 2 * (lt + 1))
                        p.inc(sc.activation(a4rs[:, b, 0:1], a4ag[:, b, 1:2],
                                            AF.Ln, bias=eps_sb[:, 0:1]), "rs4")
                        p.wait(sc, "rs4", p.n["rs4"])
                        p.inc(sc.activation(a4rs[:, b, 1:2], a4rs[:, b, 0:1],
                                            AF.Exp, scale=-0.5), "rs4")
                        rows = slice(lt * 128, (lt + 1) * 128)
                        p.wait(sc, "ap4", 3 * (lt + 1))
                        p.inc(sc.dma_start(out=res_tm[rows, :],
                                           in_=resb[:, b, :]), f"wrs{b}", 16)
                        p.inc(sc.dma_start(out=xt0_tm[rows, :],
                                           in_=xt0b[:, b, :]), f"wrs{b}", 16)
                        p.inc(sc.dma_start(out=base_tm[rows, :],
                                           in_=bseb[:, b, :]), f"wrs{b}", 16)
            A4_MM_TOTAL = p.n["mm4"]

            with nc.sbuf_tensor("xtT", [128, 2, DT, G], BF16) as xtT, \
                 nc.sbuf_tensor("hbuf", [128, 2, ET, G], BF16) as hbuf, \
                 nc.sbuf_tensor("h2w", [128, ET, 8, G], BF16) as h2w, \
                 nc.sbuf_tensor("warA", [128, 2, DT, 512], BF16) as warA, \
                 nc.sbuf_tensor("warB", [128, 2, ET, 512], BF16) as warB, \
                 nc.sbuf_tensor("wsdb", [128, 2, ST, 256], BF16) as wsdb, \
                 nc.sbuf_tensor("wrep", [128, NX, 512], BF16) as wrep, \
                 nc.sbuf_tensor("resg", [128, 2, 4, 256], F32) as resg, \
                 nc.sbuf_tensor("stg", [128, 4, 256], F32) as stg, \
                 nc.sbuf_tensor("stgb", [128, 4, 256], BF16) as stgb, \
                 nc.sbuf_tensor("gw", [128, 4, 8], F32) as gw, \
                 nc.sbuf_tensor("gmx", [128, 4, 8], F32) as gmx, \
                 nc.sbuf_tensor("gcmp", [128, 4, 8], F32) as gcmp, \
                 nc.sbuf_tensor("gnm", [128, 4, 2], F32) as gnm, \
                 nc.sbuf_tensor("wT8", [8, G], BF16) as wT8, \
                 nc.sbuf_tensor("bdnb", [NX, 2, 256], BF16) as bdnb, \
                 nc.sbuf_tensor("bsdb", [1, 2, 256], BF16) as bsdb:

                def emit_h(t, gi, n):
                    for et in range(ET):
                        u = n * ET + et
                        i1 = gi * 16 + 2 * n + et // 4
                        p.wait(t, f"wlA{i1 % 2}", 16 * (i1 // 2 + 1))
                        p.wait(t, "fhev", 64 * gi + u - 3)
                        if n == 0 and et == 0:
                            p.wait(t, "gtv", 4 * (gi + 1))
                        if n == 0 and et == 2:
                            p.wait(t, "gts", gi + 1)
                        for dt in range(DT):
                            ins = t.matmul(
                                PB4[u % 4],
                                warA[:, i1 % 2, dt,
                                     (et % 4) * 128:(et % 4) * 128 + 128],
                                xtT[:, gi % 2, dt, :],
                                start=(dt == 0), stop=(dt == DT - 1))
                        p.inc(ins, "fh")

                def emit_s(t, gi):
                    for ot in range(ST):
                        u = 48 + ot
                        i1 = gi * 16 + 12 + ot // 4
                        p.wait(t, f"wlA{i1 % 2}", 16 * (i1 // 2 + 1))
                        p.wait(t, "fhev", 64 * gi + u - 3)
                        for dt in range(DT):
                            ins = t.matmul(
                                PB4[u % 4],
                                warA[:, i1 % 2, dt,
                                     (ot % 4) * 128:(ot % 4) * 128 + 128],
                                xtT[:, gi % 2, dt, :],
                                start=(dt == 0), stop=(dt == DT - 1))
                        p.inc(ins, "fh")

                def emit_z(t, gi, n):
                    for ft in range(ET):
                        u = n * ET + ft
                        i2 = gi * 14 + 2 * n + ft // 4
                        p.wait(t, f"wlB{i2 % 2}", 16 * (i2 // 2 + 1))
                        p.wait(t, "fhev", 64 * gi + ET * (n + 1))
                        p.wait(t, "fzev", 64 * gi + u - 3)
                        if u < 4:
                            p.wait(t, "wrp", 6 * (gi + 1))
                        for et in range(ET):
                            ins = t.matmul(
                                pM[u % 4][:, :],
                                warB[:, i2 % 2, et,
                                     (ft % 4) * 128:(ft % 4) * 128 + 128],
                                hbuf[:, n % 2, et, :],
                                start=(et == 0), stop=(et == ET - 1))
                        p.inc(ins, "fz")

                def emit_zs(t, gi):
                    for k in range(2):
                        for ft in range(ET):
                            u = 48 + k * ET + ft
                            i2 = gi * 14 + 12 + ft // 4
                            p.wait(t, f"wlB{i2 % 2}", 16 * (i2 // 2 + 1))
                            p.wait(t, "fhev", 64 * gi + 48 + ET * (k + 1))
                            p.wait(t, "fzev", 64 * gi + u - 3)
                            for et in range(ET):
                                ins = t.matmul(
                                    pM[u % 4][:, :],
                                    warB[:, i2 % 2, et,
                                         (ft % 4) * 128:(ft % 4) * 128 + 128],
                                    hbuf[:, k, et, :],
                                    start=(et == 0), stop=(et == ET - 1))
                            p.inc(ins, "fz")

                for j in range(2 if PH >= 5 else 0):
                    for grp in range(NG):
                        gi = j * NG + grp
                        if gi >= FFN_GROUPS:
                            continue
                        src_tm = xt0_tm if j == 0 else xt1_tm
                        rows = slice(grp * G, (grp + 1) * G)

                        @block.sync
                        def _(s, j=j, grp=grp, gi=gi, src_tm=src_tm,
                              rows=rows):
                            if j == 1:
                                for sl in range(4):
                                    p.wait(s, f"fq{sl}", 16 * 8 * (grp + 1))
                            if gi == 0:
                                p.wait(s, "mm4", A4_MM_TOTAL)
                                p.wait(s, "wrs0", 48 * 8)
                                p.wait(s, "wrs1", 48 * 8)
                                p.wait(s, "ap4", 3 * TTC)
                                p.wait(s, "b0v", 2)
                                p.wait(s, "p0", P0_TOTAL)
                            if gi >= 2:
                                p.wait(s, "fhw", 112 * (gi - 1))
                            for dt in range(DT):
                                p.inc(s.dma_start(
                                    out=xtT[:, gi % 2, dt, :],
                                    in_=src_tm[rows,
                                               dt * 128:(dt + 1) * 128],
                                    transpose=True), f"xtb{gi % 2}", 16)
                            lorder = []
                            for n in range(NX):
                                lorder += [("wl1", 2 * n), ("wl1", 2 * n + 1)]
                                if n >= 1:
                                    lorder += [("wl2", 2 * n - 2),
                                               ("wl2", 2 * n - 1)]
                            lorder += [("wl2", 10), ("wl2", 11),
                                       ("wl1", 12), ("wl1", 13),
                                       ("wl1", 14), ("wl1", 15),
                                       ("wl2", 12), ("wl2", 13)]
                            p.wait(s, "p2", 2112 * gi)
                            for kind, li in lorder:
                                if kind == "wl1":
                                    i1 = gi * 16 + li
                                    if i1 >= 2:
                                        pg, pl = divmod(i1 - 2, 16)
                                        p.wait(s, "fh", 64 * pg + 4 * pl + 4)
                                    if li < 12:
                                        n, eh = divmod(li, 2)
                                        src = wsc_up[:, n, eh]
                                    else:
                                        src = wsc_su[:, li - 12]
                                    p.inc(s.dma_start(out=warA[:, i1 % 2],
                                                      in_=src),
                                          f"wlA{i1 % 2}", 16)
                                else:
                                    i2 = gi * 14 + li
                                    if i2 >= 2:
                                        pg, pl = divmod(i2 - 2, 14)
                                        after = (4 * pl + 4 if pl < 12
                                                 else (60 if pl == 12 else 64))
                                        p.wait(s, "fz", 64 * pg + after)
                                    if li < 12:
                                        n, fh = divmod(li, 2)
                                        src = wsc_sw[:, n, fh]
                                    else:
                                        src = wsc_sa[:, li - 12]
                                    p.inc(s.dma_start(out=warB[:, i2 % 2],
                                                      in_=src),
                                          f"wlB{i2 % 2}", 16)
                            for d8 in range(8):
                                i3 = gi * 8 + d8
                                par = d8 % 2
                                if d8 == 0:
                                    p.wait(s, "fh", 64 * (gi + 1))
                                    p.wait(s, "fz", 64 * (gi + 1))
                                if i3 >= 2:
                                    p.wait(s, "p2", 264 * (i3 - 1))
                                p.inc(s.dma_start(
                                    out=warA[:, par].rearrange(
                                        "pp a c -> pp (a c)"),
                                    in_=wsc_dn[:, d8, 0:4].rearrange(
                                        "pp a b c -> pp (a b c)")),
                                    f"wlC{d8 % 2}", 16)
                                p.inc(s.dma_start(
                                    out=warB[:, par].rearrange(
                                        "pp a c -> pp (a c)"),
                                    in_=wsc_dn[:, d8, 4:6].rearrange(
                                        "pp a b c -> pp (a b c)")),
                                    f"wlC{d8 % 2}", 16)
                                p.inc(s.dma_start(out=wsdb[:, par],
                                                  in_=wsc_sd[:, d8]),
                                      f"wlC{d8 % 2}", 16)
                                p.inc(s.dma_start(
                                    out=bdnb[:, par, :],
                                    in_=bdn_sc[:, d8 * 256:(d8 + 1) * 256]),
                                    f"wlC{d8 % 2}", 16)
                                p.inc(s.dma_start(
                                    out=bsdb[:, par, :],
                                    in_=bsd_sc[:, d8 * 256:(d8 + 1) * 256]),
                                    f"wlC{d8 % 2}", 16)
                                rsrc = res_tm if j == 0 else base_tm
                                if i3 >= 2:
                                    p.wait(s, "p2ev", 4 * (i3 - 1))
                                p.inc(s.dma_start(
                                    out=resg[:, par],
                                    in_=rsrc[rows, d8 * 256:(d8 + 1) * 256]
                                    .rearrange("(tt pp) c -> pp tt c",
                                               pp=128)), f"rsg{d8 % 2}", 16)

                        @block.tensor
                        def _(t, j=j, grp=grp, gi=gi):
                            b = gi % 2
                            p.wait(t, f"xtb{gi % 2}", 16 * 16 * (gi // 2 + 1))
                            p.wait(t, "p2ev", 32 * gi)
                            if gi == 0:
                                p.wait(t, "ev4", 64)
                            for tt in range(4):
                                p.wait(t, "gtv", 4 * gi + tt)
                                if gi == 0 and tt == 0:
                                    p.wait(t, "b0v", 2)
                                for dt in range(DT):
                                    ins = t.matmul(
                                        pBig0[:, 0:8],
                                        xtT[:, b, dt,
                                            tt * 128:(tt + 1) * 128],
                                        wg_sb[:, dt, j, :],
                                        start=(dt == 0),
                                        stop=(dt == DT - 1))
                                p.inc(ins, "gt")
                            for tt in range(4):
                                p.wait(t, "gtw", 4 * gi + tt + 1)
                                p.inc(t.transpose(
                                    pBig1[0:8, tt * 128:(tt + 1) * 128],
                                    gw[:, tt, :], ident_sb[:, :]), "gt")
                            p.wait(t, "gts", gi + 1)
                            for n in range(NX):
                                if 6 * gi + n >= 4:
                                    p.wait(t, "wrp", 6 * gi + n - 3)
                                p.inc(t.matmul(
                                    pM[n % 4][:, :],
                                    sel_sb[:, n * 128:(n + 1) * 128],
                                    wT8[0:8, :], start=True, stop=True),
                                    "gt")
                            emit_h(t, gi, 0)
                            for n in range(1, NX):
                                emit_h(t, gi, n)
                                emit_z(t, gi, n - 1)
                            emit_z(t, gi, NX - 1)
                            emit_s(t, gi)
                            emit_zs(t, gi)
                            p.wait(t, "fhw", 112 * (gi + 1))
                            for d8 in range(8):
                                i3 = gi * 8 + d8
                                par = d8 % 2
                                p.wait(t, f"wlC{d8 % 2}", 80 * (gi * 4 + d8 // 2 + 1))
                                p.wait(t, "p2ev", 4 * (i3 - 1))
                                wAf = warA[:, par].rearrange(
                                    "pp a c -> pp (a c)")
                                wBf = warB[:, par].rearrange(
                                    "pp a c -> pp (a c)")
                                first = True
                                for n in range(NX):
                                    for et in range(ET):
                                        idx = n * ET + et
                                        if idx < 32:
                                            rhs = wAf[:, idx * 256:
                                                      (idx + 1) * 256]
                                        else:
                                            rhs = wBf[:, (idx - 32) * 256:
                                                      (idx - 31) * 256]
                                        for tt in range(4):
                                            t.matmul(
                                                BANK8[2 * tt + par][:, 0:256],
                                                h2w[:, et, n,
                                                    tt * 128:(tt + 1) * 128],
                                                rhs, start=first, stop=False)
                                        first = False
                                for st in range(ST):
                                    k, ft = divmod(st, ET)
                                    for tt in range(4):
                                        t.matmul(
                                            BANK8[2 * tt + par][:, 0:256],
                                            h2w[:, ft, 6 + k,
                                                tt * 128:(tt + 1) * 128],
                                            wsdb[:, par, st, :],
                                            start=False, stop=False)
                                for tt in range(4):
                                    t.matmul(
                                        BANK8[2 * tt + par][:, 0:256],
                                        wT8[0:6, tt * 128:(tt + 1) * 128],
                                        bdnb[0:6, par, :],
                                        start=False, stop=False)
                                    ins = t.matmul(
                                        BANK8[2 * tt + par][:, 0:256],
                                        ones_sb[0:1, :],
                                        bsdb[0:1, par, :],
                                        start=False, stop=True)
                                p.inc(ins, "p2", 264)

                        @block.scalar
                        def _(sc, j=j, grp=grp, gi=gi):
                            for tt in range(4):
                                p.wait(sc, "gex", 4 * gi + tt + 1)
                                p.inc(sc.activation(
                                    gmx[:, tt, :], gw[:, tt, :], AF.Exp,
                                    bias=gnm[:, tt, 0:1]), "gey")

                            def hev(n):
                                for et in range(ET):
                                    u = n * ET + et
                                    p.wait(sc, "fh", 64 * gi + u + 1)
                                    if u == 0:
                                        p.wait(sc, "fhw", 112 * gi)
                                    if n >= 2 and et == 0:
                                        p.wait(sc, "fz",
                                               64 * gi + ET * (n - 1))
                                        p.wait(sc, "fhw",
                                               112 * gi + 16 * (n - 1))
                                    p.inc(sc.activation(
                                        hbuf[:, n % 2, et, :], PB4[u % 4],
                                        AF.Identity,
                                        bias=bup_sb[:, n, et:et + 1]),
                                        "fhev")

                            def sev(half):
                                for ot in range(8 * half, 8 * half + 8):
                                    u = 48 + ot
                                    p.wait(sc, "fh", 64 * gi + u + 1)
                                    if ot == 0:
                                        p.wait(sc, "fz", 64 * gi + 40)
                                        p.wait(sc, "fhw", 112 * gi + 80)
                                    if ot == 8:
                                        p.wait(sc, "fz", 64 * gi + 48)
                                        p.wait(sc, "fhw", 112 * gi + 96)
                                    p.inc(sc.activation(
                                        hbuf[:, ot // 8, ot % 8, :],
                                        PB4[u % 4], AF.Identity,
                                        bias=bsu_sb[:, ot:ot + 1]), "fhev")

                            def zev(n):
                                for ft in range(ET):
                                    u = n * ET + ft
                                    p.wait(sc, "fz", 64 * gi + u + 1)
                                    if u == 0:
                                        p.wait(sc, "p2", 2112 * gi)
                                    p.inc(sc.activation(
                                        h2w[:, ft, n, :], pM[u % 4][:, :],
                                        AF.Silu,
                                        bias=bsw_sb[:, n, ft:ft + 1]),
                                        "fzev")

                            def zsev():
                                for k in range(2):
                                    for ft in range(ET):
                                        u = 48 + k * ET + ft
                                        p.wait(sc, "fz", 64 * gi + u + 1)
                                        p.inc(sc.activation(
                                            h2w[:, ft, 6 + k, :],
                                            pM[u % 4][:, :], AF.Silu,
                                            bias=bsa_sb[:, ft:ft + 1]),
                                            "fzev")

                            hev(0)
                            for n in range(1, NX):
                                hev(n)
                                zev(n - 1)
                            zev(NX - 1)
                            sev(0)
                            sev(1)
                            zsev()
                            for d8 in range(8):
                                i3 = gi * 8 + d8
                                for tt in range(4):
                                    p.wait(sc, "p2ev", 4 * i3 + tt + 1)
                                    rr = slice(grp * G + tt * 128,
                                               grp * G + (tt + 1) * 128)
                                    cc = slice(d8 * 256, (d8 + 1) * 256)
                                    if j == 0:
                                        ins = sc.dma_start(
                                            out=xt1_tm[rr, cc],
                                            in_=stgb[:, tt, :])
                                        p.inc(ins, f"fq{tt}", 16)
                                    else:
                                        ins = sc.dma_start(
                                            out=y_mine[rr, cc],
                                            in_=stg[:, tt, :])
                                        p.inc(ins, f"oq{tt}", 16)

                        @block.vector
                        def _(v, j=j, grp=grp, gi=gi):
                            for tt in range(4):
                                p.wait(v, "gt", 14 * gi + tt + 1)
                                p.inc(v.tensor_add(gw[:, tt, :],
                                                   pBig0[:, 0:8],
                                                   bgr_sb[:, j, :]), "gtv")
                                p.wait(v, "gtv", p.n["gtv"])
                                p.inc(v.max(gmx[:, tt, :], gw[:, tt, :]),
                                      "gch")
                                p.wait(v, "gch", p.n["gch"])
                                p.inc(v.tensor_scalar(
                                    gcmp[:, tt, :], gw[:, tt, :],
                                    gmx[:, tt, 5:6], None, ALU.is_ge),
                                    "gch")
                                p.inc(v.tensor_scalar_mul(
                                    gnm[:, tt, 0:1], gmx[:, tt, 0:1], -1.0),
                                    "gex")
                            for tt in range(4):
                                p.wait(v, "gey", 4 * gi + tt + 1)
                                p.wait(v, "gch", p.n["gch"])
                                p.inc(v.tensor_mul(gw[:, tt, :],
                                                   gmx[:, tt, :],
                                                   gcmp[:, tt, :]), "gch")
                                p.wait(v, "gch", p.n["gch"])
                                p.inc(v.reduce_sum(gnm[:, tt, 1:2],
                                                   gw[:, tt, :],
                                                   axis=AX.X), "gch")
                                p.wait(v, "gch", p.n["gch"])
                                p.inc(v.reciprocal(gnm[:, tt, 0:1],
                                                   gnm[:, tt, 1:2]), "gch")
                                p.wait(v, "gch", p.n["gch"])
                                p.inc(v.tensor_scalar_mul(
                                    gw[:, tt, :], gw[:, tt, :],
                                    gnm[:, tt, 0:1]), "gtw")
                            p.wait(v, "gt", 14 * gi + 8)
                            p.inc(v.tensor_copy(wT8[0:8, :],
                                                pBig1[0:8, 0:512]), "gts")
                            for n in range(NX):
                                p.wait(v, "gt", 14 * gi + 8 + n + 1)
                                p.inc(v.tensor_copy(wrep[:, n, :],
                                                    pM[n % 4][:, :]), "wrp")
                            for u in range(64):
                                p.wait(v, "fzev", 64 * gi + u + 1)
                                if u < 48:
                                    n, ft = divmod(u, ET)
                                    p.wait(v, "wrp", 6 * gi + n + 1)
                                    p.inc(v.tensor_mul(
                                        h2w[:, ft, n, :], h2w[:, ft, n, :],
                                        hbuf[:, n % 2, ft, :]), "fhw")
                                    p.wait(v, "fhw", p.n["fhw"])
                                    p.inc(v.tensor_mul(
                                        h2w[:, ft, n, :], h2w[:, ft, n, :],
                                        wrep[:, n, :]), "fhw")
                                else:
                                    k, ft = divmod(u - 48, ET)
                                    p.inc(v.tensor_mul(
                                        h2w[:, ft, 6 + k, :],
                                        h2w[:, ft, 6 + k, :],
                                        hbuf[:, k, ft, :]), "fhw")
                            for d8 in range(8):
                                i3 = gi * 8 + d8
                                par = d8 % 2
                                p.wait(v, f"rsg{d8 % 2}", 16 * (gi * 4 + d8 // 2 + 1))
                                p.wait(v, "p2", 264 * (i3 + 1))
                                for tt in range(4):
                                    i3l = grp * 8 + d8
                                    if i3l >= 1:
                                        p.wait(v,
                                               (f"fq{tt}" if j == 0
                                                else f"oq{tt}"), 16 * i3l)
                                    dst = stgb if j == 0 else stg
                                    p.inc(v.scalar_tensor_tensor(
                                        dst[:, tt, :],
                                        BANK8[2 * tt + par][:, 0:256], 1.0,
                                        resg[:, par, tt, :],
                                        ALU.mult, ALU.add), "p2ev")

            @block.sync
            def _(s):
                if PH >= 5:
                    p.wait(s, "p2ev", p.n["p2ev"])
                    for sl in range(4):
                        p.wait(s, f"fq{sl}", p.n[f"fq{sl}"])
                        p.wait(s, f"oq{sl}", p.n[f"oq{sl}"])
                elif PH == 4:
                    p.wait(s, "wrs0", p.n["wrs0"])
                    p.wait(s, "wrs1", p.n["wrs1"])
                elif PH == 3:
                    p.wait(s, "cg", 4)
                elif PH == 2:
                    for sl in range(4):
                        p.wait(s, f"spl{sl}", p.n[f"spl{sl}"])
                else:
                    p.wait(s, "wxh0", p.n["wxh0"])
                    p.wait(s, "wxh1", p.n["wxh1"])

    return nc


def _prep(inputs, core):
    b, g = core // 2, core % 2
    A = np.ascontiguousarray
    x = np.asarray(inputs["L0_attn_x"])

    def pm_qkv(w):
        wt = np.asarray(w).T[:, g * 1024:(g + 1) * 1024]
        return A(wt.reshape(DT, 128, 1024).transpose(1, 0, 2))

    d = {
        "x_full": A(x[b]),
        "x_mine": A(x[b, g * TC:(g + 1) * TC]),
        "wq_pm": pm_qkv(inputs["L0_attn_wq"]),
        "wk_pm": pm_qkv(inputs["L0_attn_wk"]),
        "wv_pm": pm_qkv(inputs["L0_attn_wv"]),
        "wo_pm": A(np.asarray(inputs["L0_attn_wo"]).T
                   .reshape(DT, 128, 4, 512).transpose(1, 2, 0, 3)),
        "wg_pm": A(np.asarray(inputs["L0_ffn_Wg"]).transpose(1, 0, 2)
                   .reshape(DT, 128, 2, 8).transpose(1, 0, 2, 3)),
        "wup_pm": A(np.asarray(inputs["L0_ffn_Wup"])
                    .reshape(NX, DT, 128, 2, 512).transpose(2, 0, 3, 1, 4)),
        "wsw_pm": A(np.asarray(inputs["L0_ffn_Wsw"])
                    .reshape(NX, ET, 128, 2, 512).transpose(2, 0, 3, 1, 4)),
        "wsu_pm": A(np.asarray(inputs["L0_ffn_Wsu"])
                    .reshape(DT, 128, 4, 512).transpose(1, 2, 0, 3)),
        "wsa_pm": A(np.asarray(inputs["L0_ffn_Wsa"])
                    .reshape(ET, 128, 2, 512).transpose(1, 2, 0, 3)),
        "wdn_pm": A(np.asarray(inputs["L0_ffn_Wdn"])
                    .reshape(NX, ET, 128, 8, 256).transpose(2, 3, 0, 1, 4)),
        "wsd_pm": A(np.asarray(inputs["L0_ffn_Wsd"])
                    .reshape(ST, 128, 8, 256).transpose(1, 2, 0, 3)),
        "bupT": A(np.asarray(inputs["L0_ffn_bup"]).reshape(NX, ET, 128)
                  .transpose(2, 0, 1)),
        "bswT": A(np.asarray(inputs["L0_ffn_bsw"]).reshape(NX, ET, 128)
                  .transpose(2, 0, 1)),
        "bsuT": A(np.asarray(inputs["L0_ffn_bsu"]).reshape(ST, 128).T),
        "bsaT": A(np.asarray(inputs["L0_ffn_bsa"]).reshape(ET, 128).T),
        "bg0": A(np.asarray(inputs["L0_ffn_bg"])[0:1, :]),
        "bg1": A(np.asarray(inputs["L0_ffn_bg"])[1:2, :]),
        "bdn_r": A(np.asarray(inputs["L0_ffn_bdn"])),
        "bsd_r": A(np.asarray(inputs["L0_ffn_bsd"]).reshape(1, D)),
        "c_ident": np.eye(128, dtype=np.float32),
        "c_ones": np.ones((128, 128), dtype=ml_dtypes.bfloat16),
    }
    band = (np.arange(128)[:, None] <= (np.arange(896)[None, :] - 384))
    d["c_band"] = band.astype(ml_dtypes.bfloat16)
    sel = np.zeros((8, NX * 128), dtype=ml_dtypes.bfloat16)
    for n in range(NX):
        sel[n, n * 128:(n + 1) * 128] = 1
    d["c_sel"] = sel
    return d


def kernel(**inputs):
    nc = bass.Bass("TRN2", target_bir_lowering=False)
    build(nc)
    in_maps = [_prep(inputs, c) for c in range(8)]
    res = run_bass_kernel_spmd(nc, in_maps, core_ids=list(range(8)))
    out = np.zeros((4, TB, D), dtype=np.float32)
    for c in range(8):
        b, g = c // 2, c % 2
        out[b, g * TC:(g + 1) * TC] = np.asarray(res.results[c]["y_mine"])
    return out


# version 11
# speedup vs baseline: 1.0553x; 1.0133x over previous
import numpy as np
import ml_dtypes
from contextlib import ExitStack
import concourse.bass as bass
import concourse.mybir as mybir
from concourse.bass_utils import run_bass_kernel_spmd
from concourse.expressions import RuntimeValue

F32 = mybir.dt.float32
BF16 = mybir.dt.bfloat16
AF = mybir.ActivationFunctionType
ALU = mybir.AluOpType
AX = mybir.AxisListType
ds = bass.ds

D = 2048
DT = 16
TB = 4096
TTB = 32
TC = 2048
TTC = 16
NHC = 8
KT = 32
E = 1024
ET = 8
NX = 6
G = 512
NG = 4
ST = 16
SCALE = float(1.0 / np.sqrt(128.0))
EPS = 1e-5
RG_PAIRS = [[0, 1], [2, 3], [4, 5], [6, 7]]
PHASES = 99
FFN_GROUPS = 99

SEMS = [
    "b0", "b0v", "p0",
    "ld0", "ld1", "lnv", "rs1", "apl", "wxh0", "wxh1",
    "txl0", "txl1", "wqkv", "mm2", "ev2",
    "spl0", "spl1", "spl2", "spl3",
    "hld0", "hld1", "hcons", "aS", "aexp", "amask", "acons", "aev",
    "am0", "am1", "am2", "am3", "cg",
    "gq", "wo4", "mm4", "ev4", "ln4", "rs4", "ap4", "wrs0", "wrs1",
    "xtb0", "xtb1", "wlA0", "wlA1", "wlB0", "wlB1", "wlC0", "wlC1",
    "rsg0", "rsg1",
    "gt", "gtv", "gch", "gex", "gey", "gtw", "gts", "wrp",
    "fh", "fhev", "fz", "fzev", "fhw", "p2", "p2ev",
    "fq0", "fq1", "fq2", "fq3", "oq0", "oq1", "oq2", "oq3",
]


class P:
    def __init__(self, nc, stack, names):
        self.sem = {}
        self.n = {}
        for s in names:
            self.sem[s] = stack.enter_context(nc.semaphore(s))
            self.n[s] = 0

    def inc(self, instr, name, k=1):
        instr.then_inc(self.sem[name], k)
        self.n[name] += k
        return instr

    def wait(self, eng, name, val=None):
        v = self.n[name] if val is None else val
        if v > 0:
            eng.wait_ge(self.sem[name], v)


def build(nc):
    PH = PHASES
    def inp(name, shape, dt=F32):
        return nc.dram_tensor(name, list(shape), dt, kind="ExternalInput").ap()

    x_full = inp("x_full", (TB, D))
    x_mine = inp("x_mine", (TC, D))
    wq_pm = inp("wq_pm", (128, DT, NHC * 128))
    wk_pm = inp("wk_pm", (128, DT, NHC * 128))
    wv_pm = inp("wv_pm", (128, DT, NHC * 128))
    wo_pm = inp("wo_pm", (128, 4, DT, 512))
    wg_pm = inp("wg_pm", (128, DT, 2, 8))
    wup_pm = inp("wup_pm", (128, NX, 2, DT, 512))
    wsw_pm = inp("wsw_pm", (128, NX, 2, ET, 512))
    wsu_pm = inp("wsu_pm", (128, 4, DT, 512))
    wsa_pm = inp("wsa_pm", (128, 2, ET, 512))
    wdn_pm = inp("wdn_pm", (128, DT, NX, ET, 128))
    wsd_pm = inp("wsd_pm", (128, DT, ST, 128))
    bupT = inp("bupT", (128, NX, ET))
    bswT = inp("bswT", (128, NX, ET))
    bsuT = inp("bsuT", (128, ST))
    bsaT = inp("bsaT", (128, ET))
    bg0 = inp("bg0", (1, 8))
    bg1 = inp("bg1", (1, 8))
    bdn_r = inp("bdn_r", (NX, D))
    bsd_r = inp("bsd_r", (1, D))
    c_ident = inp("c_ident", (128, 128))
    c_identb = inp("c_identb", (128, 128), BF16)
    c_onesrow = inp("c_onesrow", (1, 512), BF16)
    c_ones = inp("c_ones", (128, 128), BF16)
    c_band = inp("c_band", (128, 896), BF16)
    c_sel = inp("c_sel", (8, NX * 128), BF16)

    y_mine = nc.dram_tensor("y_mine", [128, DT, TC], F32,
                            kind="ExternalOutput").ap()

    def scr(name, shape, dt):
        return nc.dram_tensor(name, list(shape), dt, kind="Internal").ap()

    xln_tm = scr("xln_tm", (TB, D), BF16)
    kT_sc = scr("kT_sc", (NHC, 128, TB), BF16)
    qT_sc = scr("qT_sc", (NHC, 128, TB), BF16)
    v_sc = scr("v_sc", (NHC, 128, KT, 128), BF16)
    attn_mine = scr("attn_mine", (4, NHC * 128, 1024), BF16)
    attn_full = scr("attn_full", (4, D, 1024), BF16)
    xt0_tm = scr("xt0_tm", (TC, D), BF16)
    rlo_tm = scr("rlo_tm", (TC, D), BF16)
    bhi_tm = scr("bhi_tm", (TC, D), BF16)
    blo_tm = scr("blo_tm", (TC, D), BF16)
    xt1_fm = scr("xt1_fm", (128, NG, DT, G), BF16)
    wsc_up = scr("wsc_up", (128, NX, 2, DT, 512), BF16)
    wsc_sw = scr("wsc_sw", (128, NX, 2, ET, 512), BF16)
    wsc_su = scr("wsc_su", (128, 4, DT, 512), BF16)
    wsc_sa = scr("wsc_sa", (128, 2, ET, 512), BF16)
    wsc_dn = scr("wsc_dn", (128, DT, NX, ET, 128), BF16)
    wsc_sd = scr("wsc_sd", (128, DT, ST, 128), BF16)

    a3 = []
    ACONS_END = {}
    cS = cexp = ccons = cmask = cev = 0
    for q in range(4):
        for h in range(NHC):
            for ci in range(2):
                s = 2 * q + ci
                npair = 2 * (s + 1)
                a3.append(dict(q=q, h=h, ci=ci, s=s, npair=npair,
                               Sb=cS, Eb=cexp, Cb=ccons, Mb=cmask, Vb=cev,
                               Ce=ccons + npair))
                cS += npair
                cexp += npair
                ccons += npair
                cmask += 2
                cev += 2
                if ci == 1:
                    ACONS_END[q * NHC + h] = ccons

    with ExitStack() as stk:
        p = P(nc, stk, SEMS)

        def T(name, shape, dt=F32):
            return stk.enter_context(nc.sbuf_tensor(name, list(shape), dt))

        bup_sb = T("bup_sb", (128, NX, ET))
        bsw_sb = T("bsw_sb", (128, NX, ET))
        bsu_sb = T("bsu_sb", (128, ST))
        bsa_sb = T("bsa_sb", (128, ET))
        bgA_sb = T("bgA_sb", (1, 8), BF16)
        bgB_sb = T("bgB_sb", (1, 8), BF16)
        bgr_sb = T("bgr_sb", (128, 2, 8))
        eps_sb = T("eps_sb", (128, 1))
        wg_sb = T("wg_sb", (128, DT, 2, 8), BF16)
        ident_sb = T("ident_sb", (128, 128))
        ones_sb = T("ones_sb", (128, 128), BF16)
        band_sb = T("band_sb", (128, 896), BF16)
        sel_sb = T("sel_sb", (8, NX * 128), BF16)
        identb_sb = T("identb_sb", (128, 128), BF16)
        onesrow_sb = T("onesrow_sb", (1, 512), BF16)

        pBig0 = stk.enter_context(nc.psum_tensor("pBig0", [128, 1024], F32))
        pBig1 = stk.enter_context(nc.psum_tensor("pBig1", [128, 1024], F32))
        pM = [stk.enter_context(nc.psum_tensor(f"pM{i}", [128, 512], F32))
              for i in range(4)]
        PB4 = [pBig0[:, 0:512], pBig0[:, 512:1024],
               pBig1[:, 0:512], pBig1[:, 512:1024]]
        BANK8 = [pM[0][:, :], pM[1][:, :], pM[2][:, :], pM[3][:, :]] + PB4

        with nc.Block() as block:

            @block.tensor
            def _(t):
                p.wait(t, "b0", 208)
                t.matmul(pM[0][:, 0:8], ones_sb[0:1, :], bgA_sb[0:1, :],
                         start=True, stop=True)
                p.inc(t.matmul(pM[1][:, 0:8], ones_sb[0:1, :], bgB_sb[0:1, :],
                               start=True, stop=True), "b0v")

            @block.vector
            def _(v):
                v.memset(eps_sb[:, :], EPS)
                p.wait(v, "b0v", 1)
                v.tensor_copy(bgr_sb[:, 0, :], pM[0][:, 0:8])
                p.inc(v.tensor_copy(bgr_sb[:, 1, :], pM[1][:, 0:8]), "b0v")

            with nc.sbuf_tensor("xlnT", [128, DT, TB], BF16) as xlnT, \
                 nc.sbuf_tensor("a2st", [128, 4, 512], BF16) as a2st, \
                 nc.sbuf_tensor("a1x", [128, 2, D], F32) as a1x, \
                 nc.sbuf_tensor("a1st", [128, 2, 4, 6], F32) as a1st, \
                 nc.sbuf_tensor("a1ag", [128, 2, 2], F32) as a1ag, \
                 nc.sbuf_tensor("a1rs", [128, 2, 2], F32) as a1rs, \
                 nc.sbuf_tensor("a1o", [128, 2, D], BF16) as a1o, \
                 nc.sbuf_tensor("a2w", [128, DT, 1024], BF16) as a2w:
                @block.gpsimd
                def _(g):
                    p.inc(g.dma_start(out=a2w[:, :, :], in_=wk_pm), "wqkv", 16)
                    for dst, src in [
                                (bup_sb[:, :, :], bupT), (bsw_sb[:, :, :], bswT),
                                (bsu_sb[:, :], bsuT), (bsa_sb[:, :], bsaT),
                                (bgA_sb[0:1, :], bg0), (bgB_sb[0:1, :], bg1),
                                (wg_sb[:, :, :, :], wg_pm),
                                (ident_sb[:, :], c_ident), (ones_sb[:, :], c_ones),
                                (identb_sb[:, :], c_identb),
                                (onesrow_sb[:, :], c_onesrow),
                                (band_sb[:, :], c_band), (sel_sb[:, :], c_sel)]:
                        p.inc(g.dma_start(out=dst, in_=src), "b0", 16)
                    for n in range(NX):
                        p.inc(g.dma_start(out=wsc_up[:, n], in_=wup_pm[:, n]),
                                  "p0", 16)
                        p.inc(g.dma_start(out=wsc_sw[:, n], in_=wsw_pm[:, n]),
                                  "p0", 16)
                    p.inc(g.dma_start(out=wsc_su[:, :], in_=wsu_pm), "p0", 16)
                    p.inc(g.dma_start(out=wsc_sa[:, :], in_=wsa_pm), "p0", 16)
                    for dt2 in range(DT):
                        p.inc(g.dma_start(out=wsc_dn[:, dt2],
                                          in_=wdn_pm[:, dt2]), "p0", 16)
                    p.inc(g.dma_start(out=wsc_sd[:, :], in_=wsd_pm), "p0", 16)
                B0_TOTAL = p.n["b0"]
                assert B0_TOTAL == 208, B0_TOTAL
                P0_TOTAL = p.n["p0"]

                @block.sync
                def _(s):
                    for tt in range(TTB):
                        if tt >= 2:
                            p.wait(s, "apl", tt - 1)
                        p.inc(s.dma_start(
                            out=a1x[:, tt % 2, :],
                            in_=x_full[tt * 128:(tt + 1) * 128, :]),
                            f"ld{tt % 2}", 16)

                def a1_apply(v, m):
                    b = m % 2
                    if m >= 2:
                        hh = (m - 2) // 16
                        p.wait(v, f"wxh{hh}", 16 * min(m - 16 * hh, 16))
                    p.wait(v, "rs1", 2 * (m + 1))
                    p.inc(v.tensor_scalar(a1o[:, b, :], a1x[:, b, :],
                                          a1ag[:, b, 0:1], a1rs[:, b, 1:2],
                                          ALU.subtract, ALU.mult), "apl")

                @block.vector
                def _(v):
                    for tt in range(TTB):
                        b = tt % 2
                        p.wait(v, f"ld{tt % 2}", 16 * (tt // 2 + 1))
                        for i in range(4):
                            ins = v.bn_stats(a1st[:, b, i, :],
                                             a1x[:, b, 512 * i:512 * (i + 1)])
                        p.inc(ins, "lnv")
                        p.wait(v, "lnv", p.n["lnv"])
                        if tt >= 2:
                            p.wait(v, "rs1", 2 * (tt - 1))
                        p.inc(v.bn_aggr(a1ag[:, b, :], a1st[:, b, :, :]),
                              "lnv")
                        if tt >= 1:
                            a1_apply(v, tt - 1)
                    a1_apply(v, TTB - 1)

                @block.scalar
                def _(sc):
                    p.wait(sc, "b0v", 2)
                    for tt in range(TTB):
                        b = tt % 2
                        p.wait(sc, "lnv", 2 * (tt + 1))
                        p.inc(sc.activation(a1rs[:, b, 0:1], a1ag[:, b, 1:2],
                                            AF.Ln, bias=eps_sb[:, 0:1]), "rs1")
                        p.wait(sc, "rs1", p.n["rs1"])
                        p.inc(sc.activation(a1rs[:, b, 1:2], a1rs[:, b, 0:1],
                                            AF.Exp, scale=-0.5), "rs1")
                        p.wait(sc, "apl", tt + 1)
                        p.inc(sc.dma_start(
                            out=xln_tm[tt * 128:(tt + 1) * 128, :],
                            in_=a1o[:, tt % 2, :]), f"wxh{tt // 16}", 16)

                @block.sync
                def _(s):
                    if PH < 2:
                        return
                    for th in range(2):
                        p.wait(s, f"wxh{th}", 16 * 16)
                        for dt in range(DT):
                            p.inc(s.dma_start(
                                out=xlnT[:, dt, th * 2048:(th + 1) * 2048],
                                in_=xln_tm[th * 2048:(th + 1) * 2048,
                                           dt * 128:(dt + 1) * 128],
                                transpose=True), f"txl{th}", 16)

                @block.gpsimd
                def _(g):
                    if PH < 2:
                        return
                    p.wait(g, "mm2", 64)
                    p.inc(g.dma_start(out=a2w[:, :, :], in_=wv_pm), "wqkv", 16)
                    p.wait(g, "mm2", 128)
                    p.inc(g.dma_start(out=a2w[:, :, :], in_=wq_pm), "wqkv", 16)

                @block.tensor
                def _(t):
                    if PH < 2:
                        return
                    mm = 0
                    p.wait(t, "wqkv", 16)
                    p.wait(t, "b0v", 2)
                    for ot in range(NHC):
                        for tc in range(8):
                            p.wait(t, f"txl{tc // 4}", 16 * 16)
                            if mm >= 4:
                                p.wait(t, "ev2", mm - 3)
                            for dt in range(DT):
                                ins = t.matmul(
                                    pM[mm % 4][:, :],
                                    a2w[:, dt, ot * 128:(ot + 1) * 128],
                                    xlnT[:, dt, tc * 512:(tc + 1) * 512],
                                    start=(dt == 0), stop=(dt == DT - 1))
                            p.inc(ins, "mm2")
                            mm += 1
                    for tt in range(KT):
                        for oc in range(2):
                            if tt == 0 and oc == 0:
                                p.wait(t, "wqkv", 32)
                                p.wait(t, "txl0", 16 * 16)
                            if tt == 16 and oc == 0:
                                p.wait(t, "txl1", 16 * 16)
                            if mm >= 4:
                                p.wait(t, "ev2", mm - 3)
                            for dt in range(DT):
                                ins = t.matmul(
                                    pM[mm % 4][:, :],
                                    xlnT[:, dt, tt * 128:(tt + 1) * 128],
                                    a2w[:, dt, oc * 512:(oc + 1) * 512],
                                    start=(dt == 0), stop=(dt == DT - 1))
                            p.inc(ins, "mm2")
                            mm += 1
                    p.wait(t, "wqkv", 48)
                    for ot in range(NHC):
                        for tc in range(8):
                            if mm >= 4:
                                p.wait(t, "ev2", mm - 3)
                            for dt in range(DT):
                                ins = t.matmul(
                                    pM[mm % 4][:, :],
                                    a2w[:, dt, ot * 128:(ot + 1) * 128],
                                    xlnT[:, dt, tc * 512:(tc + 1) * 512],
                                    start=(dt == 0), stop=(dt == DT - 1))
                            p.inc(ins, "mm2")
                            mm += 1

                @block.scalar
                def _(sc):
                    if PH < 2:
                        return
                    for i in range(192):
                        p.wait(sc, "mm2", i + 1)
                        if i >= 4:
                            p.wait(sc, f"spl{i % 4}", 16 * (i // 4))
                        p.inc(sc.copy(a2st[:, i % 4, :], pM[i % 4][:, :]),
                              "ev2")
                        p.wait(sc, "ev2", p.n["ev2"])
                        if i < 64:
                            ot, tc = divmod(i, 8)
                            ins = sc.dma_start(
                                out=kT_sc[ot, :, tc * 512:(tc + 1) * 512],
                                in_=a2st[:, i % 4, :])
                        elif i < 128:
                            tt, oc = divmod(i - 64, 2)
                            ins = sc.dma_start(
                                out=v_sc[oc * 4:(oc + 1) * 4, :, tt, :]
                                .rearrange("hh pp j -> pp hh j"),
                                in_=a2st[:, i % 4, :].rearrange(
                                    "pp (hh j) -> pp hh j", hh=4))
                        else:
                            ot, tc = divmod(i - 128, 8)
                            ins = sc.dma_start(
                                out=qT_sc[ot, :, tc * 512:(tc + 1) * 512],
                                in_=a2st[:, i % 4, :])
                        p.inc(ins, f"spl{i % 4}", 16)
            A2_MM_TOTAL = p.n["mm2"]

            with nc.sbuf_tensor("kbuf", [128, 2, TB], BF16) as kbuf, \
                 nc.sbuf_tensor("vbuf", [128, 2, KT, 128], BF16) as vbuf, \
                 nc.sbuf_tensor("qbuf", [128, 2, 1024], BF16) as qbuf, \
                 nc.sbuf_tensor("abuf", [128, 2, 1024], BF16) as abuf, \
                 nc.sbuf_tensor("rbuf", [128, 512], F32) as rbuf, \
                 nc.sbuf_tensor("attn", [128, NHC, TB], BF16) as attn:

                @block.sync
                def _(s):
                    if PH < 3:
                        return
                    for sl in range(4):
                        p.wait(s, f"spl{sl}", 16 * 48)
                    for q in range(4):
                        for h in range(NHC):
                            gh = q * NHC + h
                            if gh >= 2:
                                p.wait(s, "acons", ACONS_END[gh - 2])
                            p.inc(s.dma_start(out=kbuf[:, gh % 2, :],
                                              in_=kT_sc[h]), f"hld{gh % 2}", 16)
                            p.inc(s.dma_start(out=vbuf[:, gh % 2, :, :],
                                              in_=v_sc[h]), f"hld{gh % 2}", 16)
                            p.inc(s.dma_start(
                                out=qbuf[:, gh % 2, :],
                                in_=qT_sc[h, :, q * 1024:(q + 1) * 1024]),
                                f"hld{gh % 2}", 16)

                @block.tensor
                def _(t):
                    if PH < 3:
                        return
                    p.wait(t, "ev2", 192)
                    for ck in a3:
                        gh = ck["q"] * NHC + ck["h"]
                        b = gh % 2
                        qsl = slice(ck["ci"] * 512, (ck["ci"] + 1) * 512)
                        p.wait(t, f"hld{gh % 2}", 48 * (gh // 2 + 1))
                        npair, s = ck["npair"], ck["s"]

                        def consume(pp):
                            first, last = pp == 0, pp == npair - 1
                            p.wait(t, "aexp", ck["Eb"] + pp + 1)
                            if pp >= 2 * s:
                                p.wait(t, "amask",
                                       ck["Mb"] + (pp - 2 * s) + 1)
                            if first:
                                p.wait(t, "aev", ck["Vb"])
                            for u in range(2):
                                kt = 2 * pp + u
                                asl = slice(u * 512, (u + 1) * 512)
                                t.matmul(pM[0][:, :], ones_sb[:, :],
                                         abuf[:, pp % 2, asl],
                                         start=(first and u == 0),
                                         stop=(last and u == 1))
                                ins = t.matmul(pM[1][:, :],
                                               vbuf[:, b, kt, :],
                                               abuf[:, pp % 2, asl],
                                               start=(first and u == 0),
                                               stop=(last and u == 1))
                            p.inc(ins, "acons")

                        for pp in range(npair):
                            if pp >= 2:
                                p.wait(t, "aexp", ck["Eb"] + pp - 1)
                            for u in range(2):
                                kt = 2 * pp + u
                                pS = pBig0 if pp % 2 == 0 else pBig1
                                ins = t.matmul(
                                    pS[:, u * 512:(u + 1) * 512],
                                    kbuf[:, b, kt * 128:(kt + 1) * 128],
                                    qbuf[:, b, qsl], start=True, stop=True)
                            p.inc(ins, "aS")
                            if pp >= 1:
                                consume(pp - 1)
                        consume(npair - 1)

                @block.scalar
                def _(sc):
                    if PH < 3:
                        return
                    for ck in a3:
                        for pp in range(ck["npair"]):
                            pS = pBig0 if pp % 2 == 0 else pBig1
                            p.wait(sc, "aS", ck["Sb"] + pp + 1)
                            p.wait(sc, "acons", ck["Cb"] + pp - 1)
                            p.inc(sc.activation(abuf[:, pp % 2, :], pS[:, :],
                                                AF.Exp, scale=SCALE), "aexp")

                @block.vector
                def _(v):
                    if PH < 3:
                        return
                    for ck in a3:
                        s, npair = ck["s"], ck["npair"]
                        for pp in (2 * s, 2 * s + 1):
                            p.wait(v, "aexp", ck["Eb"] + pp + 1)
                            for u in range(2):
                                i = 2 * (pp - 2 * s) + u
                                off = 384 - 128 * i
                                asl = slice(u * 512, (u + 1) * 512)
                                ins = v.tensor_mul(abuf[:, pp % 2, asl],
                                                   abuf[:, pp % 2, asl],
                                                   band_sb[:, off:off + 512])
                            p.inc(ins, "amask")
                        p.wait(v, "acons", ck["Cb"] + npair)
                        p.inc(v.reciprocal(rbuf[:, :], pM[0][:, :]), "aev")
                        p.wait(v, "aev", p.n["aev"])
                        p.inc(v.tensor_mul(
                            attn[:, ck["h"], s * 512:(s + 1) * 512],
                            pM[1][:, :], rbuf[:, :]), "aev")

                @block.scalar
                def _(sc):
                    if PH < 3:
                        return
                    for q in range(4):
                        p.wait(sc, "aev", 32 * (q + 1))
                        for ch in range(2):
                            p.inc(sc.dma_start(
                                out=attn_mine[q, 512 * ch:512 * (ch + 1), :]
                                .rearrange("(ct pp) j -> pp ct j", pp=128),
                                in_=attn[:, 4 * ch:4 * (ch + 1),
                                         q * 1024:(q + 1) * 1024]),
                                f"am{q}", 16)

                @block.gpsimd
                def _(g):
                    if PH < 3:
                        return
                    for q in range(4):
                        p.wait(g, f"am{q}", 32)
                        p.inc(g.collective_compute(
                            "AllGather", ALU.bypass,
                            ins=[attn_mine[q]], outs=[attn_full[q]],
                            replica_groups=RG_PAIRS), "cg", 1)
            A3_CONS = p.n["acons"]

            with nc.sbuf_tensor("gqb", [128, DT, 1024], BF16) as gqb, \
                 nc.sbuf_tensor("wob", [128, 4, DT, 512], BF16) as wob, \
                 nc.sbuf_tensor("a4x", [128, 2, D], F32) as a4x, \
                 nc.sbuf_tensor("x1b", [128, 2, D], F32) as x1b, \
                 nc.sbuf_tensor("a4st", [128, 2, 4, 6], F32) as a4st, \
                 nc.sbuf_tensor("a4ag", [128, 2, 2], F32) as a4ag, \
                 nc.sbuf_tensor("a4rs", [128, 2, 2], F32) as a4rs, \
                 nc.sbuf_tensor("resb", [128, 2, D], F32) as resb, \
                 nc.sbuf_tensor("xt0b", [128, 2, D], BF16) as xt0b, \
                 nc.sbuf_tensor("rlob", [128, 2, D], BF16) as rlob, \
                 nc.sbuf_tensor("bhib", [128, 2, D], BF16) as bhib, \
                 nc.sbuf_tensor("blob", [128, 2, D], BF16) as blob:

                @block.sync
                def _(s):
                    if PH < 4:
                        return
                    pid = s.partition_id()
                    with s.register("rq") as rq, s.register("rqi0") as rqi0, \
                            s.register("rqi1") as rqi1:
                        s.reg_alu(rq, pid, 1, ALU.bitwise_and)
                        s.reg_mul(rq, rq, 2)
                        s.reg_alu(rqi0, rq, 0, ALU.add)
                        s.reg_alu(rqi1, rq, 1, ALU.add)
                        p.wait(s, "cg", 4)
                        p.wait(s, "acons", A3_CONS)
                        def gq_load(rr):
                            for ch in range(2):
                                p.inc(s.dma_start(
                                    out=gqb[:, 8 * ch:8 * (ch + 1), :],
                                    in_=attn_full[ds(RuntimeValue(rr, 0, 3), 1),
                                                  1024 * ch:1024 * (ch + 1),
                                                  :].rearrange(
                                        "one (ct pp) j -> pp (one ct) j",
                                        pp=128)), "gq", 16)
                        gq_load(rqi0)
                        for lt in range(TTC):
                            if lt >= 2:
                                p.wait(s, "ap4", 6 * (lt - 1))
                            p.inc(s.dma_start(
                                out=a4x[:, lt % 2, :],
                                in_=x_mine[lt * 128:(lt + 1) * 128, :]),
                                f"ld{lt % 2}", 16)
                            if lt == 7:
                                p.wait(s, "mm4", 32)
                                gq_load(rqi1)

                @block.gpsimd
                def _(g):
                    if PH < 4:
                        return
                    p.wait(g, "acons", A3_CONS)
                    for qq in range(4):
                        p.wait(g, f"am{qq}", 32)
                    p.inc(g.dma_start(out=wob[:, :], in_=wo_pm), "wo4", 16)

                @block.tensor
                def _(t):
                    if PH < 4:
                        return
                    p.wait(t, "wo4", 16)
                    for lt in range(TTC):
                        qi, ltq = divmod(lt, 8)
                        p.wait(t, "gq", 32 * (qi + 1))
                        for dc in range(4):
                            i = lt * 4 + dc
                            if i >= 2:
                                p.wait(t, "ev4", i - 1)
                            for ct in range(DT):
                                ins = t.matmul(
                                    pM[i % 2][:, :],
                                    gqb[:, ct, ltq * 128:(ltq + 1) * 128],
                                    wob[:, dc, ct, :],
                                    start=(ct == 0), stop=(ct == DT - 1))
                            p.inc(ins, "mm4")

                def a4_apply(v, m):
                    b = m % 2
                    if m >= 2:
                        p.wait(v, f"wrs{m % 2}", 64 * (m // 2))
                    p.wait(v, "rs4", 2 * (m + 1))
                    p.inc(v.tensor_scalar(resb[:, b, :], x1b[:, b, :],
                                          a4ag[:, b, 0:1], a4rs[:, b, 1:2],
                                          ALU.subtract, ALU.mult), "ap4")
                    p.wait(v, "ap4", p.n["ap4"])
                    p.inc(v.tensor_copy(xt0b[:, b, :], resb[:, b, :]), "ap4")
                    p.wait(v, "ap4", p.n["ap4"])
                    p.inc(v.tensor_sub(rlob[:, b, :], resb[:, b, :],
                                       xt0b[:, b, :]), "ap4")
                    p.inc(v.tensor_add(bhib[:, b, :], x1b[:, b, :],
                                       resb[:, b, :]), "ap4")
                    p.wait(v, "ap4", p.n["ap4"])
                    p.inc(v.tensor_sub(a4x[:, b, :], x1b[:, b, :],
                                       bhib[:, b, :]), "ap4")
                    p.wait(v, "ap4", p.n["ap4"])
                    p.inc(v.tensor_add(blob[:, b, :], a4x[:, b, :],
                                       resb[:, b, :]), "ap4")

                @block.vector
                def _(v):
                    if PH < 4:
                        return
                    for lt in range(TTC):
                        b = lt % 2
                        p.wait(v, f"ld{lt % 2}", 16 * (16 + lt // 2 + 1))
                        for dc in range(4):
                            i = lt * 4 + dc
                            p.wait(v, "mm4", i + 1)
                            p.inc(v.scalar_tensor_tensor(
                                x1b[:, b, dc * 512:(dc + 1) * 512],
                                pM[i % 2][:, :], 1.0,
                                a4x[:, b, dc * 512:(dc + 1) * 512],
                                ALU.mult, ALU.add), "ev4")
                        p.wait(v, "ev4", p.n["ev4"])
                        for i in range(4):
                            ins = v.bn_stats(a4st[:, b, i, :],
                                             x1b[:, b, 512 * i:512 * (i + 1)])
                        p.inc(ins, "ln4")
                        p.wait(v, "ln4", p.n["ln4"])
                        if lt >= 2:
                            p.wait(v, "rs4", 2 * (lt - 1))
                        p.inc(v.bn_aggr(a4ag[:, b, :], a4st[:, b, :, :]),
                              "ln4")
                        if lt >= 1:
                            a4_apply(v, lt - 1)
                    a4_apply(v, TTC - 1)

                @block.scalar
                def _(sc):
                    if PH < 4:
                        return
                    for lt in range(TTC):
                        b = lt % 2
                        p.wait(sc, "ln4", 2 * (lt + 1))
                        p.inc(sc.activation(a4rs[:, b, 0:1], a4ag[:, b, 1:2],
                                            AF.Ln, bias=eps_sb[:, 0:1]), "rs4")
                        p.wait(sc, "rs4", p.n["rs4"])
                        p.inc(sc.activation(a4rs[:, b, 1:2], a4rs[:, b, 0:1],
                                            AF.Exp, scale=-0.5), "rs4")
                        rows = slice(lt * 128, (lt + 1) * 128)
                        p.wait(sc, "ap4", 6 * (lt + 1))
                        p.inc(sc.dma_start(out=xt0_tm[rows, :],
                                           in_=xt0b[:, b, :]), f"wrs{b}", 16)
                        p.inc(sc.dma_start(out=rlo_tm[rows, :],
                                           in_=rlob[:, b, :]), f"wrs{b}", 16)
                        p.inc(sc.dma_start(out=bhi_tm[rows, :],
                                           in_=bhib[:, b, :]), f"wrs{b}", 16)
                        p.inc(sc.dma_start(out=blo_tm[rows, :],
                                           in_=blob[:, b, :]), f"wrs{b}", 16)
            A4_MM_TOTAL = p.n["mm4"]

            with ExitStack() as fstk:
                xtT = fstk.enter_context(nc.sbuf_tensor("xtT", [128, 2, DT, G], BF16))
                hbuf = fstk.enter_context(nc.sbuf_tensor("hbuf", [128, 2, ET, G], BF16))
                h2w = fstk.enter_context(nc.sbuf_tensor("h2w", [128, ET, 8, G], BF16))
                warA = fstk.enter_context(nc.sbuf_tensor("warA", [128, 2, DT, 512], BF16))
                warB = fstk.enter_context(nc.sbuf_tensor("warB", [128, 2, ET, 512], BF16))
                wsdb = fstk.enter_context(nc.sbuf_tensor("wsdb", [128, 2, ST, 128], BF16))
                wrep = fstk.enter_context(nc.sbuf_tensor("wrep", [128, NX, 512], BF16))
                rlg = fstk.enter_context(nc.sbuf_tensor("rlg", [128, 2, 512], BF16))
                bhg = fstk.enter_context(nc.sbuf_tensor("bhg", [128, 2, 512], BF16))
                stg = fstk.enter_context(nc.sbuf_tensor("stg", [128, 2, 512], F32))
                stgb = fstk.enter_context(nc.sbuf_tensor("stgb", [128, 2, 512], BF16))
                gw = fstk.enter_context(nc.sbuf_tensor("gw", [128, 4, 8], F32))
                gmx = fstk.enter_context(nc.sbuf_tensor("gmx", [128, 4, 8], F32))
                gcmp = fstk.enter_context(nc.sbuf_tensor("gcmp", [128, 4, 8], F32))
                gnm = fstk.enter_context(nc.sbuf_tensor("gnm", [128, 4, 2], F32))
                wT8 = fstk.enter_context(nc.sbuf_tensor("wT8", [8, G], BF16))
                bdn_sb = fstk.enter_context(nc.sbuf_tensor("bdn_sb", [NX, D], BF16))
                bsd_sb = fstk.enter_context(nc.sbuf_tensor("bsd_sb", [1, D], BF16))

                EXPD = {}

                @block.gpsimd
                def _(g):
                    if PH < 5:
                        return
                    p.wait(g, "mm4", A4_MM_TOTAL)
                    p.wait(g, "ap4", 6 * TTC)
                    p.wait(g, "wrs0", 64 * 8)
                    p.wait(g, "wrs1", 64 * 8)
                    p.inc(g.dma_start(out=bdn_sb[:, :], in_=bdn_r), "b0", 16)
                    p.inc(g.dma_start(out=bsd_sb[:, :], in_=bsd_r), "b0", 16)
                B0F = p.n["b0"]

                def emit_h(t, gi, n):
                    for et in range(ET):
                        u = n * ET + et
                        i1 = gi * 16 + 2 * n + et // 4
                        p.wait(t, f"wlA{i1 % 2}", 16 * (i1 // 2 + 1))
                        p.wait(t, "fhev", 64 * gi + u - 3)
                        if n == 0 and et == 0:
                            p.wait(t, "gtv", 4 * (gi + 1))
                        if n == 0 and et == 2:
                            p.wait(t, "gts", gi + 1)
                        for dt in range(DT):
                            ins = t.matmul(
                                PB4[u % 4],
                                warA[:, i1 % 2, dt,
                                     (et % 4) * 128:(et % 4) * 128 + 128],
                                xtT[:, gi % 2, dt, :],
                                start=(dt == 0), stop=(dt == DT - 1))
                        p.inc(ins, "fh")

                def emit_s(t, gi):
                    for ot in range(ST):
                        u = 48 + ot
                        i1 = gi * 16 + 12 + ot // 4
                        p.wait(t, f"wlA{i1 % 2}", 16 * (i1 // 2 + 1))
                        p.wait(t, "fhev", 64 * gi + u - 3)
                        for dt in range(DT):
                            ins = t.matmul(
                                PB4[u % 4],
                                warA[:, i1 % 2, dt,
                                     (ot % 4) * 128:(ot % 4) * 128 + 128],
                                xtT[:, gi % 2, dt, :],
                                start=(dt == 0), stop=(dt == DT - 1))
                        p.inc(ins, "fh")

                def emit_z(t, gi, n):
                    for ft in range(ET):
                        u = n * ET + ft
                        i2 = gi * 14 + 2 * n + ft // 4
                        p.wait(t, f"wlB{i2 % 2}", 16 * (i2 // 2 + 1))
                        p.wait(t, "fhev", 64 * gi + ET * (n + 1))
                        p.wait(t, "fzev", 64 * gi + u - 3)
                        if u < 4:
                            p.wait(t, "wrp", 6 * (gi + 1))
                        for et in range(ET):
                            ins = t.matmul(
                                pM[u % 4][:, :],
                                warB[:, i2 % 2, et,
                                     (ft % 4) * 128:(ft % 4) * 128 + 128],
                                hbuf[:, n % 2, et, :],
                                start=(et == 0), stop=(et == ET - 1))
                        p.inc(ins, "fz")

                def emit_zs(t, gi):
                    for k in range(2):
                        for ft in range(ET):
                            u = 48 + k * ET + ft
                            i2 = gi * 14 + 12 + ft // 4
                            p.wait(t, f"wlB{i2 % 2}", 16 * (i2 // 2 + 1))
                            p.wait(t, "fhev", 64 * gi + 48 + ET * (k + 1))
                            p.wait(t, "fzev", 64 * gi + u - 3)
                            for et in range(ET):
                                ins = t.matmul(
                                    pM[u % 4][:, :],
                                    warB[:, i2 % 2, et,
                                         (ft % 4) * 128:(ft % 4) * 128 + 128],
                                    hbuf[:, k, et, :],
                                    start=(et == 0), stop=(et == ET - 1))
                            p.inc(ins, "fz")

                for j in range(2 if PH >= 5 else 0):
                    for grp in range(NG):
                        gi = j * NG + grp
                        if gi >= FFN_GROUPS:
                            continue
                        src_tm = xt0_tm
                        rows = slice(grp * G, (grp + 1) * G)

                        @block.sync
                        def _(s, j=j, grp=grp, gi=gi, src_tm=src_tm,
                              rows=rows):
                            if j == 1:
                                for sl in range(2):
                                    p.wait(s, f"fq{sl}", 16 * 8 * (grp + 1))
                            if gi == 0:
                                p.wait(s, "mm4", A4_MM_TOTAL)
                                p.wait(s, "wrs0", 64 * 8)
                                p.wait(s, "wrs1", 64 * 8)
                                p.wait(s, "ap4", 6 * TTC)
                                p.wait(s, "b0v", 2)
                                p.wait(s, "p0", P0_TOTAL)
                            if gi >= 2:
                                p.wait(s, "fhw", 112 * (gi - 1))
                            if j == 0:
                                for dt in range(DT):
                                    p.inc(s.dma_start(
                                        out=xtT[:, gi % 2, dt, :],
                                        in_=src_tm[rows,
                                                   dt * 128:(dt + 1) * 128],
                                        transpose=True),
                                        f"xtb{gi % 2}", 16)
                            else:
                                p.inc(s.dma_start(
                                    out=xtT[:, gi % 2],
                                    in_=xt1_fm[:, grp]),
                                    f"xtb{gi % 2}", 16)
                            EXPD[("xtb", gi)] = p.n[f"xtb{gi % 2}"]
                            lorder = []
                            for n in range(NX):
                                lorder += [("wl1", 2 * n), ("wl1", 2 * n + 1)]
                                if n >= 1:
                                    lorder += [("wl2", 2 * n - 2),
                                               ("wl2", 2 * n - 1)]
                            lorder += [("wl2", 10), ("wl2", 11),
                                       ("wl1", 12), ("wl1", 13),
                                       ("wl1", 14), ("wl1", 15),
                                       ("wl2", 12), ("wl2", 13)]
                            p.wait(s, "p2", 16 * gi)
                            for kind, li in lorder:
                                if kind == "wl1":
                                    i1 = gi * 16 + li
                                    if i1 >= 2:
                                        pg, pl = divmod(i1 - 2, 16)
                                        p.wait(s, "fh", 64 * pg + 4 * pl + 4)
                                    if li < 12:
                                        n, eh = divmod(li, 2)
                                        src = wsc_up[:, n, eh]
                                    else:
                                        src = wsc_su[:, li - 12]
                                    p.inc(s.dma_start(out=warA[:, i1 % 2],
                                                      in_=src),
                                          f"wlA{i1 % 2}", 16)
                                else:
                                    i2 = gi * 14 + li
                                    if i2 >= 2:
                                        pg, pl = divmod(i2 - 2, 14)
                                        after = (4 * pl + 4 if pl < 12
                                                 else (60 if pl == 12 else 64))
                                        p.wait(s, "fz", 64 * pg + after)
                                    if li < 12:
                                        n, fh = divmod(li, 2)
                                        src = wsc_sw[:, n, fh]
                                    else:
                                        src = wsc_sa[:, li - 12]
                                    p.inc(s.dma_start(out=warB[:, i2 % 2],
                                                      in_=src),
                                          f"wlB{i2 % 2}", 16)
                            for dt2 in range(DT):
                                i3 = gi * DT + dt2
                                par = dt2 % 2
                                if dt2 == 0:
                                    p.wait(s, "fh", 64 * (gi + 1))
                                    p.wait(s, "fz", 64 * (gi + 1))
                                if i3 >= 2:
                                    p.wait(s, "p2", i3 - 1)
                                p.inc(s.dma_start(
                                    out=warA[:, par].rearrange(
                                        "pp a c -> pp (a c)")[:, 0:6144],
                                    in_=wsc_dn[:, dt2].rearrange(
                                        "pp a b c -> pp (a b c)")),
                                    f"wlC{par}", 16)
                                p.inc(s.dma_start(
                                    out=wsdb[:, par],
                                    in_=wsc_sd[:, dt2]),
                                    f"wlC{par}", 16)
                                EXPD[("wlC", gi, dt2)] = p.n[f"wlC{par}"]
                                if i3 >= 2:
                                    p.wait(s, "p2ev", i3 - 1)
                                dsl = slice(dt2 * 128, (dt2 + 1) * 128)
                                if j == 0:
                                    p.inc(s.dma_start(
                                        out=rlg[:, par, :],
                                        in_=rlo_tm[rows, dsl],
                                        transpose=True), f"rsg{par}", 16)
                                else:
                                    p.inc(s.dma_start(
                                        out=bhg[:, par, :],
                                        in_=bhi_tm[rows, dsl],
                                        transpose=True), f"rsg{par}", 16)
                                    p.inc(s.dma_start(
                                        out=rlg[:, par, :],
                                        in_=blo_tm[rows, dsl],
                                        transpose=True), f"rsg{par}", 16)
                                EXPD[("rsg", gi, dt2)] = p.n[f"rsg{par}"]

                        @block.tensor
                        def _(t, j=j, grp=grp, gi=gi):
                            b = gi % 2
                            p.wait(t, f"xtb{gi % 2}", EXPD[("xtb", gi)])
                            p.wait(t, "p2ev", 16 * gi)
                            if gi == 0:
                                p.wait(t, "ev4", 64)
                            for tt in range(4):
                                p.wait(t, "gtv", 4 * gi + tt)
                                if gi == 0 and tt == 0:
                                    p.wait(t, "b0v", 2)
                                for dt in range(DT):
                                    ins = t.matmul(
                                        pBig0[:, 0:8],
                                        xtT[:, b, dt,
                                            tt * 128:(tt + 1) * 128],
                                        wg_sb[:, dt, j, :],
                                        start=(dt == 0),
                                        stop=(dt == DT - 1))
                                p.inc(ins, "gt")
                            for tt in range(4):
                                p.wait(t, "gtw", 4 * gi + tt + 1)
                                p.inc(t.transpose(
                                    pBig1[0:8, tt * 128:(tt + 1) * 128],
                                    gw[:, tt, :], ident_sb[:, :]), "gt")
                            p.wait(t, "gts", gi + 1)
                            for n in range(NX):
                                if 6 * gi + n >= 4:
                                    p.wait(t, "wrp", 6 * gi + n - 3)
                                p.inc(t.matmul(
                                    pM[n % 4][:, :],
                                    sel_sb[:, n * 128:(n + 1) * 128],
                                    wT8[0:8, :], start=True, stop=True),
                                    "gt")
                            emit_h(t, gi, 0)
                            for n in range(1, NX):
                                emit_h(t, gi, n)
                                emit_z(t, gi, n - 1)
                            emit_z(t, gi, NX - 1)
                            emit_s(t, gi)
                            emit_zs(t, gi)
                            p.wait(t, "fhw", 112 * (gi + 1))
                            if gi == 0:
                                p.wait(t, "b0", B0F)
                            for dt2 in range(DT):
                                i3 = gi * DT + dt2
                                par = dt2 % 2
                                p.wait(t, f"wlC{par}",
                                       EXPD[("wlC", gi, dt2)])
                                p.wait(t, f"rsg{par}",
                                       EXPD[("rsg", gi, dt2)])
                                if i3 >= 2:
                                    p.wait(t, "p2ev", i3 - 1)
                                bank = pM[par][:, :]
                                wAf = warA[:, par].rearrange(
                                    "pp a c -> pp (a c)")
                                hi = (xtT[:, gi % 2, dt2, :] if j == 0
                                      else bhg[:, par, :])
                                t.matmul(bank, identb_sb[:, :], hi,
                                         start=True, stop=False)
                                t.matmul(bank, identb_sb[:, :],
                                         rlg[:, par, :],
                                         start=False, stop=False)
                                for n in range(NX):
                                    for et in range(ET):
                                        idx = (n * ET + et) * 128
                                        t.matmul(bank,
                                                 wAf[:, idx:idx + 128],
                                                 h2w[:, et, n, :],
                                                 start=False, stop=False)
                                for st in range(ST):
                                    k, ft = divmod(st, ET)
                                    t.matmul(bank, wsdb[:, par, st, :],
                                             h2w[:, ft, 6 + k, :],
                                             start=False, stop=False)
                                dsl = slice(dt2 * 128, (dt2 + 1) * 128)
                                t.matmul(bank, bdn_sb[0:6, dsl],
                                         wT8[0:6, :],
                                         start=False, stop=False)
                                ins = t.matmul(bank, bsd_sb[0:1, dsl],
                                               onesrow_sb[0:1, :],
                                               start=False, stop=True)
                                p.inc(ins, "p2")

                        @block.scalar
                        def _(sc, j=j, grp=grp, gi=gi):
                            for tt in range(4):
                                p.wait(sc, "gex", 4 * gi + tt + 1)
                                p.inc(sc.activation(
                                    gmx[:, tt, :], gw[:, tt, :], AF.Exp,
                                    bias=gnm[:, tt, 0:1]), "gey")

                            def hev(n):
                                for et in range(ET):
                                    u = n * ET + et
                                    p.wait(sc, "fh", 64 * gi + u + 1)
                                    if u == 0:
                                        p.wait(sc, "fhw", 112 * gi)
                                    if n >= 2 and et == 0:
                                        p.wait(sc, "fz",
                                               64 * gi + ET * (n - 1))
                                        p.wait(sc, "fhw",
                                               112 * gi + 16 * (n - 1))
                                    p.inc(sc.activation(
                                        hbuf[:, n % 2, et, :], PB4[u % 4],
                                        AF.Identity,
                                        bias=bup_sb[:, n, et:et + 1]),
                                        "fhev")

                            def sev(half):
                                for ot in range(8 * half, 8 * half + 8):
                                    u = 48 + ot
                                    p.wait(sc, "fh", 64 * gi + u + 1)
                                    if ot == 0:
                                        p.wait(sc, "fz", 64 * gi + 40)
                                        p.wait(sc, "fhw", 112 * gi + 80)
                                    if ot == 8:
                                        p.wait(sc, "fz", 64 * gi + 48)
                                        p.wait(sc, "fhw", 112 * gi + 96)
                                    p.inc(sc.activation(
                                        hbuf[:, ot // 8, ot % 8, :],
                                        PB4[u % 4], AF.Identity,
                                        bias=bsu_sb[:, ot:ot + 1]), "fhev")

                            def zev(n):
                                for ft in range(ET):
                                    u = n * ET + ft
                                    p.wait(sc, "fz", 64 * gi + u + 1)
                                    if u == 0:
                                        p.wait(sc, "p2", 16 * gi)
                                    p.inc(sc.activation(
                                        h2w[:, ft, n, :], pM[u % 4][:, :],
                                        AF.Silu,
                                        bias=bsw_sb[:, n, ft:ft + 1]),
                                        "fzev")

                            def zsev():
                                for k in range(2):
                                    for ft in range(ET):
                                        u = 48 + k * ET + ft
                                        p.wait(sc, "fz", 64 * gi + u + 1)
                                        p.inc(sc.activation(
                                            h2w[:, ft, 6 + k, :],
                                            pM[u % 4][:, :], AF.Silu,
                                            bias=bsa_sb[:, ft:ft + 1]),
                                            "fzev")

                            hev(0)
                            for n in range(1, NX):
                                hev(n)
                                zev(n - 1)
                            zev(NX - 1)
                            sev(0)
                            sev(1)
                            zsev()
                            for dt2 in range(DT):
                                i3 = gi * DT + dt2
                                par = dt2 % 2
                                p.wait(sc, "p2ev", i3 + 1)
                                if j == 0:
                                    ins = sc.dma_start(
                                        out=xt1_fm[:, grp, dt2, :],
                                        in_=stgb[:, par, :])
                                    p.inc(ins, f"fq{par}", 16)
                                else:
                                    ins = sc.dma_start(
                                        out=y_mine[:, dt2,
                                                   grp * G:(grp + 1) * G],
                                        in_=stg[:, par, :])
                                    p.inc(ins, f"oq{par}", 16)

                        @block.vector
                        def _(v, j=j, grp=grp, gi=gi):
                            for tt in range(4):
                                p.wait(v, "gt", 14 * gi + tt + 1)
                                p.inc(v.tensor_add(gw[:, tt, :],
                                                   pBig0[:, 0:8],
                                                   bgr_sb[:, j, :]), "gtv")
                                p.wait(v, "gtv", p.n["gtv"])
                                p.inc(v.max(gmx[:, tt, :], gw[:, tt, :]),
                                      "gch")
                                p.wait(v, "gch", p.n["gch"])
                                p.inc(v.tensor_scalar(
                                    gcmp[:, tt, :], gw[:, tt, :],
                                    gmx[:, tt, 5:6], None, ALU.is_ge),
                                    "gch")
                                p.inc(v.tensor_scalar_mul(
                                    gnm[:, tt, 0:1], gmx[:, tt, 0:1], -1.0),
                                    "gex")
                            for tt in range(4):
                                p.wait(v, "gey", 4 * gi + tt + 1)
                                p.wait(v, "gch", p.n["gch"])
                                p.inc(v.tensor_mul(gw[:, tt, :],
                                                   gmx[:, tt, :],
                                                   gcmp[:, tt, :]), "gch")
                                p.wait(v, "gch", p.n["gch"])
                                p.inc(v.reduce_sum(gnm[:, tt, 1:2],
                                                   gw[:, tt, :],
                                                   axis=AX.X), "gch")
                                p.wait(v, "gch", p.n["gch"])
                                p.inc(v.reciprocal(gnm[:, tt, 0:1],
                                                   gnm[:, tt, 1:2]), "gch")
                                p.wait(v, "gch", p.n["gch"])
                                p.inc(v.tensor_scalar_mul(
                                    gw[:, tt, :], gw[:, tt, :],
                                    gnm[:, tt, 0:1]), "gtw")
                            p.wait(v, "gt", 14 * gi + 8)
                            p.inc(v.tensor_copy(wT8[0:8, :],
                                                pBig1[0:8, 0:512]), "gts")
                            for n in range(NX):
                                p.wait(v, "gt", 14 * gi + 8 + n + 1)
                                p.inc(v.tensor_copy(wrep[:, n, :],
                                                    pM[n % 4][:, :]), "wrp")
                            for u in range(64):
                                p.wait(v, "fzev", 64 * gi + u + 1)
                                if u < 48:
                                    n, ft = divmod(u, ET)
                                    p.wait(v, "wrp", 6 * gi + n + 1)
                                    p.inc(v.tensor_mul(
                                        h2w[:, ft, n, :], h2w[:, ft, n, :],
                                        hbuf[:, n % 2, ft, :]), "fhw")
                                    p.wait(v, "fhw", p.n["fhw"])
                                    p.inc(v.tensor_mul(
                                        h2w[:, ft, n, :], h2w[:, ft, n, :],
                                        wrep[:, n, :]), "fhw")
                                else:
                                    k, ft = divmod(u - 48, ET)
                                    p.inc(v.tensor_mul(
                                        h2w[:, ft, 6 + k, :],
                                        h2w[:, ft, 6 + k, :],
                                        hbuf[:, k, ft, :]), "fhw")
                            for dt2 in range(DT):
                                i3 = gi * DT + dt2
                                par = dt2 % 2
                                il = grp * DT + dt2
                                p.wait(v, "p2", i3 + 1)
                                if il >= 2:
                                    p.wait(v, (f"fq{par}" if j == 0
                                               else f"oq{par}"),
                                           16 * (il // 2))
                                dst = stgb if j == 0 else stg
                                p.inc(v.tensor_copy(dst[:, par, :],
                                                    pM[par][:, :]), "p2ev")

            @block.sync
            def _(s):
                if PH >= 5:
                    p.wait(s, "p2ev", p.n["p2ev"])
                    for sl in range(4):
                        p.wait(s, f"fq{sl}", p.n[f"fq{sl}"])
                        p.wait(s, f"oq{sl}", p.n[f"oq{sl}"])
                elif PH == 4:
                    p.wait(s, "wrs0", p.n["wrs0"])
                    p.wait(s, "wrs1", p.n["wrs1"])
                elif PH == 3:
                    p.wait(s, "cg", 4)
                elif PH == 2:
                    for sl in range(4):
                        p.wait(s, f"spl{sl}", p.n[f"spl{sl}"])
                else:
                    p.wait(s, "wxh0", p.n["wxh0"])
                    p.wait(s, "wxh1", p.n["wxh1"])

    return nc


def _prep(inputs, core):
    b, g = core // 2, core % 2
    A = np.ascontiguousarray
    x = np.asarray(inputs["L0_attn_x"])

    def pm_qkv(w):
        wt = np.asarray(w).T[:, g * 1024:(g + 1) * 1024]
        return A(wt.reshape(DT, 128, 1024).transpose(1, 0, 2))

    d = {
        "x_full": A(x[b]),
        "x_mine": A(x[b, g * TC:(g + 1) * TC]),
        "wq_pm": pm_qkv(inputs["L0_attn_wq"]),
        "wk_pm": pm_qkv(inputs["L0_attn_wk"]),
        "wv_pm": pm_qkv(inputs["L0_attn_wv"]),
        "wo_pm": A(np.asarray(inputs["L0_attn_wo"]).T
                   .reshape(DT, 128, 4, 512).transpose(1, 2, 0, 3)),
        "wg_pm": A(np.asarray(inputs["L0_ffn_Wg"]).transpose(1, 0, 2)
                   .reshape(DT, 128, 2, 8).transpose(1, 0, 2, 3)),
        "wup_pm": A(np.asarray(inputs["L0_ffn_Wup"])
                    .reshape(NX, DT, 128, 2, 512).transpose(2, 0, 3, 1, 4)),
        "wsw_pm": A(np.asarray(inputs["L0_ffn_Wsw"])
                    .reshape(NX, ET, 128, 2, 512).transpose(2, 0, 3, 1, 4)),
        "wsu_pm": A(np.asarray(inputs["L0_ffn_Wsu"])
                    .reshape(DT, 128, 4, 512).transpose(1, 2, 0, 3)),
        "wsa_pm": A(np.asarray(inputs["L0_ffn_Wsa"])
                    .reshape(ET, 128, 2, 512).transpose(1, 2, 0, 3)),
        "wdn_pm": A(np.asarray(inputs["L0_ffn_Wdn"])
                    .reshape(NX, ET, 128, DT, 128).transpose(2, 3, 0, 1, 4)),
        "wsd_pm": A(np.asarray(inputs["L0_ffn_Wsd"])
                    .reshape(ST, 128, DT, 128).transpose(1, 2, 0, 3)),
        "bupT": A(np.asarray(inputs["L0_ffn_bup"]).reshape(NX, ET, 128)
                  .transpose(2, 0, 1)),
        "bswT": A(np.asarray(inputs["L0_ffn_bsw"]).reshape(NX, ET, 128)
                  .transpose(2, 0, 1)),
        "bsuT": A(np.asarray(inputs["L0_ffn_bsu"]).reshape(ST, 128).T),
        "bsaT": A(np.asarray(inputs["L0_ffn_bsa"]).reshape(ET, 128).T),
        "bg0": A(np.asarray(inputs["L0_ffn_bg"])[0:1, :]),
        "bg1": A(np.asarray(inputs["L0_ffn_bg"])[1:2, :]),
        "bdn_r": A(np.asarray(inputs["L0_ffn_bdn"])),
        "bsd_r": A(np.asarray(inputs["L0_ffn_bsd"]).reshape(1, D)),
        "c_ident": np.eye(128, dtype=np.float32),
        "c_identb": np.eye(128, dtype=ml_dtypes.bfloat16),
        "c_onesrow": np.ones((1, 512), dtype=ml_dtypes.bfloat16),
        "c_ones": np.ones((128, 128), dtype=ml_dtypes.bfloat16),
    }
    band = (np.arange(128)[:, None] <= (np.arange(896)[None, :] - 384))
    d["c_band"] = band.astype(ml_dtypes.bfloat16)
    sel = np.zeros((8, NX * 128), dtype=ml_dtypes.bfloat16)
    for n in range(NX):
        sel[n, n * 128:(n + 1) * 128] = 1
    d["c_sel"] = sel
    return d


def kernel(**inputs):
    nc = bass.Bass("TRN2", target_bir_lowering=False)
    build(nc)
    in_maps = [_prep(inputs, c) for c in range(8)]
    res = run_bass_kernel_spmd(nc, in_maps, core_ids=list(range(8)))
    out = np.zeros((4, TB, D), dtype=np.float32)
    for c in range(8):
        b, g = c // 2, c % 2
        y = np.asarray(res.results[c]["y_mine"])
        out[b, g * TC:(g + 1) * TC] = (
            y.reshape(128, DT, TC).transpose(2, 1, 0).reshape(TC, D))
    return out


# version 13
# speedup vs baseline: 1.1480x; 1.0879x over previous
import numpy as np
import ml_dtypes
from contextlib import ExitStack
import concourse.bass as bass
import concourse.mybir as mybir
from concourse.bass_utils import run_bass_kernel_spmd
from concourse.expressions import RuntimeValue

F32 = mybir.dt.float32
BF16 = mybir.dt.bfloat16
AF = mybir.ActivationFunctionType
ALU = mybir.AluOpType
AX = mybir.AxisListType
ds = bass.ds

D = 2048
DT = 16
TB = 4096
TTB = 32
TC = 2048
TTC = 16
NHC = 8
KT = 32
E = 1024
ET = 8
NX = 6
G = 512
NG = 4
ST = 16
SCALE = float(1.0 / np.sqrt(128.0))
EPS = 1e-5
RG_PAIRS = [[0, 1], [2, 3], [4, 5], [6, 7]]
PHASES = 99
FFN_GROUPS = 99

SEMS = [
    "b0", "b0v", "p0",
    "ld0", "ld1", "ld2", "lnv", "rs1", "apl", "wxh0", "wxh1",
    "txl0", "txl1", "wqkv", "mm2", "ev2",
    "spl0", "spl1", "spl2", "spl3",
    "hld0", "hld1", "hcons", "aS", "aexp", "amask", "acons", "aev",
    "am0", "am1", "am2", "am3", "cg",
    "gq", "wo4", "mm4", "ev4", "ln4", "rs4", "ap4", "wrs0", "wrs1",
    "xtb0", "xtb1", "wlA0", "wlA1", "wlB0", "wlB1", "wlC0", "wlC1",
    "rsg0", "rsg1",
    "gt", "gtv", "gch", "gex", "gey", "gtw", "gts", "wrp",
    "fh", "fhev", "fz", "fzev", "fhw", "p2", "p2ev",
    "fq0", "fq1", "fq2", "fq3", "oq0", "oq1", "oq2", "oq3",
]


class P:
    def __init__(self, nc, stack, names):
        self.sem = {}
        self.n = {}
        for s in names:
            self.sem[s] = stack.enter_context(nc.semaphore(s))
            self.n[s] = 0

    def inc(self, instr, name, k=1):
        instr.then_inc(self.sem[name], k)
        self.n[name] += k
        return instr

    def wait(self, eng, name, val=None):
        v = self.n[name] if val is None else val
        if v > 0:
            eng.wait_ge(self.sem[name], v)


def build(nc):
    PH = PHASES
    def inp(name, shape, dt=F32):
        return nc.dram_tensor(name, list(shape), dt, kind="ExternalInput").ap()

    x_full = inp("x_full", (TB, D))
    x_mine = inp("x_mine", (TC, D))
    wq_pm = inp("wq_pm", (128, DT, NHC * 128))
    wk_pm = inp("wk_pm", (128, DT, NHC * 128))
    wv_pm = inp("wv_pm", (128, DT, NHC * 128))
    wo_pm = inp("wo_pm", (128, 4, DT, 512))
    wg_pm = inp("wg_pm", (128, DT, 2, 8))
    wup_pm = inp("wup_pm", (128, NX, 2, DT, 512))
    wsw_pm = inp("wsw_pm", (128, NX, 2, ET, 512))
    wsu_pm = inp("wsu_pm", (128, 4, DT, 512))
    wsa_pm = inp("wsa_pm", (128, 2, ET, 512))
    wdn_pm = inp("wdn_pm", (128, DT, NX, ET, 128))
    wsd_pm = inp("wsd_pm", (128, DT, ST, 128))
    bupT = inp("bupT", (128, NX, ET))
    bswT = inp("bswT", (128, NX, ET))
    bsuT = inp("bsuT", (128, ST))
    bsaT = inp("bsaT", (128, ET))
    bg0 = inp("bg0", (1, 8))
    bg1 = inp("bg1", (1, 8))
    bdn_r = inp("bdn_r", (NX, D))
    bsd_r = inp("bsd_r", (1, D))
    c_ident = inp("c_ident", (128, 128))
    c_identb = inp("c_identb", (128, 128), BF16)
    c_onesrow = inp("c_onesrow", (1, 512), BF16)
    c_ones = inp("c_ones", (128, 128), BF16)
    c_band = inp("c_band", (128, 896), BF16)
    c_sel = inp("c_sel", (8, NX * 128), BF16)

    y_mine = nc.dram_tensor("y_mine", [128, DT, TC], F32,
                            kind="ExternalOutput").ap()

    def scr(name, shape, dt):
        return nc.dram_tensor(name, list(shape), dt, kind="Internal").ap()

    xln_tm = scr("xln_tm", (TB, D), BF16)
    kT_sc = scr("kT_sc", (NHC, 128, TB), BF16)
    qT_sc = scr("qT_sc", (NHC, 128, TB), BF16)
    v_sc = scr("v_sc", (NHC, 128, KT, 128), BF16)
    attn_mine = scr("attn_mine", (4, NHC * 128, 1024), BF16)
    attn_full = scr("attn_full", (4, D, 1024), BF16)
    xt0_tm = scr("xt0_tm", (TC, D), BF16)
    rlo_tm = scr("rlo_tm", (TC, D), BF16)
    bhi_tm = scr("bhi_tm", (TC, D), BF16)
    blo_tm = scr("blo_tm", (TC, D), BF16)
    xt1_fm = scr("xt1_fm", (128, NG, DT, G), BF16)
    wsc_up = scr("wsc_up", (128, NX, 2, DT, 512), BF16)
    wsc_sw = scr("wsc_sw", (128, NX, 2, ET, 512), BF16)
    wsc_su = scr("wsc_su", (128, 4, DT, 512), BF16)
    wsc_sa = scr("wsc_sa", (128, 2, ET, 512), BF16)
    wsc_dn = scr("wsc_dn", (128, DT, NX, ET, 128), BF16)
    wsc_sd = scr("wsc_sd", (128, DT, ST, 128), BF16)

    a3 = []
    ACONS_END = {}
    cS = cexp = ccons = cmask = cev = 0
    for q in range(4):
        for h in range(NHC):
            for ci in range(2):
                s = 2 * q + ci
                npair = 2 * (s + 1)
                a3.append(dict(q=q, h=h, ci=ci, s=s, npair=npair,
                               idx=len(a3),
                               Sb=cS, Eb=cexp, Cb=ccons, Mb=cmask, Vb=cev,
                               Ce=ccons + npair))
                cS += npair
                cexp += npair
                ccons += npair
                cmask += 2
                cev += 2
                if ci == 1:
                    ACONS_END[q * NHC + h] = ccons

    with ExitStack() as stk:
        p = P(nc, stk, SEMS)

        def T(name, shape, dt=F32):
            return stk.enter_context(nc.sbuf_tensor(name, list(shape), dt))

        bup_sb = T("bup_sb", (128, NX, ET))
        bsw_sb = T("bsw_sb", (128, NX, ET))
        bsu_sb = T("bsu_sb", (128, ST))
        bsa_sb = T("bsa_sb", (128, ET))
        bgA_sb = T("bgA_sb", (1, 8), BF16)
        bgB_sb = T("bgB_sb", (1, 8), BF16)
        bgr_sb = T("bgr_sb", (128, 2, 8))
        eps_sb = T("eps_sb", (128, 1))
        wg_sb = T("wg_sb", (128, DT, 2, 8), BF16)
        ident_sb = T("ident_sb", (128, 128))
        ones_sb = T("ones_sb", (128, 128), BF16)
        band_sb = T("band_sb", (128, 896), BF16)
        sel_sb = T("sel_sb", (8, NX * 128), BF16)
        identb_sb = T("identb_sb", (128, 128), BF16)
        onesrow_sb = T("onesrow_sb", (1, 512), BF16)

        pBig0 = stk.enter_context(nc.psum_tensor("pBig0", [128, 1024], F32))
        pBig1 = stk.enter_context(nc.psum_tensor("pBig1", [128, 1024], F32))
        pM = [stk.enter_context(nc.psum_tensor(f"pM{i}", [128, 512], F32))
              for i in range(4)]
        PB4 = [pBig0[:, 0:512], pBig0[:, 512:1024],
               pBig1[:, 0:512], pBig1[:, 512:1024]]
        BANK8 = [pM[0][:, :], pM[1][:, :], pM[2][:, :], pM[3][:, :]] + PB4

        with nc.Block() as block:

            @block.tensor
            def _(t):
                p.wait(t, "b0", 208)
                t.matmul(pM[0][:, 0:8], ones_sb[0:1, :], bgA_sb[0:1, :],
                         start=True, stop=True)
                p.inc(t.matmul(pM[1][:, 0:8], ones_sb[0:1, :], bgB_sb[0:1, :],
                               start=True, stop=True), "b0v")

            @block.vector
            def _(v):
                v.memset(eps_sb[:, :], EPS)
                p.wait(v, "b0v", 1)
                v.tensor_copy(bgr_sb[:, 0, :], pM[0][:, 0:8])
                p.inc(v.tensor_copy(bgr_sb[:, 1, :], pM[1][:, 0:8]), "b0v")

            with nc.sbuf_tensor("xlnT", [128, DT, TB], BF16) as xlnT, \
                 nc.sbuf_tensor("a2st", [128, 4, 512], BF16) as a2st, \
                 nc.sbuf_tensor("a1x", [128, 3, D], F32) as a1x, \
                 nc.sbuf_tensor("a1st", [128, 3, 4, 6], F32) as a1st, \
                 nc.sbuf_tensor("a1ag", [128, 3, 2], F32) as a1ag, \
                 nc.sbuf_tensor("a1rs", [128, 3, 2], F32) as a1rs, \
                 nc.sbuf_tensor("a1o", [128, 3, D], BF16) as a1o, \
                 nc.sbuf_tensor("a2w", [128, DT, 1024], BF16) as a2w:
                @block.gpsimd
                def _(g):
                    p.inc(g.dma_start(out=a2w[:, :, :], in_=wk_pm), "wqkv", 16)
                    for dst, src in [
                                (bup_sb[:, :, :], bupT), (bsw_sb[:, :, :], bswT),
                                (bsu_sb[:, :], bsuT), (bsa_sb[:, :], bsaT),
                                (bgA_sb[0:1, :], bg0), (bgB_sb[0:1, :], bg1),
                                (wg_sb[:, :, :, :], wg_pm),
                                (ident_sb[:, :], c_ident), (ones_sb[:, :], c_ones),
                                (identb_sb[:, :], c_identb),
                                (onesrow_sb[:, :], c_onesrow),
                                (band_sb[:, :], c_band), (sel_sb[:, :], c_sel)]:
                        p.inc(g.dma_start(out=dst, in_=src), "b0", 16)
                B0_TOTAL = p.n["b0"]
                assert B0_TOTAL == 208, B0_TOTAL

                @block.sync
                def _(s):
                    for tt in range(TTB):
                        if tt >= 3:
                            p.wait(s, "apl", tt - 2)
                        p.inc(s.dma_start(
                            out=a1x[:, tt % 3, :],
                            in_=x_full[tt * 128:(tt + 1) * 128, :]),
                            f"ld{tt % 3}", 16)

                @block.vector
                def _(v):
                    for tt in range(TTB):
                        b = tt % 3
                        p.wait(v, f"ld{tt % 3}", 16 * (tt // 3 + 1))
                        for i in range(4):
                            ins = v.bn_stats(a1st[:, b, i, :],
                                             a1x[:, b, 512 * i:512 * (i + 1)])
                        p.inc(ins, "lnv")
                        p.wait(v, "lnv", p.n["lnv"])
                        if tt >= 3:
                            p.wait(v, "rs1", 4 * (tt - 2))
                        p.inc(v.bn_aggr(a1ag[:, b, :], a1st[:, b, :, :]),
                              "lnv")

                @block.scalar
                def _(sc):
                    p.wait(sc, "b0v", 2)
                    for tt in range(TTB):
                        b = tt % 3
                        p.wait(sc, "lnv", 2 * (tt + 1))
                        if tt >= 3:
                            k = tt - 3
                            hh = (k + 1) // 16
                            if hh >= 1:
                                p.wait(sc, "wxh0", 256)
                            p.wait(sc, f"wxh{hh}",
                                   16 * min(k + 2 - 16 * hh, 16))
                        p.inc(sc.activation(a1rs[:, b, 0:1],
                                            a1ag[:, b, 1:2],
                                            AF.Ln, bias=eps_sb[:, 0:1]),
                              "rs1")
                        p.wait(sc, "rs1", p.n["rs1"])
                        p.inc(sc.activation(a1rs[:, b, 1:2],
                                            a1rs[:, b, 0:1],
                                            AF.Exp, scale=-0.5), "rs1")
                        p.wait(sc, "rs1", p.n["rs1"])
                        p.inc(sc.mul(a1rs[:, b, 0:1], a1ag[:, b, 0:1],
                                     a1rs[:, b, 1:2]), "rs1")
                        p.wait(sc, "rs1", p.n["rs1"])
                        p.inc(sc.mul(a1rs[:, b, 0:1], a1rs[:, b, 0:1],
                                     -1.0), "rs1")
                        p.wait(sc, "rs1", p.n["rs1"])
                        p.inc(sc.activation(a1o[:, b, :], a1x[:, b, :],
                                            AF.Identity,
                                            bias=a1rs[:, b, 0:1],
                                            scale=a1rs[:, b, 1:2]), "apl")
                        p.wait(sc, "apl", tt + 1)
                        p.inc(sc.dma_start(
                            out=xln_tm[tt * 128:(tt + 1) * 128, :],
                            in_=a1o[:, b, :]), f"wxh{tt // 16}", 16)

                @block.sync
                def _(s):
                    if PH < 2:
                        return
                    for th in range(2):
                        p.wait(s, f"wxh{th}", 16 * 16)
                        for dt in range(DT):
                            p.inc(s.dma_start(
                                out=xlnT[:, dt, th * 2048:(th + 1) * 2048],
                                in_=xln_tm[th * 2048:(th + 1) * 2048,
                                           dt * 128:(dt + 1) * 128],
                                transpose=True), f"txl{th}", 16)

                @block.gpsimd
                def _(g):
                    if PH < 2:
                        return
                    p.wait(g, "mm2", 64)
                    p.inc(g.dma_start(out=a2w[:, :, :], in_=wv_pm), "wqkv", 16)
                    p.wait(g, "mm2", 128)
                    p.inc(g.dma_start(out=a2w[:, :, :], in_=wq_pm), "wqkv", 16)
                    for n in range(NX):
                        p.inc(g.dma_start(out=wsc_up[:, n], in_=wup_pm[:, n]),
                                  "p0", 16)
                        p.inc(g.dma_start(out=wsc_sw[:, n], in_=wsw_pm[:, n]),
                                  "p0", 16)
                    p.inc(g.dma_start(out=wsc_su[:, :], in_=wsu_pm), "p0", 16)
                    p.inc(g.dma_start(out=wsc_sa[:, :], in_=wsa_pm), "p0", 16)
                    for dt2 in range(DT):
                        p.inc(g.dma_start(out=wsc_dn[:, dt2],
                                          in_=wdn_pm[:, dt2]), "p0", 16)
                    p.inc(g.dma_start(out=wsc_sd[:, :], in_=wsd_pm), "p0", 16)
                P0_TOTAL = p.n["p0"]

                @block.tensor
                def _(t):
                    if PH < 2:
                        return
                    mm = 0
                    p.wait(t, "wqkv", 16)
                    p.wait(t, "b0v", 2)
                    for ot in range(NHC):
                        for tc in range(8):
                            p.wait(t, f"txl{tc // 4}", 16 * 16)
                            if mm >= 4:
                                p.wait(t, "ev2", mm - 3)
                            for dt in range(DT):
                                ins = t.matmul(
                                    pM[mm % 4][:, :],
                                    a2w[:, dt, ot * 128:(ot + 1) * 128],
                                    xlnT[:, dt, tc * 512:(tc + 1) * 512],
                                    start=(dt == 0), stop=(dt == DT - 1))
                            p.inc(ins, "mm2")
                            mm += 1
                    for tt in range(KT):
                        for oc in range(2):
                            if tt == 0 and oc == 0:
                                p.wait(t, "wqkv", 32)
                                p.wait(t, "txl0", 16 * 16)
                            if tt == 16 and oc == 0:
                                p.wait(t, "txl1", 16 * 16)
                            if mm >= 4:
                                p.wait(t, "ev2", mm - 3)
                            for dt in range(DT):
                                ins = t.matmul(
                                    pM[mm % 4][:, :],
                                    xlnT[:, dt, tt * 128:(tt + 1) * 128],
                                    a2w[:, dt, oc * 512:(oc + 1) * 512],
                                    start=(dt == 0), stop=(dt == DT - 1))
                            p.inc(ins, "mm2")
                            mm += 1
                    p.wait(t, "wqkv", 48)
                    for ot in range(NHC):
                        for tc in range(8):
                            if mm >= 4:
                                p.wait(t, "ev2", mm - 3)
                            for dt in range(DT):
                                ins = t.matmul(
                                    pM[mm % 4][:, :],
                                    a2w[:, dt, ot * 128:(ot + 1) * 128],
                                    xlnT[:, dt, tc * 512:(tc + 1) * 512],
                                    start=(dt == 0), stop=(dt == DT - 1))
                            p.inc(ins, "mm2")
                            mm += 1

                @block.scalar
                def _(sc):
                    if PH < 2:
                        return
                    for i in range(192):
                        p.wait(sc, "mm2", i + 1)
                        if i >= 4:
                            p.wait(sc, f"spl{i % 4}", 16 * (i // 4))
                        p.inc(sc.copy(a2st[:, i % 4, :], pM[i % 4][:, :]),
                              "ev2")
                        p.wait(sc, "ev2", p.n["ev2"])
                        if i < 64:
                            ot, tc = divmod(i, 8)
                            ins = sc.dma_start(
                                out=kT_sc[ot, :, tc * 512:(tc + 1) * 512],
                                in_=a2st[:, i % 4, :])
                        elif i < 128:
                            tt, oc = divmod(i - 64, 2)
                            ins = sc.dma_start(
                                out=v_sc[oc * 4:(oc + 1) * 4, :, tt, :]
                                .rearrange("hh pp j -> pp hh j"),
                                in_=a2st[:, i % 4, :].rearrange(
                                    "pp (hh j) -> pp hh j", hh=4))
                        else:
                            ot, tc = divmod(i - 128, 8)
                            ins = sc.dma_start(
                                out=qT_sc[ot, :, tc * 512:(tc + 1) * 512],
                                in_=a2st[:, i % 4, :])
                        p.inc(ins, f"spl{i % 4}", 16)
            A2_MM_TOTAL = p.n["mm2"]

            with nc.sbuf_tensor("kbuf", [128, 2, TB], BF16) as kbuf, \
                 nc.sbuf_tensor("vbuf", [128, 2, KT, 128], BF16) as vbuf, \
                 nc.sbuf_tensor("qbuf", [128, 2, 1024], BF16) as qbuf, \
                 nc.sbuf_tensor("abuf", [128, 2, 1024], BF16) as abuf, \
                 nc.sbuf_tensor("rbuf", [128, 512], F32) as rbuf, \
                 nc.sbuf_tensor("attn", [128, NHC, TB], BF16) as attn:

                @block.sync
                def _(s):
                    if PH < 3:
                        return
                    for sl in range(4):
                        p.wait(s, f"spl{sl}", 16 * 48)
                    for q in range(4):
                        for h in range(NHC):
                            gh = q * NHC + h
                            if gh >= 2:
                                p.wait(s, "acons", ACONS_END[gh - 2])
                            p.inc(s.dma_start(out=kbuf[:, gh % 2, :],
                                              in_=kT_sc[h]), f"hld{gh % 2}", 16)
                            p.inc(s.dma_start(out=vbuf[:, gh % 2, :, :],
                                              in_=v_sc[h]), f"hld{gh % 2}", 16)
                            p.inc(s.dma_start(
                                out=qbuf[:, gh % 2, :],
                                in_=qT_sc[h, :, q * 1024:(q + 1) * 1024]),
                                f"hld{gh % 2}", 16)

                @block.tensor
                def _(t):
                    if PH < 3:
                        return
                    p.wait(t, "ev2", 192)
                    for ck in a3:
                        gh = ck["q"] * NHC + ck["h"]
                        b = gh % 2
                        qsl = slice(ck["ci"] * 512, (ck["ci"] + 1) * 512)
                        p.wait(t, f"hld{gh % 2}", 48 * (gh // 2 + 1))
                        npair, s = ck["npair"], ck["s"]

                        cpar = 2 * (ck["idx"] % 2)

                        def consume(pp):
                            first, last = pp == 0, pp == npair - 1
                            p.wait(t, "aexp", ck["Eb"] + pp + 1)
                            if pp >= 2 * s:
                                p.wait(t, "amask",
                                       ck["Mb"] + (pp - 2 * s) + 1)
                            if first:
                                p.wait(t, "aev", ck["Vb"] - 2)
                            for u in range(2):
                                kt = 2 * pp + u
                                asl = slice(u * 512, (u + 1) * 512)
                                t.matmul(pM[cpar][:, :], ones_sb[:, :],
                                         abuf[:, pp % 2, asl],
                                         start=(first and u == 0),
                                         stop=(last and u == 1))
                                ins = t.matmul(pM[cpar + 1][:, :],
                                               vbuf[:, b, kt, :],
                                               abuf[:, pp % 2, asl],
                                               start=(first and u == 0),
                                               stop=(last and u == 1))
                            p.inc(ins, "acons")

                        for pp in range(npair):
                            if pp >= 2:
                                p.wait(t, "aexp", ck["Eb"] + pp - 1)
                            for u in range(2):
                                kt = 2 * pp + u
                                pS = pBig0 if pp % 2 == 0 else pBig1
                                ins = t.matmul(
                                    pS[:, u * 512:(u + 1) * 512],
                                    kbuf[:, b, kt * 128:(kt + 1) * 128],
                                    qbuf[:, b, qsl], start=True, stop=True)
                            p.inc(ins, "aS")
                            if pp >= 1:
                                consume(pp - 1)
                        consume(npair - 1)

                @block.scalar
                def _(sc):
                    if PH < 3:
                        return
                    for ck in a3:
                        for pp in range(ck["npair"]):
                            pS = pBig0 if pp % 2 == 0 else pBig1
                            p.wait(sc, "aS", ck["Sb"] + pp + 1)
                            p.wait(sc, "acons", ck["Cb"] + pp - 1)
                            p.inc(sc.activation(abuf[:, pp % 2, :], pS[:, :],
                                                AF.Exp, scale=SCALE), "aexp")

                @block.vector
                def _(v):
                    if PH < 3:
                        return
                    for ck in a3:
                        s, npair = ck["s"], ck["npair"]
                        for pp in (2 * s, 2 * s + 1):
                            p.wait(v, "aexp", ck["Eb"] + pp + 1)
                            for u in range(2):
                                i = 2 * (pp - 2 * s) + u
                                off = 384 - 128 * i
                                asl = slice(u * 512, (u + 1) * 512)
                                ins = v.tensor_mul(abuf[:, pp % 2, asl],
                                                   abuf[:, pp % 2, asl],
                                                   band_sb[:, off:off + 512])
                            p.inc(ins, "amask")
                        cpar = 2 * (ck["idx"] % 2)
                        p.wait(v, "acons", ck["Cb"] + npair)
                        p.inc(v.reciprocal(rbuf[:, :], pM[cpar][:, :]),
                              "aev")
                        p.wait(v, "aev", p.n["aev"])
                        p.inc(v.tensor_mul(
                            attn[:, ck["h"], s * 512:(s + 1) * 512],
                            pM[cpar + 1][:, :], rbuf[:, :]), "aev")

                @block.scalar
                def _(sc):
                    if PH < 3:
                        return
                    for q in range(4):
                        p.wait(sc, "aev", 32 * (q + 1))
                        for ch in range(2):
                            p.inc(sc.dma_start(
                                out=attn_mine[q, 512 * ch:512 * (ch + 1), :]
                                .rearrange("(ct pp) j -> pp ct j", pp=128),
                                in_=attn[:, 4 * ch:4 * (ch + 1),
                                         q * 1024:(q + 1) * 1024]),
                                f"am{q}", 16)

                @block.gpsimd
                def _(g):
                    if PH < 3:
                        return
                    for q in range(4):
                        p.wait(g, f"am{q}", 32)
                        p.inc(g.collective_compute(
                            "AllGather", ALU.bypass,
                            ins=[attn_mine[q]], outs=[attn_full[q]],
                            replica_groups=RG_PAIRS), "cg", 1)
            A3_CONS = p.n["acons"]

            with nc.sbuf_tensor("gqb", [128, DT, 1024], BF16) as gqb, \
                 nc.sbuf_tensor("wob", [128, 4, DT, 512], BF16) as wob, \
                 nc.sbuf_tensor("a4x", [128, 2, D], F32) as a4x, \
                 nc.sbuf_tensor("x1b", [128, 2, D], F32) as x1b, \
                 nc.sbuf_tensor("a4st", [128, 2, 4, 6], F32) as a4st, \
                 nc.sbuf_tensor("a4ag", [128, 2, 2], F32) as a4ag, \
                 nc.sbuf_tensor("a4rs", [128, 2, 2], F32) as a4rs, \
                 nc.sbuf_tensor("resb", [128, 2, D], F32) as resb, \
                 nc.sbuf_tensor("xt0b", [128, 2, D], BF16) as xt0b, \
                 nc.sbuf_tensor("rlob", [128, 2, D], BF16) as rlob, \
                 nc.sbuf_tensor("bhib", [128, 2, D], BF16) as bhib, \
                 nc.sbuf_tensor("blob", [128, 2, D], BF16) as blob:

                @block.sync
                def _(s):
                    if PH < 4:
                        return
                    pid = s.partition_id()
                    with s.register("rq") as rq, s.register("rqi0") as rqi0, \
                            s.register("rqi1") as rqi1:
                        s.reg_alu(rq, pid, 1, ALU.bitwise_and)
                        s.reg_mul(rq, rq, 2)
                        s.reg_alu(rqi0, rq, 0, ALU.add)
                        s.reg_alu(rqi1, rq, 1, ALU.add)
                        p.wait(s, "cg", 4)
                        p.wait(s, "acons", A3_CONS)
                        def gq_load(rr):
                            for ch in range(2):
                                p.inc(s.dma_start(
                                    out=gqb[:, 8 * ch:8 * (ch + 1), :],
                                    in_=attn_full[ds(RuntimeValue(rr, 0, 3), 1),
                                                  1024 * ch:1024 * (ch + 1),
                                                  :].rearrange(
                                        "one (ct pp) j -> pp (one ct) j",
                                        pp=128)), "gq", 16)
                        gq_load(rqi0)
                        for lt in range(TTC):
                            if lt >= 2:
                                p.wait(s, "ap4", 6 * (lt - 1))
                            p.inc(s.dma_start(
                                out=a4x[:, lt % 2, :],
                                in_=x_mine[lt * 128:(lt + 1) * 128, :]),
                                f"ld{lt % 2}", 16)
                            if lt == 7:
                                p.wait(s, "mm4", 32)
                                gq_load(rqi1)

                @block.gpsimd
                def _(g):
                    if PH < 4:
                        return
                    p.wait(g, "acons", A3_CONS)
                    for qq in range(4):
                        p.wait(g, f"am{qq}", 32)
                    p.inc(g.dma_start(out=wob[:, :], in_=wo_pm), "wo4", 16)

                @block.tensor
                def _(t):
                    if PH < 4:
                        return
                    p.wait(t, "wo4", 16)
                    for lt in range(TTC):
                        qi, ltq = divmod(lt, 8)
                        p.wait(t, "gq", 32 * (qi + 1))
                        for dc in range(4):
                            i = lt * 4 + dc
                            if i >= 2:
                                p.wait(t, "ev4", i - 1)
                            for ct in range(DT):
                                ins = t.matmul(
                                    pM[i % 2][:, :],
                                    gqb[:, ct, ltq * 128:(ltq + 1) * 128],
                                    wob[:, dc, ct, :],
                                    start=(ct == 0), stop=(ct == DT - 1))
                            p.inc(ins, "mm4")

                def a4_apply(v, m):
                    b = m % 2
                    if m >= 2:
                        p.wait(v, f"wrs{m % 2}", 64 * (m // 2))
                    p.wait(v, "rs4", 2 * (m + 1))
                    p.inc(v.tensor_scalar(resb[:, b, :], x1b[:, b, :],
                                          a4ag[:, b, 0:1], a4rs[:, b, 1:2],
                                          ALU.subtract, ALU.mult), "ap4")
                    p.wait(v, "ap4", p.n["ap4"])
                    p.inc(v.tensor_copy(xt0b[:, b, :], resb[:, b, :]), "ap4")
                    p.wait(v, "ap4", p.n["ap4"])
                    p.inc(v.tensor_sub(rlob[:, b, :], resb[:, b, :],
                                       xt0b[:, b, :]), "ap4")
                    p.inc(v.tensor_add(bhib[:, b, :], x1b[:, b, :],
                                       resb[:, b, :]), "ap4")
                    p.wait(v, "ap4", p.n["ap4"])
                    p.inc(v.tensor_sub(a4x[:, b, :], x1b[:, b, :],
                                       bhib[:, b, :]), "ap4")
                    p.wait(v, "ap4", p.n["ap4"])
                    p.inc(v.tensor_add(blob[:, b, :], a4x[:, b, :],
                                       resb[:, b, :]), "ap4")

                @block.vector
                def _(v):
                    if PH < 4:
                        return
                    for lt in range(TTC):
                        b = lt % 2
                        p.wait(v, f"ld{lt % 2}", 16 * (11 + lt // 2 + 1))
                        for dc in range(4):
                            i = lt * 4 + dc
                            p.wait(v, "mm4", i + 1)
                            p.inc(v.scalar_tensor_tensor(
                                x1b[:, b, dc * 512:(dc + 1) * 512],
                                pM[i % 2][:, :], 1.0,
                                a4x[:, b, dc * 512:(dc + 1) * 512],
                                ALU.mult, ALU.add), "ev4")
                        p.wait(v, "ev4", p.n["ev4"])
                        for i in range(4):
                            ins = v.bn_stats(a4st[:, b, i, :],
                                             x1b[:, b, 512 * i:512 * (i + 1)])
                        p.inc(ins, "ln4")
                        p.wait(v, "ln4", p.n["ln4"])
                        if lt >= 2:
                            p.wait(v, "rs4", 2 * (lt - 1))
                        p.inc(v.bn_aggr(a4ag[:, b, :], a4st[:, b, :, :]),
                              "ln4")
                        if lt >= 1:
                            a4_apply(v, lt - 1)
                    a4_apply(v, TTC - 1)

                @block.scalar
                def _(sc):
                    if PH < 4:
                        return
                    for lt in range(TTC):
                        b = lt % 2
                        p.wait(sc, "ln4", 2 * (lt + 1))
                        p.inc(sc.activation(a4rs[:, b, 0:1], a4ag[:, b, 1:2],
                                            AF.Ln, bias=eps_sb[:, 0:1]), "rs4")
                        p.wait(sc, "rs4", p.n["rs4"])
                        p.inc(sc.activation(a4rs[:, b, 1:2], a4rs[:, b, 0:1],
                                            AF.Exp, scale=-0.5), "rs4")
                        rows = slice(lt * 128, (lt + 1) * 128)
                        p.wait(sc, "ap4", 6 * (lt + 1))
                        p.inc(sc.dma_start(out=xt0_tm[rows, :],
                                           in_=xt0b[:, b, :]), f"wrs{b}", 16)
                        p.inc(sc.dma_start(out=rlo_tm[rows, :],
                                           in_=rlob[:, b, :]), f"wrs{b}", 16)
                        p.inc(sc.dma_start(out=bhi_tm[rows, :],
                                           in_=bhib[:, b, :]), f"wrs{b}", 16)
                        p.inc(sc.dma_start(out=blo_tm[rows, :],
                                           in_=blob[:, b, :]), f"wrs{b}", 16)
            A4_MM_TOTAL = p.n["mm4"]

            with ExitStack() as fstk:
                xtT = fstk.enter_context(nc.sbuf_tensor("xtT", [128, 2, DT, G], BF16))
                hbuf = fstk.enter_context(nc.sbuf_tensor("hbuf", [128, 2, ET, G], BF16))
                h2w = fstk.enter_context(nc.sbuf_tensor("h2w", [128, ET, 8, G], BF16))
                warA = fstk.enter_context(nc.sbuf_tensor("warA", [128, 2, DT, 512], BF16))
                warB = fstk.enter_context(nc.sbuf_tensor("warB", [128, 2, ET, 512], BF16))
                wsdb = fstk.enter_context(nc.sbuf_tensor("wsdb", [128, 2, ST, 128], BF16))
                wrep = fstk.enter_context(nc.sbuf_tensor("wrep", [128, NX, 512], BF16))
                rlg = fstk.enter_context(nc.sbuf_tensor("rlg", [128, 2, 512], BF16))
                bhg = fstk.enter_context(nc.sbuf_tensor("bhg", [128, 2, 512], BF16))
                stg = fstk.enter_context(nc.sbuf_tensor("stg", [128, 2, 512], F32))
                stgf = fstk.enter_context(nc.sbuf_tensor("stgf", [128, 2, 512], F32))
                stgb = fstk.enter_context(nc.sbuf_tensor("stgb", [128, 2, 512], BF16))
                gw = fstk.enter_context(nc.sbuf_tensor("gw", [128, 4, 8], F32))
                gmx = fstk.enter_context(nc.sbuf_tensor("gmx", [128, 4, 8], F32))
                gcmp = fstk.enter_context(nc.sbuf_tensor("gcmp", [128, 4, 8], F32))
                gnm = fstk.enter_context(nc.sbuf_tensor("gnm", [128, 4, 2], F32))
                wT8 = fstk.enter_context(nc.sbuf_tensor("wT8", [8, G], BF16))
                bdn_sb = fstk.enter_context(nc.sbuf_tensor("bdn_sb", [NX, D], BF16))
                bsd_sb = fstk.enter_context(nc.sbuf_tensor("bsd_sb", [1, D], BF16))

                EXPD = {}

                @block.gpsimd
                def _(g):
                    if PH < 5:
                        return
                    p.wait(g, "mm4", A4_MM_TOTAL)
                    p.wait(g, "ap4", 6 * TTC)
                    p.wait(g, "wrs0", 64 * 8)
                    p.wait(g, "wrs1", 64 * 8)
                    p.inc(g.dma_start(out=bdn_sb[:, :], in_=bdn_r), "b0", 16)
                    p.inc(g.dma_start(out=bsd_sb[:, :], in_=bsd_r), "b0", 16)
                B0F = p.n["b0"]

                def emit_h(t, gi, n):
                    for et in range(ET):
                        u = n * ET + et
                        i1 = gi * 16 + 2 * n + et // 4
                        p.wait(t, f"wlA{i1 % 2}", 16 * (i1 // 2 + 1))
                        p.wait(t, "fhev", 64 * gi + u - 3)
                        if n == 0 and et == 0:
                            p.wait(t, "gtv", 4 * (gi + 1))
                        if n == 0 and et == 2:
                            p.wait(t, "gts", gi + 1)
                        for dt in range(DT):
                            ins = t.matmul(
                                PB4[u % 4],
                                warA[:, i1 % 2, dt,
                                     (et % 4) * 128:(et % 4) * 128 + 128],
                                xtT[:, gi % 2, dt, :],
                                start=(dt == 0), stop=(dt == DT - 1))
                        p.inc(ins, "fh")

                def emit_s(t, gi):
                    for ot in range(ST):
                        u = 48 + ot
                        i1 = gi * 16 + 12 + ot // 4
                        p.wait(t, f"wlA{i1 % 2}", 16 * (i1 // 2 + 1))
                        p.wait(t, "fhev", 64 * gi + u - 3)
                        for dt in range(DT):
                            ins = t.matmul(
                                PB4[u % 4],
                                warA[:, i1 % 2, dt,
                                     (ot % 4) * 128:(ot % 4) * 128 + 128],
                                xtT[:, gi % 2, dt, :],
                                start=(dt == 0), stop=(dt == DT - 1))
                        p.inc(ins, "fh")

                def emit_z(t, gi, n):
                    for ft in range(ET):
                        u = n * ET + ft
                        i2 = gi * 14 + 2 * n + ft // 4
                        p.wait(t, f"wlB{i2 % 2}", 16 * (i2 // 2 + 1))
                        p.wait(t, "fhev", 64 * gi + ET * (n + 1))
                        p.wait(t, "fzev", 64 * gi + u - 3)
                        if u < 4:
                            p.wait(t, "wrp", 6 * (gi + 1))
                        for et in range(ET):
                            ins = t.matmul(
                                pM[u % 4][:, :],
                                warB[:, i2 % 2, et,
                                     (ft % 4) * 128:(ft % 4) * 128 + 128],
                                hbuf[:, n % 2, et, :],
                                start=(et == 0), stop=(et == ET - 1))
                        p.inc(ins, "fz")

                def emit_zs(t, gi):
                    for k in range(2):
                        for ft in range(ET):
                            u = 48 + k * ET + ft
                            i2 = gi * 14 + 12 + ft // 4
                            p.wait(t, f"wlB{i2 % 2}", 16 * (i2 // 2 + 1))
                            p.wait(t, "fhev", 64 * gi + 48 + ET * (k + 1))
                            p.wait(t, "fzev", 64 * gi + u - 3)
                            for et in range(ET):
                                ins = t.matmul(
                                    pM[u % 4][:, :],
                                    warB[:, i2 % 2, et,
                                         (ft % 4) * 128:(ft % 4) * 128 + 128],
                                    hbuf[:, k, et, :],
                                    start=(et == 0), stop=(et == ET - 1))
                            p.inc(ins, "fz")

                for j in range(2 if PH >= 5 else 0):
                    for grp in range(NG):
                        gi = j * NG + grp
                        if gi >= FFN_GROUPS:
                            continue
                        src_tm = xt0_tm
                        rows = slice(grp * G, (grp + 1) * G)

                        @block.sync
                        def _(s, j=j, grp=grp, gi=gi, src_tm=src_tm,
                              rows=rows):
                            if j == 1:
                                for sl in range(2):
                                    p.wait(s, f"fq{sl}", 16 * 8 * (grp + 1))
                            if gi == 0:
                                p.wait(s, "mm4", A4_MM_TOTAL)
                                p.wait(s, "wrs0", 64 * 8)
                                p.wait(s, "wrs1", 64 * 8)
                                p.wait(s, "ap4", 6 * TTC)
                                p.wait(s, "b0v", 2)
                                p.wait(s, "p0", P0_TOTAL)
                            if gi >= 2:
                                p.wait(s, "fhw", 112 * (gi - 1))
                            if j == 0:
                                for dt in range(DT):
                                    p.inc(s.dma_start(
                                        out=xtT[:, gi % 2, dt, :],
                                        in_=src_tm[rows,
                                                   dt * 128:(dt + 1) * 128],
                                        transpose=True),
                                        f"xtb{gi % 2}", 16)
                            else:
                                p.inc(s.dma_start(
                                    out=xtT[:, gi % 2],
                                    in_=xt1_fm[:, grp]),
                                    f"xtb{gi % 2}", 16)
                            EXPD[("xtb", gi)] = p.n[f"xtb{gi % 2}"]
                            lorder = []
                            for n in range(NX):
                                lorder += [("wl1", 2 * n), ("wl1", 2 * n + 1)]
                                if n >= 1:
                                    lorder += [("wl2", 2 * n - 2),
                                               ("wl2", 2 * n - 1)]
                            lorder += [("wl2", 10), ("wl2", 11),
                                       ("wl1", 12), ("wl1", 13),
                                       ("wl1", 14), ("wl1", 15),
                                       ("wl2", 12), ("wl2", 13)]
                            p.wait(s, "p2", 16 * gi)
                            for kind, li in lorder:
                                if kind == "wl1":
                                    i1 = gi * 16 + li
                                    if i1 >= 2:
                                        pg, pl = divmod(i1 - 2, 16)
                                        p.wait(s, "fh", 64 * pg + 4 * pl + 4)
                                    if li < 12:
                                        n, eh = divmod(li, 2)
                                        src = wsc_up[:, n, eh]
                                    else:
                                        src = wsc_su[:, li - 12]
                                    p.inc(s.dma_start(out=warA[:, i1 % 2],
                                                      in_=src),
                                          f"wlA{i1 % 2}", 16)
                                else:
                                    i2 = gi * 14 + li
                                    if i2 >= 2:
                                        pg, pl = divmod(i2 - 2, 14)
                                        after = (4 * pl + 4 if pl < 12
                                                 else (60 if pl == 12 else 64))
                                        p.wait(s, "fz", 64 * pg + after)
                                    if li < 12:
                                        n, fh = divmod(li, 2)
                                        src = wsc_sw[:, n, fh]
                                    else:
                                        src = wsc_sa[:, li - 12]
                                    p.inc(s.dma_start(out=warB[:, i2 % 2],
                                                      in_=src),
                                          f"wlB{i2 % 2}", 16)
                            for dt2 in range(DT):
                                i3 = gi * DT + dt2
                                par = dt2 % 2
                                if dt2 == 0:
                                    p.wait(s, "fh", 64 * (gi + 1))
                                    p.wait(s, "fz", 64 * (gi + 1))
                                if i3 >= 2:
                                    p.wait(s, "p2", i3 - 1)
                                p.inc(s.dma_start(
                                    out=warA[:, par].rearrange(
                                        "pp a c -> pp (a c)")[:, 0:6144],
                                    in_=wsc_dn[:, dt2].rearrange(
                                        "pp a b c -> pp (a b c)")),
                                    f"wlC{par}", 16)
                                p.inc(s.dma_start(
                                    out=wsdb[:, par],
                                    in_=wsc_sd[:, dt2]),
                                    f"wlC{par}", 16)
                                EXPD[("wlC", gi, dt2)] = p.n[f"wlC{par}"]
                                if i3 >= 2:
                                    p.wait(s, "p2ev", 2 * (i3 - 1))
                                dsl = slice(dt2 * 128, (dt2 + 1) * 128)
                                if j == 0:
                                    p.inc(s.dma_start(
                                        out=rlg[:, par, :],
                                        in_=rlo_tm[rows, dsl],
                                        transpose=True), f"rsg{par}", 16)
                                else:
                                    p.inc(s.dma_start(
                                        out=bhg[:, par, :],
                                        in_=bhi_tm[rows, dsl],
                                        transpose=True), f"rsg{par}", 16)
                                    p.inc(s.dma_start(
                                        out=rlg[:, par, :],
                                        in_=blo_tm[rows, dsl],
                                        transpose=True), f"rsg{par}", 16)
                                EXPD[("rsg", gi, dt2)] = p.n[f"rsg{par}"]

                        @block.tensor
                        def _(t, j=j, grp=grp, gi=gi):
                            b = gi % 2
                            p.wait(t, f"xtb{gi % 2}", EXPD[("xtb", gi)])
                            p.wait(t, "p2ev", 32 * gi)
                            if gi == 0:
                                p.wait(t, "ev4", 64)
                            for tt in range(4):
                                p.wait(t, "gtv", 4 * gi + tt)
                                if gi == 0 and tt == 0:
                                    p.wait(t, "b0v", 2)
                                for dt in range(DT):
                                    ins = t.matmul(
                                        pBig0[:, 0:8],
                                        xtT[:, b, dt,
                                            tt * 128:(tt + 1) * 128],
                                        wg_sb[:, dt, j, :],
                                        start=(dt == 0),
                                        stop=(dt == DT - 1))
                                p.inc(ins, "gt")
                            for tt in range(4):
                                p.wait(t, "gtw", 4 * gi + tt + 1)
                                p.inc(t.transpose(
                                    pBig1[0:8, tt * 128:(tt + 1) * 128],
                                    gw[:, tt, :], ident_sb[:, :]), "gt")
                            p.wait(t, "gts", gi + 1)
                            for n in range(NX):
                                if 6 * gi + n >= 4:
                                    p.wait(t, "wrp", 6 * gi + n - 3)
                                p.inc(t.matmul(
                                    pM[n % 4][:, :],
                                    sel_sb[:, n * 128:(n + 1) * 128],
                                    wT8[0:8, :], start=True, stop=True),
                                    "gt")
                            emit_h(t, gi, 0)
                            for n in range(1, NX):
                                emit_h(t, gi, n)
                                emit_z(t, gi, n - 1)
                            emit_z(t, gi, NX - 1)
                            emit_s(t, gi)
                            emit_zs(t, gi)
                            p.wait(t, "fhw", 112 * (gi + 1))
                            if gi == 0:
                                p.wait(t, "b0", B0F)
                            for dt2 in range(DT):
                                i3 = gi * DT + dt2
                                par = dt2 % 2
                                p.wait(t, f"wlC{par}",
                                       EXPD[("wlC", gi, dt2)])
                                if i3 >= 2:
                                    p.wait(t, "p2ev", 2 * (i3 - 1))
                                bank = pM[par][:, :]
                                wAf = warA[:, par].rearrange(
                                    "pp a c -> pp (a c)")
                                first2 = True
                                for n in range(NX):
                                    for et in range(ET):
                                        idx = (n * ET + et) * 128
                                        t.matmul(bank,
                                                 wAf[:, idx:idx + 128],
                                                 h2w[:, et, n, :],
                                                 start=first2, stop=False)
                                        first2 = False
                                for st in range(ST):
                                    k, ft = divmod(st, ET)
                                    t.matmul(bank, wsdb[:, par, st, :],
                                             h2w[:, ft, 6 + k, :],
                                             start=False, stop=False)
                                dsl = slice(dt2 * 128, (dt2 + 1) * 128)
                                t.matmul(bank, bdn_sb[0:6, dsl],
                                         wT8[0:6, :],
                                         start=False, stop=False)
                                ins = t.matmul(bank, bsd_sb[0:1, dsl],
                                               onesrow_sb[0:1, :],
                                               start=False, stop=True)
                                p.inc(ins, "p2")

                        @block.scalar
                        def _(sc, j=j, grp=grp, gi=gi):
                            for tt in range(4):
                                p.wait(sc, "gex", 4 * gi + tt + 1)
                                p.inc(sc.activation(
                                    gmx[:, tt, :], gw[:, tt, :], AF.Exp,
                                    bias=gnm[:, tt, 0:1]), "gey")

                            def hev(n):
                                for et in range(ET):
                                    u = n * ET + et
                                    p.wait(sc, "fh", 64 * gi + u + 1)
                                    if u == 0:
                                        p.wait(sc, "fhw", 112 * gi)
                                    if n >= 2 and et == 0:
                                        p.wait(sc, "fz",
                                               64 * gi + ET * (n - 1))
                                        p.wait(sc, "fhw",
                                               112 * gi + 16 * (n - 1))
                                    p.inc(sc.activation(
                                        hbuf[:, n % 2, et, :], PB4[u % 4],
                                        AF.Identity,
                                        bias=bup_sb[:, n, et:et + 1]),
                                        "fhev")

                            def sev(half):
                                for ot in range(8 * half, 8 * half + 8):
                                    u = 48 + ot
                                    p.wait(sc, "fh", 64 * gi + u + 1)
                                    if ot == 0:
                                        p.wait(sc, "fz", 64 * gi + 40)
                                        p.wait(sc, "fhw", 112 * gi + 80)
                                    if ot == 8:
                                        p.wait(sc, "fz", 64 * gi + 48)
                                        p.wait(sc, "fhw", 112 * gi + 96)
                                    p.inc(sc.activation(
                                        hbuf[:, ot // 8, ot % 8, :],
                                        PB4[u % 4], AF.Identity,
                                        bias=bsu_sb[:, ot:ot + 1]), "fhev")

                            def zev(n):
                                for ft in range(ET):
                                    u = n * ET + ft
                                    p.wait(sc, "fz", 64 * gi + u + 1)
                                    if u == 0:
                                        p.wait(sc, "p2", 16 * gi)
                                    p.inc(sc.activation(
                                        h2w[:, ft, n, :], pM[u % 4][:, :],
                                        AF.Silu,
                                        bias=bsw_sb[:, n, ft:ft + 1]),
                                        "fzev")

                            def zsev():
                                for k in range(2):
                                    for ft in range(ET):
                                        u = 48 + k * ET + ft
                                        p.wait(sc, "fz", 64 * gi + u + 1)
                                        p.inc(sc.activation(
                                            h2w[:, ft, 6 + k, :],
                                            pM[u % 4][:, :], AF.Silu,
                                            bias=bsa_sb[:, ft:ft + 1]),
                                            "fzev")

                            hev(0)
                            for n in range(1, NX):
                                hev(n)
                                zev(n - 1)
                            zev(NX - 1)
                            sev(0)
                            sev(1)
                            zsev()
                            for dt2 in range(DT):
                                i3 = gi * DT + dt2
                                par = dt2 % 2
                                p.wait(sc, "p2ev", 2 * (i3 + 1))
                                if j == 0:
                                    ins = sc.dma_start(
                                        out=xt1_fm[:, grp, dt2, :],
                                        in_=stgb[:, par, :])
                                    p.inc(ins, f"fq{par}", 16)
                                else:
                                    ins = sc.dma_start(
                                        out=y_mine[:, dt2,
                                                   grp * G:(grp + 1) * G],
                                        in_=stg[:, par, :])
                                    p.inc(ins, f"oq{par}", 16)

                        @block.vector
                        def _(v, j=j, grp=grp, gi=gi):
                            for tt in range(4):
                                p.wait(v, "gt", 14 * gi + tt + 1)
                                p.inc(v.tensor_add(gw[:, tt, :],
                                                   pBig0[:, 0:8],
                                                   bgr_sb[:, j, :]), "gtv")
                                p.wait(v, "gtv", p.n["gtv"])
                                p.inc(v.max(gmx[:, tt, :], gw[:, tt, :]),
                                      "gch")
                                p.wait(v, "gch", p.n["gch"])
                                p.inc(v.tensor_scalar(
                                    gcmp[:, tt, :], gw[:, tt, :],
                                    gmx[:, tt, 5:6], None, ALU.is_ge),
                                    "gch")
                                p.inc(v.tensor_scalar_mul(
                                    gnm[:, tt, 0:1], gmx[:, tt, 0:1], -1.0),
                                    "gex")
                            for tt in range(4):
                                p.wait(v, "gey", 4 * gi + tt + 1)
                                p.wait(v, "gch", p.n["gch"])
                                p.inc(v.tensor_mul(gw[:, tt, :],
                                                   gmx[:, tt, :],
                                                   gcmp[:, tt, :]), "gch")
                                p.wait(v, "gch", p.n["gch"])
                                p.inc(v.reduce_sum(gnm[:, tt, 1:2],
                                                   gw[:, tt, :],
                                                   axis=AX.X), "gch")
                                p.wait(v, "gch", p.n["gch"])
                                p.inc(v.reciprocal(gnm[:, tt, 0:1],
                                                   gnm[:, tt, 1:2]), "gch")
                                p.wait(v, "gch", p.n["gch"])
                                p.inc(v.tensor_scalar_mul(
                                    gw[:, tt, :], gw[:, tt, :],
                                    gnm[:, tt, 0:1]), "gtw")
                            p.wait(v, "gt", 14 * gi + 8)
                            p.inc(v.tensor_copy(wT8[0:8, :],
                                                pBig1[0:8, 0:512]), "gts")
                            for n in range(NX):
                                p.wait(v, "gt", 14 * gi + 8 + n + 1)
                                p.inc(v.tensor_copy(wrep[:, n, :],
                                                    pM[n % 4][:, :]), "wrp")
                            for u in range(64):
                                p.wait(v, "fzev", 64 * gi + u + 1)
                                if u < 48:
                                    n, ft = divmod(u, ET)
                                    p.wait(v, "wrp", 6 * gi + n + 1)
                                    p.inc(v.tensor_mul(
                                        h2w[:, ft, n, :], h2w[:, ft, n, :],
                                        hbuf[:, n % 2, ft, :]), "fhw")
                                    p.wait(v, "fhw", p.n["fhw"])
                                    p.inc(v.tensor_mul(
                                        h2w[:, ft, n, :], h2w[:, ft, n, :],
                                        wrep[:, n, :]), "fhw")
                                else:
                                    k, ft = divmod(u - 48, ET)
                                    p.inc(v.tensor_mul(
                                        h2w[:, ft, 6 + k, :],
                                        h2w[:, ft, 6 + k, :],
                                        hbuf[:, k, ft, :]), "fhw")
                            for dt2 in range(DT):
                                i3 = gi * DT + dt2
                                par = dt2 % 2
                                il = grp * DT + dt2
                                p.wait(v, "p2", i3 + 1)
                                p.wait(v, f"rsg{par}",
                                       EXPD[("rsg", gi, dt2)])
                                if il >= 2:
                                    p.wait(v, (f"fq{par}" if j == 0
                                               else f"oq{par}"),
                                           16 * (il // 2))
                                hi = (xtT[:, gi % 2, dt2, :] if j == 0
                                      else bhg[:, par, :])
                                p.inc(v.tensor_add(stgf[:, par, :],
                                                   pM[par][:, :], hi),
                                      "p2ev")
                                p.wait(v, "p2ev", p.n["p2ev"])
                                dst = stgb if j == 0 else stg
                                p.inc(v.tensor_add(dst[:, par, :],
                                                   stgf[:, par, :],
                                                   rlg[:, par, :]), "p2ev")

            @block.sync
            def _(s):
                if PH >= 5:
                    p.wait(s, "p2ev", p.n["p2ev"])
                    for sl in range(4):
                        p.wait(s, f"fq{sl}", p.n[f"fq{sl}"])
                        p.wait(s, f"oq{sl}", p.n[f"oq{sl}"])
                elif PH == 4:
                    p.wait(s, "wrs0", p.n["wrs0"])
                    p.wait(s, "wrs1", p.n["wrs1"])
                elif PH == 3:
                    p.wait(s, "cg", 4)
                elif PH == 2:
                    for sl in range(4):
                        p.wait(s, f"spl{sl}", p.n[f"spl{sl}"])
                else:
                    p.wait(s, "wxh0", p.n["wxh0"])
                    p.wait(s, "wxh1", p.n["wxh1"])

    return nc


def _prep(inputs, core):
    b, g = core // 2, core % 2
    A = np.ascontiguousarray
    x = np.asarray(inputs["L0_attn_x"])

    def pm_qkv(w):
        wt = np.asarray(w).T[:, g * 1024:(g + 1) * 1024]
        return A(wt.reshape(DT, 128, 1024).transpose(1, 0, 2))

    d = {
        "x_full": A(x[b]),
        "x_mine": A(x[b, g * TC:(g + 1) * TC]),
        "wq_pm": pm_qkv(inputs["L0_attn_wq"]),
        "wk_pm": pm_qkv(inputs["L0_attn_wk"]),
        "wv_pm": pm_qkv(inputs["L0_attn_wv"]),
        "wo_pm": A(np.asarray(inputs["L0_attn_wo"]).T
                   .reshape(DT, 128, 4, 512).transpose(1, 2, 0, 3)),
        "wg_pm": A(np.asarray(inputs["L0_ffn_Wg"]).transpose(1, 0, 2)
                   .reshape(DT, 128, 2, 8).transpose(1, 0, 2, 3)),
        "wup_pm": A(np.asarray(inputs["L0_ffn_Wup"])
                    .reshape(NX, DT, 128, 2, 512).transpose(2, 0, 3, 1, 4)),
        "wsw_pm": A(np.asarray(inputs["L0_ffn_Wsw"])
                    .reshape(NX, ET, 128, 2, 512).transpose(2, 0, 3, 1, 4)),
        "wsu_pm": A(np.asarray(inputs["L0_ffn_Wsu"])
                    .reshape(DT, 128, 4, 512).transpose(1, 2, 0, 3)),
        "wsa_pm": A(np.asarray(inputs["L0_ffn_Wsa"])
                    .reshape(ET, 128, 2, 512).transpose(1, 2, 0, 3)),
        "wdn_pm": A(np.asarray(inputs["L0_ffn_Wdn"])
                    .reshape(NX, ET, 128, DT, 128).transpose(2, 3, 0, 1, 4)),
        "wsd_pm": A(np.asarray(inputs["L0_ffn_Wsd"])
                    .reshape(ST, 128, DT, 128).transpose(1, 2, 0, 3)),
        "bupT": A(np.asarray(inputs["L0_ffn_bup"]).reshape(NX, ET, 128)
                  .transpose(2, 0, 1)),
        "bswT": A(np.asarray(inputs["L0_ffn_bsw"]).reshape(NX, ET, 128)
                  .transpose(2, 0, 1)),
        "bsuT": A(np.asarray(inputs["L0_ffn_bsu"]).reshape(ST, 128).T),
        "bsaT": A(np.asarray(inputs["L0_ffn_bsa"]).reshape(ET, 128).T),
        "bg0": A(np.asarray(inputs["L0_ffn_bg"])[0:1, :]),
        "bg1": A(np.asarray(inputs["L0_ffn_bg"])[1:2, :]),
        "bdn_r": A(np.asarray(inputs["L0_ffn_bdn"])),
        "bsd_r": A(np.asarray(inputs["L0_ffn_bsd"]).reshape(1, D)),
        "c_ident": np.eye(128, dtype=np.float32),
        "c_identb": np.eye(128, dtype=ml_dtypes.bfloat16),
        "c_onesrow": np.ones((1, 512), dtype=ml_dtypes.bfloat16),
        "c_ones": np.ones((128, 128), dtype=ml_dtypes.bfloat16),
    }
    band = (np.arange(128)[:, None] <= (np.arange(896)[None, :] - 384))
    d["c_band"] = band.astype(ml_dtypes.bfloat16)
    sel = np.zeros((8, NX * 128), dtype=ml_dtypes.bfloat16)
    for n in range(NX):
        sel[n, n * 128:(n + 1) * 128] = 1
    d["c_sel"] = sel
    return d


def kernel(**inputs):
    nc = bass.Bass("TRN2", target_bir_lowering=False)
    build(nc)
    in_maps = [_prep(inputs, c) for c in range(8)]
    res = run_bass_kernel_spmd(nc, in_maps, core_ids=list(range(8)))
    out = np.zeros((4, TB, D), dtype=np.float32)
    for c in range(8):
        b, g = c // 2, c % 2
        y = np.asarray(res.results[c]["y_mine"])
        out[b, g * TC:(g + 1) * TC] = (
            y.reshape(128, DT, TC).transpose(2, 1, 0).reshape(TC, D))
    return out
